# version 10
# baseline (speedup 1.0000x reference)
import numpy as np
import ml_dtypes
import concourse.bass as bass
import concourse.mybir as mybir
from concourse.bass_utils import run_bass_kernel_spmd

F32 = mybir.dt.float32
BF16 = mybir.dt.bfloat16
I32 = mybir.dt.int32
AF = mybir.ActivationFunctionType
ALU = mybir.AluOpType
BF16NP = ml_dtypes.bfloat16

D = 4096
SEQ = 4096
NCORE = 8
TOK = 1024
NC_D = D // 128
NH0 = 32
TOPK = 819
HID = 11008
NHC = HID // 128
INV_SQRT_DH = float(1.0 / np.sqrt(128.0))
Z_Q = 0.8416212335729143
NEWTON_G = float(1.0 / (SEQ * 0.2799619))
LN_EPS = 1e-5

DQ_HC = 11
DQ_OC = 4

GROUPS4 = [[0, 1, 2, 3], [4, 5, 6, 7]]
GROUPS8 = [list(range(8))]

ENGS = ("sync", "vector", "scalar", "tensor", "gpsimd")


class Sem:
    def __init__(self, nc, name):
        self.h = nc.alloc_semaphore(name)
        self.n = 0


class Prog:

    SELF_CHAIN = ("vector", "scalar", "gpsimd")

    def __init__(self, nc):
        self.nc = nc
        self.q = {e: [] for e in ENGS}
        self.selfsem = {e: Sem(nc, f"self_{e}") for e in self.SELF_CHAIN}
        self.chain = {}

    def emit(self, eng, fn):
        self.q[eng].append(fn)

    def op(self, eng, fn, inc=None, k=1):
        if eng in self.SELF_CHAIN:
            prev = self.chain.get(eng)
            if inc is not None:
                inc.n += k
                upd = (inc.h, k)
                self.chain[eng] = (inc.h, inc.n)
            else:
                ss = self.selfsem[eng]
                ss.n += 1
                upd = (ss.h, 1)
                self.chain[eng] = (ss.h, ss.n)

            def wrapped(e, f=fn, prev=prev, upd=upd):
                if prev is not None:
                    e.wait_ge(prev[0], prev[1])
                f(e).then_inc(upd[0], upd[1])
            self.emit(eng, wrapped)
        elif inc is None:
            self.emit(eng, fn)
        else:
            inc.n += k
            self.emit(eng, lambda e, f=fn, s=inc.h, kk=k: f(e).then_inc(s, kk))

    def raw(self, eng, fn, inc, k=1):
        inc.n += k
        self.emit(eng, lambda e, f=fn, s=inc.h, kk=k: f(e).then_inc(s, kk))

    def dma(self, eng, out, in_, sem):
        prev = self.chain.get(eng) if eng in self.SELF_CHAIN else None
        sem.n += 16
        def fn(e, o=out, i=in_, s=sem.h, prev=prev):
            if prev is not None:
                e.wait_ge(prev[0], prev[1])
            e.dma_start(out=o, in_=i).then_inc(s, 16)
        self.emit(eng, fn)

    def wait(self, eng, sem, val=None):
        v = sem.n if val is None else val
        if v > 0:
            self.emit(eng, lambda e, s=sem.h, vv=v: e.wait_ge(s, vv))

    def run(self):
        with self.nc.Block() as block:
            @block.sync
            def _(sync):
                for fn in self.q["sync"]:
                    fn(sync)

            @block.vector
            def _(vector):
                for fn in self.q["vector"]:
                    fn(vector)

            @block.scalar
            def _(scalar):
                for fn in self.q["scalar"]:
                    fn(scalar)

            @block.tensor
            def _(tensor):
                for fn in self.q["tensor"]:
                    fn(tensor)

            @block.gpsimd
            def _(gpsimd):
                for fn in self.q["gpsimd"]:
                    fn(gpsimd)


class Common:
    def __init__(self, nc, p):
        self.nc = nc
        self.p = p
        al = nc.alloc_sbuf_tensor
        self.ident = al("cm_ident", [128, 128], BF16)
        self.ones_c16 = al("cm_ones_c16", [128, 1], BF16)
        self.ones_c32 = al("cm_ones_c32", [128, 1], F32)
        self.ones_r16 = al("cm_ones_r16", [1, 128], BF16)
        i_t = nc.inline_tensor(np.eye(128, dtype=BF16NP), name="c_ident")
        oc16 = nc.inline_tensor(np.ones((128, 1), dtype=BF16NP), name="c_oc16")
        oc32 = nc.inline_tensor(np.ones((128, 1), dtype=np.float32), name="c_oc32")
        or16 = nc.inline_tensor(np.ones((1, 128), dtype=BF16NP), name="c_or16")
        self.ld = Sem(nc, "cm_ld")
        self.ld2 = Sem(nc, "cm_ld2")
        self.ld3 = Sem(nc, "cm_ld3")
        self.sv = Sem(nc, "cm_v")
        self.ss = Sem(nc, "cm_s")
        self.st = Sem(nc, "cm_t")
        self.sg = Sem(nc, "cm_g")
        self.cc = Sem(nc, "cm_cc")
        self.slot = [Sem(nc, f"cm_sl{i}") for i in range(8)]
        self.all_sems = [self.ld, self.ld2, self.ld3, self.sv, self.ss,
                         self.st, self.sg] + self.slot
        p.dma("sync", self.ident[:], i_t.ap(), self.ld)
        p.dma("sync", self.ones_c16[:], oc16.ap(), self.ld)
        p.dma("sync", self.ones_c32[:], oc32.ap(), self.ld)
        p.dma("sync", self.ones_r16[:], or16.ap(), self.ld)

    def barrier(self):
        for e in ENGS:
            for s in self.all_sems:
                self.p.wait(e, s)



def emit_ln_stats_rows(p, cm, rows, ps_s1, ps_s2, ntok):
    rstd = rows[0:1, 0:ntok]
    murs = rows[0:1, ntok:2 * ntok]
    mu = ps_s1
    p.wait("vector", cm.st)
    p.op("vector", lambda e: e.tensor_scalar_mul(mu, ps_s1, 1.0 / D))
    p.op("vector", lambda e: e.tensor_scalar_mul(murs, ps_s2, 1.0 / D),
         inc=cm.sv)
    p.wait("scalar", cm.sv)
    p.op("scalar", lambda e: e.activation(out=rstd, in_=mu, func=AF.Square,
                                          bias=0.0, scale=1.0), inc=cm.ss)
    p.wait("vector", cm.ss)
    p.op("vector", lambda e: e.tensor_sub(murs, murs, rstd))
    p.op("vector", lambda e: e.tensor_scalar_add(murs, murs, LN_EPS), inc=cm.sv)
    p.wait("scalar", cm.sv)
    p.op("scalar", lambda e: e.activation(out=murs, in_=murs, func=AF.Ln,
                                          bias=0.0, scale=1.0))
    p.op("scalar", lambda e: e.activation(out=rstd, in_=murs, func=AF.Exp,
                                          bias=0.0, scale=-0.5), inc=cm.ss)
    p.wait("vector", cm.ss)
    p.op("vector", lambda e: e.tensor_mul(murs, mu, rstd), inc=cm.sv)


def emit_ln_bcast(p, cm, rows, ntok, ps_bc, bcast_r, bcast_m, rows16):
    nh = ntok // 512
    p.wait("vector", cm.sv)
    p.op("vector", lambda e: e.tensor_copy(out=rows16[0:1, 0:2 * ntok],
                                           in_=rows[0:1, 0:2 * ntok]), inc=cm.sv)
    rstd = rows16[0:1, 0:ntok]
    murs = rows16[0:1, ntok:2 * ntok]
    p.wait("tensor", cm.sv)
    for h in range(nh):
        p.op("tensor", lambda e, h=h: e.matmul(
            ps_bc[:, h * 512:(h + 1) * 512], cm.ones_r16[:],
            rstd[0:1, h * 512:(h + 1) * 512], start=True, stop=True))
    for h in range(nh):
        p.op("tensor", lambda e, h=h: e.matmul(
            ps_bc[:, ntok + h * 512:ntok + (h + 1) * 512], cm.ones_r16[:],
            murs[0:1, h * 512:(h + 1) * 512], start=True, stop=True),
            inc=cm.st if h == nh - 1 else None)
    p.wait("vector", cm.st)
    p.op("vector", lambda e: e.tensor_copy(out=bcast_r[:], in_=ps_bc[:, 0:ntok]))
    p.op("vector", lambda e: e.tensor_copy(out=bcast_m[:], in_=ps_bc[:, ntok:2 * ntok]),
         inc=cm.sv)


def emit_ln_resident(p, cm, X, XN, ntok, sq, rows, ps_s1, ps_s2, ps_bc,
                     bcast_r, bcast_m, rows16, tmp32):
    nh = ntok // 512
    p.wait("tensor", cm.sv)
    for c in range(NC_D):
        for h in range(nh):
            p.op("tensor", lambda e, c=c, h=h: e.matmul(
                ps_s1[0:1, h * 512:(h + 1) * 512], cm.ones_c16[:],
                X[:, c, h * 512:(h + 1) * 512],
                start=(c == 0), stop=(c == NC_D - 1)))
    st_marks = {}
    for c in range(NC_D):
        if c >= 2:
            p.wait("scalar", cm.st, st_marks[c - 2])
        p.op("scalar", lambda e, c=c: e.activation(
            out=sq[:, (c % 2) * ntok:(c % 2) * ntok + ntok], in_=X[:, c, :],
            func=AF.Square, bias=0.0, scale=1.0), inc=cm.ss)
        p.wait("tensor", cm.ss)
        for h in range(nh):
            p.op("tensor", lambda e, c=c, h=h: e.matmul(
                ps_s2[0:1, h * 512:(h + 1) * 512], cm.ones_c16[:],
                sq[:, (c % 2) * ntok + h * 512:(c % 2) * ntok + (h + 1) * 512],
                start=(c == 0), stop=(c == NC_D - 1)),
                inc=cm.st if h == nh - 1 else None)
        st_marks[c] = cm.st.n
    emit_ln_stats_rows(p, cm, rows, ps_s1[0:1, 0:ntok], ps_s2[0:1, 0:ntok], ntok)
    emit_ln_bcast(p, cm, rows, ntok, ps_bc, bcast_r, bcast_m, rows16)
    p.wait("vector", cm.sv)
    for c in range(NC_D):
        p.op("vector", lambda e, c=c: e.tensor_mul(tmp32, X[:, c, :],
                                                   bcast_r[:]))
        p.op("vector", lambda e, c=c: e.tensor_sub(XN[:, c, :], tmp32,
                                                   bcast_m[:]),
             inc=cm.sv if c == NC_D - 1 else None)


def emit_ln_stream(p, cm, xT_hbm, XN, ntok, xbuf, sq, rows, ps_s1, ps_s2,
                   ps_bc, bcast_r32, bcast_m, rows16, tmp32):
    nh = ntok // 512
    xr = xT_hbm.rearrange("(c p) t -> p c t", p=128)
    st_marks = {}
    for c in range(NC_D):
        b = c % 2
        if c >= 2:
            p.wait("sync", cm.st, st_marks[c - 2])
        p.dma("sync", xbuf[b][:], xr[:, c, :], cm.slot[b])
        p.wait("tensor", cm.slot[b])
        for h in range(nh):
            p.op("tensor", lambda e, c=c, h=h: e.matmul(
                ps_s1[0:1, h * 512:(h + 1) * 512], cm.ones_c32[:],
                xbuf[c % 2][:, h * 512:(h + 1) * 512],
                start=(c == 0), stop=(c == NC_D - 1)))
        p.wait("scalar", cm.slot[b])
        if c >= 2:
            p.wait("scalar", cm.st, st_marks[c - 2])
        p.op("scalar", lambda e, c=c: e.activation(
            out=sq[:, (c % 2) * ntok:(c % 2) * ntok + ntok], in_=xbuf[c % 2][:],
            func=AF.Square, bias=0.0, scale=1.0), inc=cm.ss)
        p.wait("tensor", cm.ss)
        for h in range(nh):
            p.op("tensor", lambda e, c=c, h=h: e.matmul(
                ps_s2[0:1, h * 512:(h + 1) * 512], cm.ones_c16[:],
                sq[:, (c % 2) * ntok + h * 512:(c % 2) * ntok + (h + 1) * 512],
                start=(c == 0), stop=(c == NC_D - 1)),
                inc=cm.st)
        st_marks[c] = cm.st.n
    emit_ln_stats_rows(p, cm, rows, ps_s1[0:1, 0:ntok], ps_s2[0:1, 0:ntok], ntok)
    emit_ln_bcast(p, cm, rows, ntok, ps_bc, bcast_r32, bcast_m, rows16)
    v_marks = {}
    p.wait("sync", cm.ss)
    p.wait("sync", cm.st)
    for c in range(NC_D):
        b = c % 2
        if c >= 2:
            p.wait("sync", cm.sv, v_marks[c - 2])
        p.dma("sync", xbuf[b][:], xr[:, c, :], cm.slot[b])
        p.wait("vector", cm.slot[b])
        p.op("vector", lambda e, c=c: e.tensor_mul(tmp32, xbuf[c % 2][:],
                                                   bcast_r32[:]))
        p.op("vector", lambda e, c=c: e.tensor_sub(XN[:, c, :], tmp32,
                                                   bcast_m[:]), inc=cm.sv)
        v_marks[c] = cm.sv.n



def emit_proj_fm_half(p, cm, W_hbm, XN, nmc, ntok, wpan32, wpan16, ps_pair,
                      evac, nic=NC_D):
    nh = ntok // 512
    cast_marks = {}
    group_marks = {}
    evac_marks = {}
    ss_base = cm.ss.n
    nq = max(1, nic // 8)
    nparts = (nic + nq - 1) // nq
    pctr = 0
    pmark = {0: 0, 1: 0}
    for mc in range(nmc):
        b = mc % 2
        for part in range(nparts):
            pb = pctr % 2
            lo = part * nq
            hi = min(nic, lo + nq)
            src = W_hbm[lo * 128:hi * 128,
                        mc * 128:(mc + 1) * 128].rearrange(
                "(ic p) m -> p ic m", p=128)
            p.wait("sync", cm.sv, pmark[pb])
            p.dma("sync",
                  wpan32[pb][:, 0:(hi - lo) * 128].rearrange(
                      "p (ic m) -> p ic m", m=128), src, cm.slot[2 + pb])
            p.wait("vector", cm.slot[2 + pb])
            if mc >= 2 and part == 0:
                p.wait("vector", cm.st, group_marks[mc - 2])
            p.op("vector", lambda e, b=b, pb=pb, lo=lo, n=(hi - lo) * 128:
                 e.tensor_copy(out=wpan16[b][:, lo * 128:lo * 128 + n],
                               in_=wpan32[pb][:, 0:n]), inc=cm.sv)
            pmark[pb] = cm.sv.n
            pctr += 1
        cast_marks[mc] = cm.sv.n
        ps = ps_pair[mc % 2]
        p.wait("tensor", cm.sv)
        if mc >= 2:
            p.wait("tensor", cm.ss, evac_marks[mc - 2])
        else:
            p.wait("tensor", cm.ss, ss_base)
        for ic in range(nic):
            for h in range(nh):
                p.op("tensor", lambda e, b=b, ic=ic, h=h, ps=ps: e.matmul(
                    ps[:, h * 512:(h + 1) * 512],
                    wpan16[b][:, ic * 128:(ic + 1) * 128],
                    XN[:, ic, h * 512:(h + 1) * 512],
                    start=(ic == 0), stop=(ic == nic - 1)),
                    inc=cm.st if (ic == nic - 1 and h == nh - 1) else None)
        group_marks[mc] = cm.st.n
        evac(mc, ps)
        evac_marks[mc] = cm.ss.n



def build_full():
    nc = bass.Bass("TRN2", target_bir_lowering=False)

    xT = nc.dram_tensor("xT", [D, TOK], F32, kind="ExternalInput").ap()
    wqT = nc.dram_tensor("wqT", [D, D], F32, kind="ExternalInput").ap()
    wkT = nc.dram_tensor("wkT", [D, D], F32, kind="ExternalInput").ap()
    wvT = nc.dram_tensor("wvT", [D, D], F32, kind="ExternalInput").ap()
    bq_r = nc.dram_tensor("bq_r", [128, NC_D], F32, kind="ExternalInput").ap()
    bk_r = nc.dram_tensor("bk_r", [128, NC_D], F32, kind="ExternalInput").ap()
    bv_row = nc.dram_tensor("bv_row", [1, D], F32, kind="ExternalInput").ap()
    pk1 = nc.dram_tensor("pk1", [DQ_HC * 4096, 64], I32, kind="ExternalInput").ap()
    pk3 = nc.dram_tensor("pk3", [DQ_HC * 4096, 64], I32, kind="ExternalInput").ap()
    s1_r = nc.dram_tensor("s1_r", [128, DQ_HC * 64], F32, kind="ExternalInput").ap()
    s3_r = nc.dram_tensor("s3_r", [128, DQ_HC * 64], F32, kind="ExternalInput").ap()
    pk2 = nc.dram_tensor("pk2", [DQ_OC * 128, NHC * 64], I32,
                         kind="ExternalInput").ap()
    s2_r = nc.dram_tensor("s2_r", [128, DQ_OC * 172], F32, kind="ExternalInput").ap()
    woT = nc.dram_tensor("woT", [D, D], F32, kind="ExternalInput").ap()
    bo_r = nc.dram_tensor("bo_r", [128, NC_D], F32, kind="ExternalInput").ap()
    wq1 = nc.dram_tensor("wq1", [D, D], F32, kind="ExternalInput").ap()
    wk1 = nc.dram_tensor("wk1", [D, 1024], F32, kind="ExternalInput").ap()
    wv1 = nc.dram_tensor("wv1", [D, 1024], F32, kind="ExternalInput").ap()
    cosb = nc.dram_tensor("cosb", [64, TOK], BF16, kind="ExternalInput").ap()
    sinb = nc.dram_tensor("sinb", [64, TOK], BF16, kind="ExternalInput").ap()
    wo1 = nc.dram_tensor("wo1", [D, D], F32, kind="ExternalInput").ap()
    bwT = nc.dram_tensor("bwT", [D, D], F32, kind="ExternalInput").ap()
    bb_r = nc.dram_tensor("bb_r", [128, NC_D], F32, kind="ExternalInput").ap()
    aT = nc.dram_tensor("aT", [D, 64], F32, kind="ExternalInput").ap()
    bstk = nc.dram_tensor("bstk", [1024, D], F32, kind="ExternalInput").ap()
    rwT = nc.dram_tensor("rwT", [D, 16], F32, kind="ExternalInput").ap()
    lmask = nc.dram_tensor("lmask", [2, NC_D, 128, 512], BF16,
                           kind="ExternalInput").ap()

    outT = nc.dram_tensor("outT", [D, TOK], F32, kind="ExternalOutput").ap()

    q0T = nc.dram_tensor("q0T", [D, TOK], BF16, kind="Internal").ap()
    k0_loc = nc.dram_tensor("k0_loc", [D, TOK], BF16, kind="Internal").ap()
    v0_loc = nc.dram_tensor("v0_loc", [TOK, D], BF16, kind="Internal").ap()
    k0g = nc.dram_tensor("k0g", [4, D, TOK], BF16, kind="Internal").ap()
    v0g = nc.dram_tensor("v0g", [4, TOK, D], BF16, kind="Internal").ap()
    w1t_loc = nc.dram_tensor("w1t_loc", [128, DQ_HC * NC_D, 128], BF16,
                             kind="Internal").ap()
    w3t_loc = nc.dram_tensor("w3t_loc", [128, DQ_HC * NC_D, 128], BF16,
                             kind="Internal").ap()
    w2t_loc = nc.dram_tensor("w2t_loc", [128, DQ_OC * NHC, 128], BF16,
                             kind="Internal").ap()
    w1g = nc.dram_tensor("w1g", [8, 128, DQ_HC * NC_D, 128], BF16,
                         kind="Internal").ap()
    w3g = nc.dram_tensor("w3g", [8, 128, DQ_HC * NC_D, 128], BF16,
                         kind="Internal").ap()
    w2g = nc.dram_tensor("w2g", [8, 128, DQ_OC * NHC, 128], BF16,
                         kind="Internal").ap()
    ctxT = nc.dram_tensor("ctxT", [D, TOK], BF16, kind="Internal").ap()
    x2T = nc.dram_tensor("x2T", [D, TOK], BF16, kind="Internal").ap()
    hsp = nc.dram_tensor("hsp", [NHC, 128, TOK], BF16, kind="Internal").ap()
    q1T = nc.dram_tensor("q1T", [D, TOK], BF16, kind="Internal").ap()
    k1_loc = nc.dram_tensor("k1_loc", [1024, TOK], BF16, kind="Internal").ap()
    v1_loc = nc.dram_tensor("v1_loc", [TOK, 1024], BF16, kind="Internal").ap()
    k1g = nc.dram_tensor("k1g", [4, 1024, TOK], BF16, kind="Internal").ap()
    v1g = nc.dram_tensor("v1g", [4, TOK, 1024], BF16, kind="Internal").ap()
    ctxs = nc.dram_tensor("ctxs", [D, TOK], BF16, kind="Internal").ap()

    p = Prog(nc)
    cm = Common(nc, p)
    al = nc.alloc_sbuf_tensor

    AR_A = al("arA", [128, 32768], BF16).ap()
    AR_B = al("arB", [128, 32768], BF16).ap()
    AR_C = al("arC", [128, 24576], BF16).ap()
    sq = al("p_sq", [128, 2048], BF16).ap()
    evacb = [al(f"p_evac{i}", [128, 1024], BF16).ap() for i in range(2)]
    rows = al("p_rows", [1, 2048], F32).ap()
    ropeb = al("p_ropeb", [128, 2048], BF16).ap()
    bcast_r16 = al("p_bcr16", [128, 1024], BF16).ap()
    bcast_m16 = al("p_bcm16", [128, 1024], F32).ap()
    bias_q = al("p_bq", [128, NC_D], F32).ap()
    bias_k = al("p_bk", [128, NC_D], F32).ap()
    bias_o = al("p_bo", [128, NC_D], F32).ap()
    bias_b = al("p_bb", [128, NC_D], F32).ap()

    pool = nc.alloc_psum_tensor("pool", [128, 4096], F32)
    ps_s1 = pool[0:1, 0:1024]
    ps_s2 = pool[0:1, 1024:2048]
    ps_bc = pool[:, 2048:4096]
    ps_pair = [pool[:, 0:1024], pool[:, 1024:2048]]

    Af = AR_A
    Bf = AR_B
    Cf = AR_C

    XN = Af.rearrange("p (c t) -> p c t", t=TOK)
    vcache = Bf[:, 0:16384].rearrange("p (c v) -> p c v", v=512)
    xbuf = [Bf[:, 16384:18432].bitcast(F32), Bf[:, 18432:20480].bitcast(F32)]
    bcast_r32 = Bf[:, 20480:22528].bitcast(F32)
    bcast_m = Bf[:, 22528:24576].bitcast(F32)
    bv_r16 = Bf[0:1, 24576:28672]
    wpan32 = [Cf[:, 0:2048].bitcast(F32), Cf[:, 2048:4096].bitcast(F32)]
    wpan16 = [Cf[:, 4096:8192], Cf[:, 8192:12288]]
    dq_sc = Cf[:, 12288:15104].bitcast(F32)
    dq_sc2 = Cf[:, 15104:16480].bitcast(F32)
    dq_tr = [Cf[:, 16480:16992], Cf[:, 16992:17504]]
    rows16 = sq[0:1, 0:2048]

    xnflat = Af
    pkbuf = [xnflat[:, 0:5504].bitcast(I32),
             xnflat[:, 5504:11008].bitcast(I32)]
    dq_i = xnflat[:, 11008:16512].bitcast(I32)
    dq_f = xnflat[:, 16512:22016].bitcast(F32)
    dq_w = xnflat[:, 22016:27520].rearrange("p (g f) -> p g f", f=128)

    ps_v8 = [pool[:, i * 512:(i + 1) * 512] for i in range(8)]
    ps_tr = pool[:, 0:1024].bitcast(BF16)

    p.dma("sync", bias_q[:], bq_r, cm.ld)
    p.dma("sync", bias_k[:], bk_r, cm.ld)
    p.dma("sync", bias_o[:], bo_r, cm.ld)
    p.dma("sync", bias_b[:], bb_r, cm.ld)
    p.dma("sync", ropeb[0:64, 0:TOK], cosb, cm.ld)
    p.dma("sync", ropeb[0:64, TOK:2 * TOK], sinb, cm.ld)
    p.dma("sync", ropeb[64:128, 0:TOK], cosb, cm.ld)
    p.dma("sync", ropeb[64:128, TOK:2 * TOK], sinb, cm.ld)
    p.dma("sync", dq_sc[:, 0:DQ_HC * 64], s1_r, cm.ld)
    p.dma("sync", dq_sc[:, DQ_HC * 64:2 * DQ_HC * 64], s3_r, cm.ld)
    p.dma("sync", dq_sc2[:], s2_r, cm.ld)
    for i in range(4):
        p.dma("sync", xbuf[i % 2][0:1, 0:1024],
              bv_row[0:1, i * 1024:(i + 1) * 1024], cm.ld)
        cm.barrier()
        p.op("vector", lambda e, i=i: e.tensor_copy(
            out=bv_r16[0:1, i * 1024:(i + 1) * 1024],
            in_=xbuf[i % 2][0:1, 0:1024]), inc=cm.sv)
    cm.barrier()
    p.op("vector", lambda e: e.tensor_scalar_mul(bias_q[:], bias_q[:],
                                                 INV_SQRT_DH), inc=cm.sv)
    cm.barrier()

    emit_ln_stream(p, cm, xT, XN, TOK, xbuf, sq, rows, ps_s1, ps_s2, ps_bc,
                   bcast_r32, bcast_m, rows16, sq.bitcast(F32)[:, 0:TOK])
    cm.barrier()

    def mk_evac(out_hbm, bias_cols, scale):
        out_r = out_hbm.rearrange("(c p) t -> p c t", p=128)

        def evac(mc, ps):
            b = mc % 2
            p.wait("scalar", cm.st)
            p.wait("scalar", cm.slot[4 + b])
            p.op("scalar", lambda e, mc=mc, ps=ps, b=b: e.activation(
                out=evacb[b][:], in_=ps[:], func=AF.Identity,
                bias=bias_cols[:, mc:mc + 1], scale=scale), inc=cm.ss)
            p.dma("scalar", out_r[:, mc, :], evacb[b][:], cm.slot[4 + b])
        return evac

    emit_proj_fm_half(p, cm, wkT, XN, NC_D, TOK, wpan32, wpan16, ps_pair,
                      mk_evac(k0_loc, bias_k, 1.0))
    cm.barrier()

    for v8 in range(8):
        cast_marks = {}
        for ic in range(NC_D):
            b = ic % 2
            if ic >= 2:
                p.wait("sync", cm.sv, cast_marks[ic - 2])
            src = wvT[ic * 128:(ic + 1) * 128, v8 * 512:(v8 + 1) * 512]
            p.dma("sync", wpan32[b][:, 0:512], src, cm.slot[2 + b])
            p.wait("vector", cm.slot[2 + b])
            p.op("vector", lambda e, b=b, ic=ic: e.tensor_copy(
                out=vcache[:, ic, :], in_=wpan32[b][:, 0:512]), inc=cm.sv)
            cast_marks[ic] = cm.sv.n
        p.wait("tensor", cm.sv)
        for tc in range(8):
            ps = ps_v8[tc]
            for ic in range(NC_D):
                p.op("tensor", lambda e, ic=ic, tc=tc, ps=ps, v8=v8: e.matmul(
                    ps[:], XN[:, ic, tc * 128:(tc + 1) * 128],
                    vcache[:, ic, :], start=(ic == 0), stop=False))
            p.op("tensor", lambda e, ps=ps, v8=v8: e.matmul(
                ps[:], cm.ones_r16[:],
                bv_r16[0:1, v8 * 512:(v8 + 1) * 512],
                start=False, stop=True), inc=cm.st)
        p.wait("vector", cm.st)
        for tc in range(8):
            b = tc % 2
            p.wait("vector", cm.slot[4 + b])
            p.op("vector", lambda e, tc=tc, b=b: e.tensor_copy(
                out=evacb[b][:, 0:512], in_=ps_v8[tc][:]), inc=cm.sv)
            p.wait("scalar", cm.sv)
            p.dma("scalar", v0_loc[tc * 128:(tc + 1) * 128,
                                   v8 * 512:(v8 + 1) * 512],
                  evacb[b][:, 0:512], cm.slot[4 + b])
        cm.barrier()

    p.raw("gpsimd", lambda e: e.collective_compute(
        "AllGather", ALU.bypass, replica_groups=GROUPS4,
        ins=[k0_loc.opt()], outs=[k0g.opt()]), cm.cc)
    p.raw("gpsimd", lambda e: e.collective_compute(
        "AllGather", ALU.bypass, replica_groups=GROUPS4,
        ins=[v0_loc.opt()], outs=[v0g.opt()]), cm.cc)
    CC_KV0 = cm.cc.n

    emit_proj_fm_half(p, cm, wqT, XN, NC_D, TOK, wpan32, wpan16, ps_pair,
                      mk_evac(q0T, bias_q, INV_SQRT_DH))
    cm.barrier()

    def emit_dequant(pk_src_fn, sc_of, out_hbm, n_sub, ntiles, tilebase_of):
        shift_marks = {}
        p.wait("sync", cm.slot[0])
        p.wait("sync", cm.slot[1])
        p.wait("sync", cm.sv)
        for s in range(n_sub):
            b = s % 2
            n = ntiles * 64
            if s >= 2:
                p.wait("sync", cm.sv, shift_marks[s - 2])
            srcap = pk_src_fn(s)
            dstap = pkbuf[b][:, 0:n]
            if len(srcap.shape) == 3:
                dstap = dstap.rearrange("p (r c) -> p r c", c=srcap.shape[2])
            p.dma("sync", dstap, srcap, cm.slot[b])
            p.wait("vector", cm.slot[b])
            sc = sc_of(s)
            for half in range(2):
                if half == 0:
                    p.op("vector", lambda e, b=b, n=n: e.tensor_single_scalar(
                        out=dq_i[:, 0:n], in_=pkbuf[b][:, 0:n], scalar=4,
                        op=ALU.arith_shift_right), inc=cm.sv)
                else:
                    p.op("vector", lambda e, b=b, n=n: e.tensor_single_scalar(
                        out=dq_i[:, 0:n], in_=pkbuf[b][:, 0:n], scalar=28,
                        op=ALU.logical_shift_left))
                    p.op("vector", lambda e, n=n: e.tensor_single_scalar(
                        out=dq_i[:, 0:n], in_=dq_i[:, 0:n], scalar=28,
                        op=ALU.arith_shift_right), inc=cm.sv)
                if half == 1:
                    shift_marks[s] = cm.sv.n
                p.wait("scalar", cm.sv)
                p.op("scalar", lambda e, n=n: e.activation(
                    out=dq_f[:, 0:n], in_=dq_i[:, 0:n], func=AF.Copy,
                    bias=0.0, scale=1.0), inc=cm.ss)
                p.wait("vector", cm.ss)
                p.op("vector", lambda e, n=n, nt=ntiles, sc=sc, half=half:
                     e.tensor_mul(
                         dq_w[:, 0:nt, half * 64:half * 64 + 64],
                         dq_f[:, 0:n].rearrange("p (g j) -> p g j", j=64),
                         sc[:, half::2].to_broadcast([128, nt, 64])), inc=cm.sv)
            p.wait("tensor", cm.sv)
            for g0 in range(0, ntiles, 4):
                gb = (g0 // 4) % 2
                ng = min(4, ntiles - g0)
                p.wait("tensor", cm.slot[6 + gb])
                for t in range(ng):
                    p.op("tensor", lambda e, g0=g0, t=t, gb=gb: e.transpose(
                        ps_tr.rearrange("p (u f) -> p u f", f=128)[:, gb * 8 + t, :],
                        dq_w[:, g0 + t, :], cm.ident[:]),
                        inc=cm.st if t == ng - 1 else None)
                p.wait("vector", cm.st)
                p.op("vector", lambda e, gb=gb, n=ng * 128: e.tensor_copy(
                    out=dq_tr[gb][:, 0:n],
                    in_=ps_tr[:, gb * 1024:gb * 1024 + n]), inc=cm.sv)
                p.wait("gpsimd", cm.sv)
                base = tilebase_of(s) + g0
                dst = out_hbm[:, base:base + ng, :]
                p.dma("gpsimd", dst,
                      dq_tr[gb][:, 0:ng * 128].rearrange("i (t h) -> i t h", h=128),
                      cm.slot[6 + gb])

    emit_dequant(
        lambda s: pk1[s * 4096:(s + 1) * 4096, :].rearrange(
            "(p r) c -> p r c", p=128),
        lambda s: dq_sc[:, s * 64:(s + 1) * 64],
        w1t_loc, DQ_HC, NC_D, lambda s: s * NC_D)
    emit_dequant(
        lambda s: pk3[s * 4096:(s + 1) * 4096, :].rearrange(
            "(p r) c -> p r c", p=128),
        lambda s: dq_sc[:, DQ_HC * 64 + s * 64:DQ_HC * 64 + (s + 1) * 64],
        w3t_loc, DQ_HC, NC_D, lambda s: s * NC_D)
    emit_dequant(
        lambda s: pk2[(s // 2) * 128:(s // 2) * 128 + 128,
                      (s % 2) * 43 * 64:(s % 2) * 43 * 64 + 43 * 64],
        lambda s: dq_sc2[:, (s // 2) * 172 + (s % 2) * 86:
                         (s // 2) * 172 + (s % 2) * 86 + 86],
        w2t_loc, DQ_OC * 2, 43, lambda s: (s // 2) * NHC + (s % 2) * 43)

    cm.barrier()

    p.raw("gpsimd", lambda e: e.collective_compute(
        "AllGather", ALU.bypass, replica_groups=GROUPS8,
        ins=[w1t_loc.opt()], outs=[w1g.opt()]), cm.cc)
    p.raw("gpsimd", lambda e: e.collective_compute(
        "AllGather", ALU.bypass, replica_groups=GROUPS8,
        ins=[w3t_loc.opt()], outs=[w3g.opt()]), cm.cc)
    p.raw("gpsimd", lambda e: e.collective_compute(
        "AllGather", ALU.bypass, replica_groups=GROUPS8,
        ins=[w2t_loc.opt()], outs=[w2g.opt()]), cm.cc)
    CC_W = cm.cc.n

    NKC = SEQ // 128
    E2 = [Af[:, 0:16384].rearrange("p (c q) -> p c q", q=512),
          Af[:, 16384:32768].rearrange("p (c q) -> p c q", q=512)]
    M = Bf[:, 0:16384].rearrange("p (c q) -> p c q", q=512)
    Kh = [Bf[:, 16384:20480], Bf[:, 20480:24576]]
    Qh = [Bf[:, 24576:25600], Bf[:, 25600:26624]]
    tb = Bf[:, 26624:27136]
    rowsA = Bf[0:1, 27136:28160].bitcast(F32)
    rowsB0 = Bf[0:1, 28160:29184].bitcast(F32)
    rowsG = Bf[0:1, 29184:30208].bitcast(F32)
    rowsT = Bf[0:1, 30208:31232].bitcast(F32)
    rowsT16 = Bf[0:1, 31232:31744]
    rowsR16 = Bf[0:1, 31744:32256]
    Vh = [Cf[:, 0:4096].rearrange("p (c d) -> p c d", d=128),
          Cf[:, 4096:8192].rearrange("p (c d) -> p c d", d=128)]
    ctxb = [Cf[:, 8192:8704], Cf[:, 8704:9216]]
    rb = Cf[:, 9216:10240].bitcast(F32)
    S4raw = Cf[:, 10240:12288].rearrange("p (c q) -> p c q", q=512)

    ps_s = [pool[:, i * 512:(i + 1) * 512] for i in range(4)]
    ps_ctx = pool[:, 4 * 512:5 * 512]
    ps_r1 = pool[0:1, 5 * 512:6 * 512]
    ps_r2 = pool[0:1, 6 * 512:7 * 512]
    ps_bc2 = pool[:, 7 * 512:8 * 512]

    S4raws = [S4raw, Cf[:, 12288:14336].rearrange("p (c q) -> p c q", q=512)]

    p.wait("sync", cm.cc, CC_KV0)
    p.wait("gpsimd", cm.cc, CC_KV0)
    e_done = {}
    pbank = {}
    tree_mark = [0]

    def a0_stage1(g):
        h, qg = g // 2, g % 2
        hb = h % 2
        E = E2[g % 2]
        S4r = S4raws[g % 2]
        if qg == 0:
            p.wait("sync", cm.st)
            for gg in range(4):
                p.dma("sync", Kh[hb][:, gg * 1024:(gg + 1) * 1024],
                      k0g[gg, h * 128:(h + 1) * 128, :], cm.slot[hb])
                p.dma("sync", Vh[hb][:, gg * 8:(gg + 1) * 8, :],
                      v0g[gg][:, h * 128:(h + 1) * 128].rearrange(
                          "(c p) d -> p c d", p=128), cm.slot[hb])
            p.dma("sync", Qh[hb][:], q0T[h * 128:(h + 1) * 128, :], cm.slot[hb])
        p.wait("tensor", cm.slot[hb])
        for c in range(NKC):
            bk = c % 4
            if bk in pbank:
                p.wait("tensor", cm.ss, pbank[bk][0])
                if pbank[bk][1]:
                    p.wait("tensor", cm.sv, pbank[bk][1])
            p.op("tensor", lambda e, c=c, bk=bk, hb=hb, qg=qg: e.matmul(
                ps_s[bk][:], Kh[hb][:, c * 128:(c + 1) * 128],
                Qh[hb][:, qg * 512:(qg + 1) * 512],
                start=True, stop=True), inc=cm.st)
            p.wait("scalar", cm.st)
            if g >= 2 and c == 0:
                p.wait("scalar", cm.st, e_done[g - 2][0])
                p.wait("scalar", cm.sv, e_done[g - 2][1])
            p.op("scalar", lambda e, c=c, bk=bk, E=E: e.activation(
                out=E[:, c, :], in_=ps_s[bk][:], func=AF.Exp,
                bias=0.0, scale=1.0), inc=cm.ss)
            svm = None
            if c < 4:
                if c == 0:
                    p.wait("vector", cm.ss)
                    p.wait("vector", cm.st)
                else:
                    p.wait("vector", cm.st)
                p.op("vector", lambda e, c=c, bk=bk, S4r=S4r: e.tensor_copy(
                    out=S4r[:, c, :], in_=ps_s[bk][:]), inc=cm.sv)
                svm = cm.sv.n
            pbank[bk] = (cm.ss.n, svm)

    def a0_stage2(g):
        h, qg = g // 2, g % 2
        hb = h % 2
        E = E2[g % 2]
        S4r = S4raws[g % 2]
        p.wait("tensor", cm.sv)
        for c in range(4):
            p.op("tensor", lambda e, c=c, S4r=S4r: e.matmul(
                ps_r1[:], cm.ones_c16[:], S4r[:, c, :],
                start=(c == 0), stop=(c == 3)),
                inc=cm.st if c == 3 else None)
        p.wait("scalar", cm.sv)
        p.op("scalar", lambda e, S4r=S4r: e.activation(
            out=sq[:], in_=S4r.rearrange("p c q -> p (c q)"),
            func=AF.Square, bias=0.0, scale=1.0), inc=cm.ss)
        p.wait("tensor", cm.ss)
        for c in range(4):
            p.op("tensor", lambda e, c=c: e.matmul(
                ps_r2[:], cm.ones_c16[:], sq[:, c * 512:(c + 1) * 512],
                start=(c == 0), stop=(c == 3)),
                inc=cm.st if c == 3 else None)
        p.wait("vector", cm.st)
        p.op("vector", lambda e: e.tensor_scalar_mul(rowsA, ps_r1, 1.0 / 512))
        p.op("vector", lambda e: e.tensor_scalar_mul(rowsB0, ps_r2, 1.0 / 512))
        p.op("vector", lambda e: e.tensor_mul(rowsT, rowsA, rowsA))
        p.op("vector", lambda e: e.tensor_sub(rowsB0, rowsB0, rowsT))
        p.op("vector", lambda e: e.tensor_scalar_add(rowsB0, rowsB0, 1e-6),
             inc=cm.sv)
        p.wait("scalar", cm.sv)
        p.op("scalar", lambda e: e.activation(out=rowsB0, in_=rowsB0,
                                              func=AF.Ln, bias=0.0, scale=1.0))
        p.op("scalar", lambda e: e.activation(out=rowsB0, in_=rowsB0,
                                              func=AF.Exp, bias=0.0,
                                              scale=0.5), inc=cm.ss)
        p.wait("vector", cm.ss)
        p.op("vector", lambda e: e.tensor_scalar_mul(rowsG, rowsB0,
                                                     float(NEWTON_G)))
        p.op("vector", lambda e: e.tensor_scalar_mul(rowsT, rowsB0, float(Z_Q)))
        p.op("vector", lambda e: e.tensor_add(rowsT, rowsT, rowsA),
             inc=cm.sv)
        for it in range(3):
            p.wait("scalar", cm.sv)
            p.op("scalar", lambda e: e.activation(
                out=rowsT16, in_=rowsT, func=AF.Exp, bias=0.0, scale=1.0),
                inc=cm.ss)
            p.wait("tensor", cm.ss)
            p.wait("tensor", cm.sv)
            p.op("tensor", lambda e: e.matmul(
                ps_bc2[:], cm.ones_r16[:], rowsT16[0:1, :],
                start=True, stop=True), inc=cm.st)
            p.wait("vector", cm.st)
            p.op("vector", lambda e: e.tensor_copy(out=tb[:], in_=ps_bc2[:]),
                 inc=cm.sv)
            if it == 2:
                p.op("vector", lambda e, E=E: e.tensor_tensor(
                    out=M, in0=E,
                    in1=tb.rearrange("p (u q) -> p u q", u=1).to_broadcast(
                        [128, NKC, 512]),
                    op=ALU.is_gt))
                p.op("vector", lambda e, E=E: e.tensor_mul(E, M, E),
                     inc=cm.sv)
                break
            p.op("vector", lambda e, E=E: e.tensor_tensor(
                out=M, in0=E,
                in1=tb.rearrange("p (u q) -> p u q", u=1).to_broadcast(
                    [128, NKC, 512]),
                op=ALU.is_gt))
            p.op("vector", lambda e: e.tensor_add(
                M[:, 0:16, :], M[:, 0:16, :], M[:, 16:32, :]))
            p.op("vector", lambda e: e.tensor_add(
                M[:, 0:8, :], M[:, 0:8, :], M[:, 8:16, :]))
            p.op("vector", lambda e: e.tensor_add(
                M[:, 0:4, :], M[:, 0:4, :], M[:, 4:8, :]), inc=cm.sv)
            p.wait("tensor", cm.sv)
            for c in range(4):
                p.op("tensor", lambda e, c=c: e.matmul(
                    ps_r1[:], cm.ones_c16[:], M[:, c, :],
                    start=(c == 0), stop=(c == 3)),
                    inc=cm.st if c == 3 else None)
            p.wait("vector", cm.st)
            p.op("vector", lambda e: e.tensor_scalar_add(rowsB0, ps_r1,
                                                         -float(TOPK)))
            p.op("vector", lambda e: e.tensor_mul(rowsB0, rowsB0, rowsG))
            p.op("vector", lambda e: e.tensor_add(rowsT, rowsT, rowsB0),
                 inc=cm.sv)
        sm_mark = cm.sv.n
        p.wait("tensor", cm.sv, sm_mark)
        for c in range(NKC):
            p.op("tensor", lambda e, c=c, hb=hb, E=E: e.matmul(
                ps_ctx[:], Vh[hb][:, c, :], E[:, c, :],
                start=(c == 0), stop=(c == NKC - 1)))
        for c in range(NKC):
            p.op("tensor", lambda e, c=c, E=E: e.matmul(
                ps_r2[:], cm.ones_c16[:], E[:, c, :],
                start=(c == 0), stop=(c == NKC - 1)),
                inc=cm.st if c == NKC - 1 else None)
        pv_mark = cm.st.n
        e_done[g] = (pv_mark, sm_mark)
        p.wait("vector", cm.st)
        p.op("vector", lambda e: e.reciprocal(out=rowsB0, in_=ps_r2))
        p.op("vector", lambda e: e.tensor_copy(out=rowsR16, in_=rowsB0),
             inc=cm.sv)
        p.wait("tensor", cm.sv)
        p.op("tensor", lambda e: e.matmul(ps_bc2[:], cm.ones_r16[:],
                                          rowsR16[0:1, :],
                                          start=True, stop=True), inc=cm.st)
        p.wait("vector", cm.st)
        p.op("vector", lambda e: e.tensor_copy(out=rb[:], in_=ps_bc2[:]))
        gb = qg % 2
        p.wait("vector", cm.slot[4 + gb])
        p.op("vector", lambda e, gb=gb: e.tensor_mul(ctxb[gb][:], ps_ctx[:],
                                                     rb[:]), inc=cm.sv)
        p.wait("scalar", cm.sv)
        p.dma("scalar", ctxT[h * 128:(h + 1) * 128, qg * 512:(qg + 1) * 512],
              ctxb[gb][:], cm.slot[4 + gb])

    for g in range(2 * NH0):
        a0_stage1(g)
        if g >= 1:
            a0_stage2(g - 1)
    a0_stage2(2 * NH0 - 1)
    p.wait("sync", cm.slot[4])
    p.wait("sync", cm.slot[5])
    cm.barrier()

    A = Af.rearrange("p (c t) -> p c t", t=TOK)
    B = Bf.rearrange("p (c t) -> p c t", t=TOK)
    scr8 = Cf[:, 0:8192]
    xbuf3 = [scr8[:, 0:2048].bitcast(F32), scr8[:, 2048:4096].bitcast(F32)]
    xb16 = scr8[:, 4096:5120]
    w13 = [scr8[:, 0:4096].rearrange("p (ic m) -> p ic m", m=128),
           scr8[:, 4096:8192].rearrange("p (ic m) -> p ic m", m=128)]
    vcache3 = scr8[:, 0:8192].rearrange("p (ic v) -> p ic v", v=256)
    wpan32m = [Cf[:, 8192:10240].bitcast(F32), Cf[:, 10240:12288].bitcast(F32)]
    wpan16m = [Cf[:, 12288:16384], Cf[:, 16384:20480]]
    htile = [Cf[:, 20480:21504], Cf[:, 21504:22528]]
    w2til = [Cf[:, 22528:23040].rearrange("p (o h) -> p o h", h=128),
             Cf[:, 23040:23552].rearrange("p (o h) -> p o h", h=128)]
    rtmp = sq

    ps_up = [[pool[:, 0:1024], pool[:, 1024:2048]],
             [pool[:, 2048:3072], pool[:, 3072:4096]]]
    ps_dn = [pool[:, i * 512:(i + 1) * 512] for i in range(8)]
    ps_v4 = [pool[:, i * 256:(i + 1) * 256] for i in range(8)]

    ctr = ctxT.rearrange("(c p) t -> p c t", p=128)
    for c in range(NC_D):
        p.dma("sync", A[:, c, :], ctr[:, c, :], cm.ld)
    cm.barrier()

    xr = xT.rearrange("(c p) t -> p c t", p=128)

    def evac_oproj(mc, ps):
        b = mc % 2
        p.wait("sync", cm.sv)
        p.dma("sync", xbuf3[b][:], xr[:, mc, :], cm.slot[6 + b])
        p.wait("scalar", cm.st)
        p.op("scalar", lambda e, mc=mc, ps=ps, b=b: e.activation(
            out=evacb[b][:], in_=ps[:], func=AF.Identity,
            bias=bias_o[:, mc:mc + 1], scale=1.0), inc=cm.ss)
        p.wait("vector", cm.slot[6 + b])
        p.wait("vector", cm.ss)
        p.op("vector", lambda e, b=b: e.tensor_copy(out=xb16[:], in_=xbuf3[b][:]))
        p.op("vector", lambda e, mc=mc, b=b: e.tensor_add(
            B[:, mc, :], evacb[b][:], xb16[:]), inc=cm.sv)

    emit_proj_fm_half(p, cm, woT, A, NC_D, TOK, wpan32m, wpan16m, ps_pair,
                      evac_oproj)
    cm.barrier()

    emit_ln_resident(p, cm, B, A, TOK, sq, rows, ps_s1, ps_s2, ps_bc,
                     bcast_r16, bcast_m16, rows16, xbuf3[0])
    cm.barrier()

    w13p = [w13,
            [Cf[:, 12288:16384].rearrange("p (ic m) -> p ic m", m=128),
             Cf[:, 16384:20480].rearrange("p (ic m) -> p ic m", m=128)]]

    def w13_src(wg, hc):
        core, qq = hc // DQ_HC, hc % DQ_HC
        return wg[core][:, qq * NC_D:(qq + 1) * NC_D, :]

    p.wait("sync", cm.cc, CC_W)
    p.wait("gpsimd", cm.cc, CC_W)
    up_marks = {}
    fm_marks = {}
    st_base_up = cm.st.n
    for hc in range(NHC):
        ub = hc % 2
        w13c = w13p[ub]
        p.wait("sync", cm.st, fm_marks.get(hc - 2, st_base_up))
        p.wait("gpsimd", cm.st, fm_marks.get(hc - 2, st_base_up))
        p.dma("sync", w13c[0], w13_src(w1g, hc), cm.slot[0 + ub])
        p.dma("gpsimd", w13c[1], w13_src(w3g, hc), cm.slot[2 + ub])
        p.wait("tensor", cm.slot[0 + ub])
        p.wait("tensor", cm.slot[2 + ub])
        if hc >= 2:
            p.wait("tensor", cm.ss, up_marks[hc - 2][0])
            p.wait("tensor", cm.sv, up_marks[hc - 2][1])
        psA, psB = ps_up[ub]
        for ic in range(NC_D):
            for hh in range(2):
                p.op("tensor", lambda e, ic=ic, hh=hh, psA=psA, w13c=w13c:
                     e.matmul(
                         psA[:, hh * 512:(hh + 1) * 512], w13c[0][:, ic, :],
                         A[:, ic, hh * 512:(hh + 1) * 512],
                         start=(ic == 0), stop=(ic == NC_D - 1)))
        for ic in range(NC_D):
            for hh in range(2):
                p.op("tensor", lambda e, ic=ic, hh=hh, psB=psB, w13c=w13c:
                     e.matmul(
                         psB[:, hh * 512:(hh + 1) * 512], w13c[1][:, ic, :],
                         A[:, ic, hh * 512:(hh + 1) * 512],
                         start=(ic == 0), stop=(ic == NC_D - 1)),
                     inc=cm.st if (ic == NC_D - 1 and hh == 1) else None)
        fm_marks[hc] = cm.st.n
        p.wait("scalar", cm.st)
        p.wait("scalar", cm.slot[4 + ub])
        p.op("scalar", lambda e, psA=psA, ub=ub: e.activation(
            out=htile[ub][:], in_=psA[:], func=AF.Silu, bias=0.0, scale=1.0),
            inc=cm.ss)
        s_mark = cm.ss.n
        p.wait("vector", cm.ss)
        p.op("vector", lambda e, psB=psB, ub=ub: e.tensor_copy(
            out=evacb[ub][:], in_=psB[:]))
        p.op("vector", lambda e, ub=ub: e.tensor_mul(
            htile[ub][:], htile[ub][:], evacb[ub][:]), inc=cm.sv)
        up_marks[hc] = (s_mark, cm.sv.n)
        p.wait("scalar", cm.sv)
        p.dma("scalar", hsp[hc], htile[ub][:], cm.slot[4 + ub])
    cm.barrier()

    def w2_src(quad, hc):
        return w2g[quad].rearrange("p (q h) m -> p q h m", h=NHC)[:, :, hc, :]

    dn_marks = {}
    st_base_dn = cm.st.n
    for quad in range(8):
        p.wait("tensor", cm.sv)
        for hc in range(NHC):
            db = hc % 2
            it = quad * NHC + hc
            p.wait("sync", cm.st, dn_marks.get(it - 2, st_base_dn))
            p.wait("gpsimd", cm.st, dn_marks.get(it - 2, st_base_dn))
            p.dma("sync", htile[db][:], hsp[hc], cm.slot[db])
            p.dma("gpsimd", w2til[db][:], w2_src(quad, hc), cm.slot[db])
            p.wait("tensor", cm.slot[db])
            for o in range(4):
                for hh in range(2):
                    p.op("tensor",
                         lambda e, o=o, hh=hh, db=db, hc=hc: e.matmul(
                             ps_dn[o * 2 + hh][:], w2til[db][:, o, :],
                             htile[db][:, hh * 512:(hh + 1) * 512],
                             start=(hc == 0), stop=(hc == NHC - 1)),
                         inc=cm.st if (o == 3 and hh == 1) else None)
            dn_marks[it] = cm.st.n
        p.wait("vector", cm.st)
        for o in range(4):
            mc = quad * 4 + o
            p.op("vector", lambda e, o=o: e.tensor_copy(
                out=evacb[o % 2][:, 0:512], in_=ps_dn[o * 2][:]))
            p.op("vector", lambda e, o=o: e.tensor_copy(
                out=evacb[o % 2][:, 512:1024], in_=ps_dn[o * 2 + 1][:]))
            p.op("vector", lambda e, o=o, mc=mc: e.tensor_add(
                A[:, mc, :], evacb[o % 2][:], B[:, mc, :]),
                inc=cm.sv if o == 3 else None)
    cm.barrier()

    for c in range(NC_D):
        p.dma("sync", x2T.rearrange("(c p) t -> p c t", p=128)[:, c, :],
              A[:, c, :], cm.ld)
    emit_ln_resident(p, cm, A, B, TOK, sq, rows, ps_s1, ps_s2, ps_bc,
                     bcast_r16, bcast_m16, rows16, xbuf3[0])
    cm.barrier()

    cos_t = ropeb[0:64, 0:TOK]
    sin_t = ropeb[0:64, TOK:2 * TOK]
    cos_t64 = ropeb[64:128, 0:TOK]
    sin_t64 = ropeb[64:128, TOK:2 * TOK]

    def mk_evac_rope(out_hbm):
        out_r = out_hbm.rearrange("(c p) t -> p c t", p=128)

        def evac(mc, ps):
            b = mc % 2
            p.wait("vector", cm.st)
            p.wait("vector", cm.slot[4 + b])
            p.op("vector", lambda e, ps=ps, b=b: e.tensor_copy(
                out=evacb[b][:], in_=ps[:]), inc=cm.ss)
            x1 = evacb[b][0:64, :]
            x2 = evacb[b][64:128, :]
            t1 = rtmp[0:64, 0:TOK]
            t2 = rtmp[0:64, TOK:2 * TOK]
            t3 = rtmp[64:128, 0:TOK]
            p.op("vector", lambda e, x1=x1: e.tensor_mul(t1, x1, cos_t))
            p.op("vector", lambda e, x2=x2: e.tensor_mul(t2, x2, sin_t64))
            p.op("vector", lambda e, x2=x2: e.tensor_mul(x2, x2, cos_t64))
            p.op("vector", lambda e, x1=x1: e.tensor_mul(t3, x1, sin_t))
            p.op("vector", lambda e, x2=x2: e.tensor_add(x2, x2, t3))
            p.op("vector", lambda e, x1=x1: e.tensor_sub(x1, t1, t2),
                 inc=cm.ss)
            p.wait("scalar", cm.ss)
            p.dma("scalar", out_r[:, mc, :], evacb[b][:], cm.slot[4 + b])
        return evac

    emit_proj_fm_half(p, cm, wk1, B, 8, TOK, wpan32m, wpan16m, ps_pair,
                      mk_evac_rope(k1_loc))
    cm.barrier()

    for v8 in range(4):
        cast_marks = {}
        for ic in range(NC_D):
            b = ic % 2
            if ic >= 2:
                p.wait("sync", cm.sv, cast_marks[ic - 2])
            src = wv1[ic * 128:(ic + 1) * 128, v8 * 256:(v8 + 1) * 256]
            p.dma("sync", wpan32m[b][:, 0:256], src, cm.slot[2 + b])
            p.wait("vector", cm.slot[2 + b])
            p.op("vector", lambda e, b=b, ic=ic: e.tensor_copy(
                out=vcache3[:, ic, :], in_=wpan32m[b][:, 0:256]), inc=cm.sv)
            cast_marks[ic] = cm.sv.n
        p.wait("tensor", cm.sv)
        for tc in range(8):
            ps = ps_v4[tc]
            for ic in range(NC_D):
                p.op("tensor", lambda e, ic=ic, tc=tc, ps=ps: e.matmul(
                    ps[:], B[:, ic, tc * 128:(tc + 1) * 128],
                    vcache3[:, ic, :],
                    start=(ic == 0), stop=(ic == NC_D - 1)),
                    inc=cm.st if ic == NC_D - 1 else None)
        p.wait("vector", cm.st)
        for tc in range(8):
            b = tc % 2
            p.wait("vector", cm.slot[4 + b])
            p.op("vector", lambda e, tc=tc, b=b: e.tensor_copy(
                out=evacb[b][:, 0:256], in_=ps_v4[tc][:]), inc=cm.sv)
            p.wait("scalar", cm.sv)
            p.dma("scalar", v1_loc[tc * 128:(tc + 1) * 128,
                                   v8 * 256:(v8 + 1) * 256],
                  evacb[b][:, 0:256], cm.slot[4 + b])
        cm.barrier()

    p.raw("gpsimd", lambda e: e.collective_compute(
        "AllGather", ALU.bypass, replica_groups=GROUPS4,
        ins=[k1_loc.opt()], outs=[k1g.opt()]), cm.cc)
    p.raw("gpsimd", lambda e: e.collective_compute(
        "AllGather", ALU.bypass, replica_groups=GROUPS4,
        ins=[v1_loc.opt()], outs=[v1g.opt()]), cm.cc)
    CC_KV1 = cm.cc.n

    emit_proj_fm_half(p, cm, wq1, B, NC_D, TOK, wpan32m, wpan16m, ps_pair,
                      mk_evac_rope(q1T))
    cm.barrier()

    S4 = Af[:, 0:16384].rearrange("p (c q) -> p c q", q=512)
    E4 = Af[:, 16384:32768].rearrange("p (c q) -> p c q", q=512)
    MK = Bf[:, 0:16384].rearrange("p (c q) -> p c q", q=512)
    Kh4 = [Bf[:, 16384:20480], Bf[:, 16384:20480]]
    Vh4 = [Bf[:, 20480:24576].rearrange("p (c d) -> p c d", d=128)] * 2
    Qh4 = [Bf[:, 24576 + i * 512:24576 + (i + 1) * 512] for i in range(2)]
    rowsB4 = Bf[0:1, 25600:26624].bitcast(F32)
    rowsR16_4 = Bf[0:1, 26624:27136]
    rb4 = Bf[:, 27136:28160].bitcast(F32)
    ctxb4 = [Bf[:, 28160 + i * 512:28160 + (i + 1) * 512] for i in range(2)]

    p.wait("sync", cm.cc, CC_KV1)
    e_done4 = {}
    pbank4 = {}
    g4 = 0
    for qg in range(2):
        p.wait("sync", cm.sv)
        p.wait("sync", cm.st)
        for c in range(NC_D):
            p.dma("sync", MK[:, c, :], lmask[qg, c], cm.ld)
        p.wait("vector", cm.ld)
        for h in range(NH0):
            kv = h // 4
            eb4 = g4 % 2
            E4c = [S4, E4][eb4]
            if h % 4 == 0:
                p.wait("sync", cm.st)
                for g in range(4):
                    p.dma("sync", Kh4[0][:, g * 1024:(g + 1) * 1024],
                          k1g[g, kv * 128:(kv + 1) * 128, :], cm.slot[0])
                    p.dma("sync", Vh4[0][:, g * 8:(g + 1) * 8, :],
                          v1g[g][:, kv * 128:(kv + 1) * 128].rearrange(
                              "(c p) d -> p c d", p=128), cm.slot[0])
            p.wait("sync", cm.st)
            p.dma("sync", Qh4[h % 2][:],
                  q1T[h * 128:(h + 1) * 128, qg * 512:(qg + 1) * 512],
                  cm.slot[2 + (h % 2)])
            kb = 0
            p.wait("tensor", cm.slot[0])
            p.wait("tensor", cm.slot[2 + (h % 2)])
            for c in range(NC_D):
                bk = c % 4
                if bk in pbank4:
                    p.wait("tensor", cm.sv, pbank4[bk])
                p.op("tensor", lambda e, c=c, bk=bk, kb=kb, h=h: e.matmul(
                    ps_s[bk][:], Kh4[kb][:, c * 128:(c + 1) * 128],
                    Qh4[h % 2][:], start=True, stop=True), inc=cm.st)
                p.wait("vector", cm.st)
                if g4 >= 2 and c == 0:
                    p.wait("vector", cm.st, e_done4[g4 - 2])
                p.op("vector", lambda e, c=c, bk=bk, E4c=E4c: e.tensor_tensor(
                    out=E4c[:, c, :], in0=ps_s[bk][:], in1=MK[:, c, :],
                    op=ALU.add), inc=cm.sv)
                pbank4[bk] = cm.sv.n
                if c % 4 == 3:
                    p.wait("scalar", cm.sv)
                    p.op("scalar", lambda e, c=c, E4c=E4c: e.activation(
                        out=E4c.rearrange("p c q -> p (c q)")[
                            :, (c - 3) * 512:(c + 1) * 512],
                        in_=E4c.rearrange("p c q -> p (c q)")[
                            :, (c - 3) * 512:(c + 1) * 512],
                        func=AF.Exp, bias=0.0, scale=1.0), inc=cm.ss)
            p.wait("tensor", cm.ss)
            p.wait("tensor", cm.sv)
            for c in range(NC_D):
                p.op("tensor", lambda e, c=c, kb=kb, E4c=E4c: e.matmul(
                    ps_ctx[:], Vh4[kb][:, c, :], E4c[:, c, :],
                    start=(c == 0), stop=(c == NC_D - 1)))
            for c in range(NC_D):
                p.op("tensor", lambda e, c=c, E4c=E4c: e.matmul(
                    ps_r2[:], cm.ones_c16[:], E4c[:, c, :],
                    start=(c == 0), stop=(c == NC_D - 1)),
                    inc=cm.st if c == NC_D - 1 else None)
            e_done4[g4] = cm.st.n
            p.wait("vector", cm.st)
            p.op("vector", lambda e: e.reciprocal(out=rowsB4, in_=ps_r2))
            p.op("vector", lambda e: e.tensor_copy(out=rowsR16_4, in_=rowsB4),
                 inc=cm.sv)
            p.wait("tensor", cm.sv)
            p.op("tensor", lambda e: e.matmul(ps_bc2[:], cm.ones_r16[:],
                                              rowsR16_4[0:1, :],
                                              start=True, stop=True), inc=cm.st)
            p.wait("vector", cm.st)
            p.op("vector", lambda e: e.tensor_copy(out=rb4[:], in_=ps_bc2[:]))
            gb = h % 2
            p.wait("vector", cm.slot[4 + gb])
            p.op("vector", lambda e, gb=gb: e.tensor_mul(ctxb4[gb][:], ps_ctx[:],
                                                         rb4[:]), inc=cm.sv)
            p.wait("scalar", cm.sv)
            p.dma("scalar", ctxs[h * 128:(h + 1) * 128,
                                 qg * 512:(qg + 1) * 512],
                  ctxb4[gb][:], cm.slot[4 + gb])
            g4 += 1
    p.wait("sync", cm.slot[4])
    p.wait("sync", cm.slot[5])
    cm.barrier()

    wpan32f = [Cf[:, 10240:12288].bitcast(F32), Cf[:, 12288:14336].bitcast(F32)]
    wpan16f = [Cf[:, 14336:18432], Cf[:, 18432:22528]]
    x2b = [Cf[:, 0:1024], Cf[:, 1024:2048]]
    _f = Cf[:, 22528:24576].bitcast(F32)
    fin32 = [_f, _f]

    ctr2 = ctxs.rearrange("(c p) t -> p c t", p=128)
    for c in range(NC_D):
        p.dma("sync", A[:, c, :], ctr2[:, c, :], cm.ld)
    cm.barrier()

    x2r = x2T.rearrange("(c p) t -> p c t", p=128)

    def evac_oproj1(mc, ps):
        b = mc % 2
        p.wait("sync", cm.sv)
        p.dma("sync", x2b[b][:], x2r[:, mc, :], cm.slot[6 + b])
        p.wait("vector", cm.st)
        p.op("vector", lambda e, ps=ps, b=b: e.tensor_copy(
            out=evacb[b][:], in_=ps[:]), inc=cm.ss)
        p.wait("vector", cm.slot[6 + b])
        p.op("vector", lambda e, mc=mc, b=b: e.tensor_add(
            B[:, mc, :], evacb[b][:], x2b[b][:]), inc=cm.sv)

    emit_proj_fm_half(p, cm, wo1, A, NC_D, TOK, wpan32f, wpan16f, ps_pair,
                      evac_oproj1)
    cm.barrier()

    emit_ln_resident(p, cm, B, A, TOK, sq, rows, ps_s1, ps_s2, ps_bc,
                     bcast_r16, bcast_m16, rows16, Cf[:, 0:2048].bitcast(F32))
    cm.barrier()

    rh16 = Cf[0:64, 14336:15360]
    lgr = Cf[0:16, 15360:17408].bitcast(F32)
    lgn = Cf[:, 17408:17664].bitcast(F32).rearrange("p (c e) -> p c e", e=16)
    cmbn = Cf[:, 17664:17920].bitcast(F32).rearrange("p (c e) -> p c e", e=16)
    cmbT = Cf[0:16, 17920:18944]
    mx8 = Cf[:, 18944:18960].bitcast(F32)
    nm = Cf[:, 18960:18968].bitcast(F32)
    id16 = Cf[0:16, 18968:19000].bitcast(F32)
    rhs_stk = Cf[:, 2048:10240].rearrange("p (c t) -> p c t", t=TOK)
    span = x2b[0]

    i16 = nc.inline_tensor(np.eye(16, dtype=np.float32), name="c_i16")
    selnp = np.zeros((16, 16, 64), dtype=BF16NP)
    for e_ in range(16):
        selnp[e_, e_, :] = 1
    sel_t = nc.inline_tensor(selnp, name="c_sel")
    sel = lgr.bitcast(BF16)[:, 0:1024].rearrange("p (a m) -> p a m", m=64)
    p.dma("sync", id16[:], i16.ap(), cm.ld)
    cm.barrier()

    for (W, nmo, dst_ps) in ((aT, 64, pool[0:64, 2048:3072]),
                             (rwT, 16, pool[0:16, 3072:4096])):
        t_marks = {}
        w_base = cm.st.n
        for ic in range(NC_D):
            b = ic % 2
            p.wait("sync", cm.sv)
            p.dma("sync", wpan32f[b][:, 0:nmo], W[ic * 128:(ic + 1) * 128, :],
                  cm.slot[2 + b])
            p.wait("vector", cm.slot[2 + b])
            p.wait("vector", cm.st, t_marks[ic - 2] if ic >= 2 else w_base)
            p.op("vector", lambda e, b=b, nmo=nmo: e.tensor_copy(
                out=span[:, b * 512:b * 512 + nmo],
                in_=wpan32f[b][:, 0:nmo]), inc=cm.sv)
            p.wait("tensor", cm.sv)
            for hh in range(2):
                p.op("tensor", lambda e, ic=ic, hh=hh, b=b, nmo=nmo,
                     dst_ps=dst_ps: e.matmul(
                         dst_ps[:, hh * 512:(hh + 1) * 512],
                         span[:, b * 512:b * 512 + nmo],
                         A[:, ic, hh * 512:(hh + 1) * 512],
                         start=(ic == 0), stop=(ic == NC_D - 1)),
                     inc=cm.st if hh == 1 else None)
            t_marks[ic] = cm.st.n
    p.wait("vector", cm.st)
    p.op("vector", lambda e: e.tensor_copy(out=rh16[:], in_=pool[0:64, 2048:3072]))
    p.op("vector", lambda e: e.tensor_copy(out=lgr[:], in_=pool[0:16, 3072:4096]),
         inc=cm.sv)
    cm.barrier()

    for tc in range(8):
        p.wait("tensor", cm.sv)
        p.op("tensor", lambda e, tc=tc: e.transpose(
            pool[:, 2048 + (tc % 2) * 512:2048 + (tc % 2) * 512 + 16],
            lgr[:, tc * 128:(tc + 1) * 128], id16[:]), inc=cm.st)
        p.wait("vector", cm.st)
        p.op("vector", lambda e, tc=tc: e.tensor_copy(
            out=lgn[:, tc, :], in_=pool[:, 2048 + (tc % 2) * 512:
                                        2048 + (tc % 2) * 512 + 16]),
            inc=cm.sv)
    p.wait("vector", cm.sv)
    for tc in range(8):
        p.op("vector", lambda e, tc=tc: e.reduce_max(
            nm[:, 0:1], lgn[:, tc, :], axis=mybir.AxisListType.X))
        p.op("vector", lambda e: e.tensor_scalar_mul(nm[:, 0:1], nm[:, 0:1],
                                                     -1.0), inc=cm.sv)
        p.wait("scalar", cm.sv)
        p.op("scalar", lambda e, tc=tc: e.activation(
            out=cmbn[:, tc, :], in_=lgn[:, tc, :], func=AF.Exp,
            bias=nm[:, 0:1], scale=1.0, accum_out=nm[:, 1:2]), inc=cm.ss)
        p.wait("vector", cm.ss)
        p.op("vector", lambda e: e.reciprocal(out=nm[:, 1:2], in_=nm[:, 1:2]))
        p.op("vector", lambda e, tc=tc: e.tensor_scalar(
            out=cmbn[:, tc, :], in0=cmbn[:, tc, :], scalar1=nm[:, 1:2],
            scalar2=None, op0=ALU.mult))
        p.op("vector", lambda e, tc=tc: e.max(out=mx8[:], in_=cmbn[:, tc, :]))
        p.op("vector", lambda e: e.tensor_add(nm[:, 2:3], mx8[:, 0:1],
                                              mx8[:, 1:2]))
        p.op("vector", lambda e: e.tensor_scalar_add(nm[:, 2:3], nm[:, 2:3],
                                                     1e-6))
        p.op("vector", lambda e: e.reciprocal(out=nm[:, 2:3], in_=nm[:, 2:3]))
        p.op("vector", lambda e: e.tensor_scalar_mul(nm[:, 2:3], nm[:, 2:3],
                                                     0.5))
        p.op("vector", lambda e, tc=tc: e.tensor_scalar(
            out=lgn[:, tc, :], in0=cmbn[:, tc, :], scalar1=mx8[:, 0:1],
            scalar2=None, op0=ALU.is_equal))
        p.op("vector", lambda e, tc=tc: e.tensor_scalar(
            out=lgn[:, tc, :], in0=lgn[:, tc, :], scalar1=mx8[:, 0:1],
            scalar2=None, op0=ALU.mult))
        p.op("vector", lambda e, tc=tc: e.tensor_scalar(
            out=cmbn[:, tc, :], in0=cmbn[:, tc, :],
            scalar1=mx8[:, 1:2], scalar2=None, op0=ALU.is_equal))
        p.op("vector", lambda e, tc=tc: e.tensor_scalar(
            out=cmbn[:, tc, :], in0=cmbn[:, tc, :], scalar1=mx8[:, 1:2],
            scalar2=None, op0=ALU.mult))
        p.op("vector", lambda e, tc=tc: e.tensor_add(cmbn[:, tc, :],
                                                     cmbn[:, tc, :],
                                                     lgn[:, tc, :]))
        p.op("vector", lambda e, tc=tc: e.tensor_scalar(
            out=cmbn[:, tc, :], in0=cmbn[:, tc, :], scalar1=nm[:, 2:3],
            scalar2=None, op0=ALU.mult), inc=cm.sv)
    cm.barrier()
    for tc in range(8):
        p.op("vector", lambda e, tc=tc: e.tensor_copy(
            out=span[:, (tc % 2) * 512:(tc % 2) * 512 + 16],
            in_=cmbn[:, tc, :]), inc=cm.sv)
        p.wait("tensor", cm.sv)
        psb = pool[:, 2048 + (tc % 2) * 512:2048 + (tc % 2) * 512 + 256].bitcast(BF16)
        p.op("tensor", lambda e, tc=tc, psb=psb: e.transpose(
            psb[0:16, 0:128], span[:, (tc % 2) * 512:(tc % 2) * 512 + 16],
            cm.ident[:]), inc=cm.st)
        p.wait("vector", cm.st)
        p.op("vector", lambda e, tc=tc, psb=psb: e.tensor_copy(
            out=cmbT[:, tc * 128:(tc + 1) * 128],
            in_=psb[0:16, 0:128]), inc=cm.sv)
    cm.barrier()

    p.dma("sync", sel, sel_t.ap(), cm.ld)
    cm.barrier()
    for ee in range(16):
        rc, half = ee // 2, ee % 2
        p.wait("tensor", cm.sv)
        p.op("tensor", lambda e, ee=ee: e.matmul(
            pool[0:64, 2048 + (ee % 2) * 512:2048 + (ee % 2) * 512 + 512],
            sel[:, ee, :], cmbT[:, 0:512],
            start=True, stop=True))
        p.op("tensor", lambda e, ee=ee: e.matmul(
            pool[0:64, 3072 + (ee % 2) * 512:3072 + (ee % 2) * 512 + 512],
            sel[:, ee, :], cmbT[:, 512:1024],
            start=True, stop=True), inc=cm.st)
        p.wait("vector", cm.st)
        p.op("vector", lambda e, ee=ee: e.tensor_copy(
            out=span[0:64, 0:512],
            in_=pool[0:64, 2048 + (ee % 2) * 512:2048 + (ee % 2) * 512 + 512]))
        p.op("vector", lambda e, ee=ee: e.tensor_copy(
            out=span[0:64, 512:1024],
            in_=pool[0:64, 3072 + (ee % 2) * 512:3072 + (ee % 2) * 512 + 512]),
            inc=cm.sv)
        p.op("vector", lambda e, ee=ee, rc=rc, half=half: e.tensor_mul(
            rhs_stk[half * 64:half * 64 + 64, rc, :], rh16[:],
            span[0:64, 0:1024]), inc=cm.sv)
    cm.barrier()

    bpan = [sq[:, i * 1024:(i + 1) * 1024].rearrange("p (rc m) -> p rc m", m=128)
            for i in range(2)]

    def evac_fin(mc, ps):
        b = mc % 2
        p.wait("scalar", cm.st)
        p.op("scalar", lambda e, mc=mc, ps=ps, b=b: e.activation(
            out=evacb[b][:], in_=ps[:], func=AF.Identity,
            bias=bias_b[:, mc:mc + 1], scale=1.0), inc=cm.ss)
        p.wait("vector", cm.ss)
        p.wait("vector", cm.slot[6])
        p.op("vector", lambda e, mc=mc, b=b: e.tensor_add(
            fin32[b][:], evacb[b][:], B[:, mc, :]), inc=cm.sv)
        p.wait("scalar", cm.sv)
        p.dma("scalar", outT.rearrange("(c p) t -> p c t", p=128)[:, mc, :],
              fin32[b][:], cm.slot[6])

    cast_marks = {}
    group_marks = {}
    evac_marks = {}
    pctr = 0
    pmark = {0: 0, 1: 0}
    for mc in range(NC_D):
        b = mc % 2
        for part in range(5):
            pb = pctr % 2
            if part < 4:
                lo = part * 8
                srcw = bwT[lo * 128:(lo + 8) * 128,
                           mc * 128:(mc + 1) * 128].rearrange(
                    "(ic p) m -> p ic m", p=128)
            else:
                srcw = bstk[:, mc * 128:(mc + 1) * 128].rearrange(
                    "(ic p) m -> p ic m", p=128)
            p.wait("sync", cm.sv, pmark[pb])
            p.dma("sync", wpan32f[pb][:, 0:1024].rearrange(
                "p (ic m) -> p ic m", m=128), srcw, cm.slot[2 + pb])
            p.wait("vector", cm.slot[2 + pb])
            if mc >= 2 and part == 0:
                p.wait("vector", cm.st, group_marks[mc - 2])
            if part < 4:
                p.op("vector", lambda e, b=b, pb=pb, lo=lo: e.tensor_copy(
                    out=wpan16f[b][:, lo * 128:lo * 128 + 1024],
                    in_=wpan32f[pb][:, 0:1024]), inc=cm.sv)
            else:
                p.op("vector", lambda e, b=b, pb=pb: e.tensor_copy(
                    out=bpan[b].rearrange("p ic m -> p (ic m)"),
                    in_=wpan32f[pb][:, 0:1024]), inc=cm.sv)
            pmark[pb] = cm.sv.n
            pctr += 1
        cast_marks[mc] = cm.sv.n
        ps = ps_pair[mc % 2]
        p.wait("tensor", cm.sv)
        if mc >= 2:
            p.wait("tensor", cm.ss, evac_marks[mc - 2])
        for ic in range(NC_D):
            for hh in range(2):
                p.op("tensor", lambda e, b=b, ic=ic, hh=hh, ps=ps: e.matmul(
                    ps[:, hh * 512:(hh + 1) * 512],
                    wpan16f[b][:, ic * 128:(ic + 1) * 128],
                    A[:, ic, hh * 512:(hh + 1) * 512],
                    start=(ic == 0), stop=False))
        for rc in range(8):
            for hh in range(2):
                p.op("tensor", lambda e, b=b, rc=rc, hh=hh, ps=ps: e.matmul(
                    ps[:, hh * 512:(hh + 1) * 512], bpan[b][:, rc, :],
                    rhs_stk[:, rc, hh * 512:(hh + 1) * 512],
                    start=False, stop=(rc == 7)),
                    inc=cm.st if (rc == 7 and hh == 1) else None)
        group_marks[mc] = cm.st.n
        evac_fin(mc, ps)
        evac_marks[mc] = cm.ss.n

    p.wait("sync", cm.slot[6])
    p.wait("sync", cm.slot[7])
    cm.barrier()
    p.run()
    return nc



def _f32(a):
    return np.ascontiguousarray(a, dtype=np.float32)


def _ci32(a):
    return np.ascontiguousarray(a, dtype=np.int32)


def w13_chunks(core):
    return [(core * DQ_HC + i) if (core * DQ_HC + i) < NHC else 0
            for i in range(DQ_HC)]


_NC_CACHE = {}


def _get_nc():
    if "full" not in _NC_CACHE:
        _NC_CACHE["full"] = build_full()
    return _NC_CACHE["full"]


def kernel(**inputs):
    x = np.asarray(inputs["L0_attn_hidden_states"])
    wqT = _f32(np.asarray(inputs["L0_attn_wq"]).T)
    wkT = _f32(np.asarray(inputs["L0_attn_wk"]).T)
    wvT = _f32(np.asarray(inputs["L0_attn_wv"]).T)
    bq_r = _f32(np.asarray(inputs["L0_attn_bq"]).reshape(NC_D, 128).T)
    bk_r = _f32(np.asarray(inputs["L0_attn_bk"]).reshape(NC_D, 128).T)
    bv_row = _f32(np.asarray(inputs["L0_attn_bv"]).reshape(1, D))
    w1 = np.asarray(inputs["L0_ffn_w1"])
    s1 = np.asarray(inputs["L0_ffn_s1"])
    w2 = np.asarray(inputs["L0_ffn_w2"])
    s2 = np.asarray(inputs["L0_ffn_s2"])
    w3 = np.asarray(inputs["L0_ffn_w3"])
    s3 = np.asarray(inputs["L0_ffn_s3"])
    woT = _f32(np.asarray(inputs["L0_attn_wo"]).T)
    bo_r = _f32(np.asarray(inputs["L0_attn_bo"]).reshape(NC_D, 128).T)
    wq1 = _f32(inputs["L1_attn_Wq"])
    wk1 = _f32(inputs["L1_attn_Wk"])
    wv1 = _f32(inputs["L1_attn_Wv"])
    wo1 = _f32(inputs["L1_attn_Wo"])
    bwT = _f32(np.asarray(inputs["L1_ffn_base_w"]).T)
    bb_r = _f32(np.asarray(inputs["L1_ffn_base_b"]).reshape(NC_D, 128).T)
    aT = _f32(np.asarray(inputs["L1_ffn_A"]).T)
    bstk = _f32(np.asarray(inputs["L1_ffn_B"]).transpose(0, 2, 1).reshape(1024, D))
    rwT = _f32(np.asarray(inputs["L1_ffn_router_w"]).T)
    inv = 1.0 / (10000.0 ** (np.arange(0, 128, 2)[:64] / 128.0))

    in_maps = []
    for c in range(NCORE):
        b, j = c // 4, c % 4
        lst = w13_chunks(c)
        pk1 = _ci32(np.concatenate([w1[hc * 4096:(hc + 1) * 4096] for hc in lst]))
        pk3 = _ci32(np.concatenate([w3[hc * 4096:(hc + 1) * 4096] for hc in lst]))
        s1v = s1.reshape(HID, 64)
        s3v = s3.reshape(HID, 64)
        s1_r = _f32(np.concatenate([s1v[hc * 128:(hc + 1) * 128] for hc in lst],
                                   axis=1))
        s3_r = _f32(np.concatenate([s3v[hc * 128:(hc + 1) * 128] for hc in lst],
                                   axis=1))
        o0 = c * DQ_OC * 128
        pk2 = _ci32(w2[o0 * 86:(o0 + DQ_OC * 128) * 86].reshape(DQ_OC * 128,
                                                                86 * 64))
        s2v = s2.reshape(D, 172)
        s2_r = _f32(np.concatenate(
            [s2v[(c * DQ_OC + q) * 128:(c * DQ_OC + q + 1) * 128]
             for q in range(DQ_OC)], axis=1))
        pos = np.arange(j * TOK, (j + 1) * TOK)
        fr = pos[None, :] * inv[:, None]
        lm = np.zeros((2, NC_D, 128, 512), np.float32)
        for qg in range(2):
            kpos = (128 * np.arange(NC_D)[:, None, None]
                    + np.arange(128)[None, :, None])
            qpos = TOK * j + 512 * qg + np.arange(512)[None, None, :]
            lm[qg] = np.where(kpos <= qpos, 0.0, -30000.0)
        in_maps.append({
            "xT": _f32(x[b, j * TOK:(j + 1) * TOK, :].T),
            "wqT": wqT, "wkT": wkT, "wvT": wvT,
            "bq_r": bq_r, "bk_r": bk_r, "bv_row": bv_row,
            "pk1": pk1, "pk3": pk3, "s1_r": s1_r, "s3_r": s3_r,
            "pk2": pk2, "s2_r": s2_r,
            "woT": woT, "bo_r": bo_r,
            "wq1": wq1, "wk1": wk1, "wv1": wv1,
            "cosb": np.cos(fr).astype(BF16NP),
            "sinb": np.sin(fr).astype(BF16NP),
            "wo1": wo1, "bwT": bwT, "bb_r": bb_r, "aT": aT,
            "bstk": bstk, "rwT": rwT,
            "lmask": lm.astype(BF16NP),
        })

    res = run_bass_kernel_spmd(_get_nc(), in_maps,
                               core_ids=list(range(NCORE))).results

    out = np.empty((2, SEQ, D), np.float32)
    for c in range(NCORE):
        b, j = c // 4, c % 4
        out[b, j * TOK:(j + 1) * TOK, :] = res[c]["outT"].T
    return out


# version 12
# speedup vs baseline: 1.1153x; 1.1153x over previous
import numpy as np
import ml_dtypes
import concourse.bass as bass
import concourse.mybir as mybir
from concourse.bass_utils import run_bass_kernel_spmd

F32 = mybir.dt.float32
BF16 = mybir.dt.bfloat16
I32 = mybir.dt.int32
AF = mybir.ActivationFunctionType
ALU = mybir.AluOpType
BF16NP = ml_dtypes.bfloat16

D = 4096
SEQ = 4096
NCORE = 8
TOK = 1024
NC_D = D // 128
NH0 = 32
TOPK = 819
HID = 11008
NHC = HID // 128
INV_SQRT_DH = float(1.0 / np.sqrt(128.0))
Z_Q = 0.8416212335729143
NEWTON_G = float(1.0 / (SEQ * 0.2799619))
LN_EPS = 1e-5

DQ_HC = 11
DQ_OC = 4

GROUPS4 = [[0, 1, 2, 3], [4, 5, 6, 7]]
GROUPS8 = [list(range(8))]

ENGS = ("sync", "vector", "scalar", "tensor", "gpsimd")


class Sem:
    def __init__(self, nc, name):
        self.h = nc.alloc_semaphore(name)
        self.n = 0


class Prog:

    SELF_CHAIN = ("vector", "scalar", "gpsimd")

    def __init__(self, nc):
        self.nc = nc
        self.q = {e: [] for e in ENGS}
        self.selfsem = {e: Sem(nc, f"self_{e}") for e in self.SELF_CHAIN}
        self.chain = {}

    def emit(self, eng, fn):
        self.q[eng].append(fn)

    def op(self, eng, fn, inc=None, k=1):
        if eng in self.SELF_CHAIN:
            prev = self.chain.get(eng)
            if inc is not None:
                inc.n += k
                upd = (inc.h, k)
                self.chain[eng] = (inc.h, inc.n)
            else:
                ss = self.selfsem[eng]
                ss.n += 1
                upd = (ss.h, 1)
                self.chain[eng] = (ss.h, ss.n)

            def wrapped(e, f=fn, prev=prev, upd=upd):
                if prev is not None:
                    e.wait_ge(prev[0], prev[1])
                f(e).then_inc(upd[0], upd[1])
            self.emit(eng, wrapped)
        elif inc is None:
            self.emit(eng, fn)
        else:
            inc.n += k
            self.emit(eng, lambda e, f=fn, s=inc.h, kk=k: f(e).then_inc(s, kk))

    def raw(self, eng, fn, inc, k=1):
        inc.n += k
        self.emit(eng, lambda e, f=fn, s=inc.h, kk=k: f(e).then_inc(s, kk))

    def dma(self, eng, out, in_, sem):
        prev = self.chain.get(eng) if eng in self.SELF_CHAIN else None
        sem.n += 16
        def fn(e, o=out, i=in_, s=sem.h, prev=prev):
            if prev is not None:
                e.wait_ge(prev[0], prev[1])
            e.dma_start(out=o, in_=i).then_inc(s, 16)
        self.emit(eng, fn)

    def wait(self, eng, sem, val=None):
        v = sem.n if val is None else val
        if v > 0:
            self.emit(eng, lambda e, s=sem.h, vv=v: e.wait_ge(s, vv))

    def run(self):
        with self.nc.Block() as block:
            @block.sync
            def _(sync):
                for fn in self.q["sync"]:
                    fn(sync)

            @block.vector
            def _(vector):
                for fn in self.q["vector"]:
                    fn(vector)

            @block.scalar
            def _(scalar):
                for fn in self.q["scalar"]:
                    fn(scalar)

            @block.tensor
            def _(tensor):
                for fn in self.q["tensor"]:
                    fn(tensor)

            @block.gpsimd
            def _(gpsimd):
                for fn in self.q["gpsimd"]:
                    fn(gpsimd)


class Common:
    def __init__(self, nc, p):
        self.nc = nc
        self.p = p
        al = nc.alloc_sbuf_tensor
        self.ident = al("cm_ident", [128, 128], BF16)
        self.ones_c16 = al("cm_ones_c16", [128, 1], BF16)
        self.ones_c32 = al("cm_ones_c32", [128, 1], F32)
        self.ones_r16 = al("cm_ones_r16", [1, 128], BF16)
        i_t = nc.inline_tensor(np.eye(128, dtype=BF16NP), name="c_ident")
        oc16 = nc.inline_tensor(np.ones((128, 1), dtype=BF16NP), name="c_oc16")
        oc32 = nc.inline_tensor(np.ones((128, 1), dtype=np.float32), name="c_oc32")
        or16 = nc.inline_tensor(np.ones((1, 128), dtype=BF16NP), name="c_or16")
        self.ld = Sem(nc, "cm_ld")
        self.ld2 = Sem(nc, "cm_ld2")
        self.ld3 = Sem(nc, "cm_ld3")
        self.sv = Sem(nc, "cm_v")
        self.ss = Sem(nc, "cm_s")
        self.st = Sem(nc, "cm_t")
        self.sg = Sem(nc, "cm_g")
        self.cc = Sem(nc, "cm_cc")
        self.slot = [Sem(nc, f"cm_sl{i}") for i in range(8)]
        self.all_sems = [self.ld, self.ld2, self.ld3, self.sv, self.ss,
                         self.st, self.sg] + self.slot
        p.dma("sync", self.ident[:], i_t.ap(), self.ld)
        p.dma("sync", self.ones_c16[:], oc16.ap(), self.ld)
        p.dma("sync", self.ones_c32[:], oc32.ap(), self.ld)
        p.dma("sync", self.ones_r16[:], or16.ap(), self.ld)

    def barrier(self):
        for e in ENGS:
            for s in self.all_sems:
                self.p.wait(e, s)



def emit_ln_stats_rows(p, cm, rows, ps_s1, ps_s2, ntok):
    rstd = rows[0:1, 0:ntok]
    murs = rows[0:1, ntok:2 * ntok]
    mu = ps_s1
    p.wait("vector", cm.st)
    p.op("vector", lambda e: e.tensor_scalar_mul(mu, ps_s1, 1.0 / D))
    p.op("vector", lambda e: e.tensor_scalar_mul(murs, ps_s2, 1.0 / D),
         inc=cm.sv)
    p.wait("scalar", cm.sv)
    p.op("scalar", lambda e: e.activation(out=rstd, in_=mu, func=AF.Square,
                                          bias=0.0, scale=1.0), inc=cm.ss)
    p.wait("vector", cm.ss)
    p.op("vector", lambda e: e.tensor_sub(murs, murs, rstd))
    p.op("vector", lambda e: e.tensor_scalar_add(murs, murs, LN_EPS), inc=cm.sv)
    p.wait("scalar", cm.sv)
    p.op("scalar", lambda e: e.activation(out=murs, in_=murs, func=AF.Ln,
                                          bias=0.0, scale=1.0))
    p.op("scalar", lambda e: e.activation(out=rstd, in_=murs, func=AF.Exp,
                                          bias=0.0, scale=-0.5), inc=cm.ss)
    p.wait("vector", cm.ss)
    p.op("vector", lambda e: e.tensor_mul(murs, mu, rstd), inc=cm.sv)


def emit_ln_bcast(p, cm, rows, ntok, ps_bc, bcast_r, bcast_m, rows16):
    nh = ntok // 512
    p.wait("vector", cm.sv)
    p.op("vector", lambda e: e.tensor_copy(out=rows16[0:1, 0:2 * ntok],
                                           in_=rows[0:1, 0:2 * ntok]), inc=cm.sv)
    rstd = rows16[0:1, 0:ntok]
    murs = rows16[0:1, ntok:2 * ntok]
    p.wait("tensor", cm.sv)
    for h in range(nh):
        p.op("tensor", lambda e, h=h: e.matmul(
            ps_bc[:, h * 512:(h + 1) * 512], cm.ones_r16[:],
            rstd[0:1, h * 512:(h + 1) * 512], start=True, stop=True))
    for h in range(nh):
        p.op("tensor", lambda e, h=h: e.matmul(
            ps_bc[:, ntok + h * 512:ntok + (h + 1) * 512], cm.ones_r16[:],
            murs[0:1, h * 512:(h + 1) * 512], start=True, stop=True),
            inc=cm.st if h == nh - 1 else None)
    p.wait("vector", cm.st)
    p.op("vector", lambda e: e.tensor_copy(out=bcast_r[:], in_=ps_bc[:, 0:ntok]))
    p.op("vector", lambda e: e.tensor_copy(out=bcast_m[:], in_=ps_bc[:, ntok:2 * ntok]),
         inc=cm.sv)


def emit_ln_resident(p, cm, X, XN, ntok, sq, rows, ps_s1, ps_s2, ps_bc,
                     bcast_r, bcast_m, rows16, tmp32):
    nh = ntok // 512
    p.wait("tensor", cm.sv)
    for c in range(NC_D):
        for h in range(nh):
            p.op("tensor", lambda e, c=c, h=h: e.matmul(
                ps_s1[0:1, h * 512:(h + 1) * 512], cm.ones_c16[:],
                X[:, c, h * 512:(h + 1) * 512],
                start=(c == 0), stop=(c == NC_D - 1)))
    st_marks = {}
    for c in range(NC_D):
        if c >= 2:
            p.wait("scalar", cm.st, st_marks[c - 2])
        p.op("scalar", lambda e, c=c: e.activation(
            out=sq[:, (c % 2) * ntok:(c % 2) * ntok + ntok], in_=X[:, c, :],
            func=AF.Square, bias=0.0, scale=1.0), inc=cm.ss)
        p.wait("tensor", cm.ss)
        for h in range(nh):
            p.op("tensor", lambda e, c=c, h=h: e.matmul(
                ps_s2[0:1, h * 512:(h + 1) * 512], cm.ones_c16[:],
                sq[:, (c % 2) * ntok + h * 512:(c % 2) * ntok + (h + 1) * 512],
                start=(c == 0), stop=(c == NC_D - 1)),
                inc=cm.st if h == nh - 1 else None)
        st_marks[c] = cm.st.n
    emit_ln_stats_rows(p, cm, rows, ps_s1[0:1, 0:ntok], ps_s2[0:1, 0:ntok], ntok)
    emit_ln_bcast(p, cm, rows, ntok, ps_bc, bcast_r, bcast_m, rows16)
    p.wait("vector", cm.sv)
    for c in range(NC_D):
        p.op("vector", lambda e, c=c: e.tensor_mul(tmp32, X[:, c, :],
                                                   bcast_r[:]))
        p.op("vector", lambda e, c=c: e.tensor_sub(XN[:, c, :], tmp32,
                                                   bcast_m[:]),
             inc=cm.sv if c == NC_D - 1 else None)


def emit_ln_stream(p, cm, xT_hbm, XN, ntok, xbuf, sq, rows, ps_s1, ps_s2,
                   ps_bc, bcast_r32, bcast_m, rows16, tmp32):
    nh = ntok // 512
    xr = xT_hbm.rearrange("(c p) t -> p c t", p=128)
    st_marks = {}
    for c in range(NC_D):
        b = c % 2
        if c >= 2:
            p.wait("sync", cm.st, st_marks[c - 2])
        p.dma("sync", xbuf[b][:], xr[:, c, :], cm.slot[b])
        p.wait("tensor", cm.slot[b])
        for h in range(nh):
            p.op("tensor", lambda e, c=c, h=h: e.matmul(
                ps_s1[0:1, h * 512:(h + 1) * 512], cm.ones_c32[:],
                xbuf[c % 2][:, h * 512:(h + 1) * 512],
                start=(c == 0), stop=(c == NC_D - 1)))
        p.wait("scalar", cm.slot[b])
        if c >= 2:
            p.wait("scalar", cm.st, st_marks[c - 2])
        p.op("scalar", lambda e, c=c: e.activation(
            out=sq[:, (c % 2) * ntok:(c % 2) * ntok + ntok], in_=xbuf[c % 2][:],
            func=AF.Square, bias=0.0, scale=1.0), inc=cm.ss)
        p.wait("tensor", cm.ss)
        for h in range(nh):
            p.op("tensor", lambda e, c=c, h=h: e.matmul(
                ps_s2[0:1, h * 512:(h + 1) * 512], cm.ones_c16[:],
                sq[:, (c % 2) * ntok + h * 512:(c % 2) * ntok + (h + 1) * 512],
                start=(c == 0), stop=(c == NC_D - 1)),
                inc=cm.st)
        st_marks[c] = cm.st.n
    emit_ln_stats_rows(p, cm, rows, ps_s1[0:1, 0:ntok], ps_s2[0:1, 0:ntok], ntok)
    emit_ln_bcast(p, cm, rows, ntok, ps_bc, bcast_r32, bcast_m, rows16)
    v_marks = {}
    p.wait("sync", cm.ss)
    p.wait("sync", cm.st)
    for c in range(NC_D):
        b = c % 2
        if c >= 2:
            p.wait("sync", cm.sv, v_marks[c - 2])
        p.dma("sync", xbuf[b][:], xr[:, c, :], cm.slot[b])
        p.wait("vector", cm.slot[b])
        p.op("vector", lambda e, c=c: e.tensor_mul(tmp32, xbuf[c % 2][:],
                                                   bcast_r32[:]))
        p.op("vector", lambda e, c=c: e.tensor_sub(XN[:, c, :], tmp32,
                                                   bcast_m[:]), inc=cm.sv)
        v_marks[c] = cm.sv.n



def emit_proj_fm_half(p, cm, W_hbm, XN, nmc, ntok, wpan32, wpan16, ps_pair,
                      evac, nic=NC_D):
    nh = ntok // 512
    cast_marks = {}
    group_marks = {}
    evac_marks = {}
    ss_base = cm.ss.n
    nq = max(1, nic // 8)
    nparts = (nic + nq - 1) // nq
    pctr = 0
    pmark = {0: 0, 1: 0}
    for mc in range(nmc):
        b = mc % 2
        for part in range(nparts):
            pb = pctr % 2
            lo = part * nq
            hi = min(nic, lo + nq)
            src = W_hbm[lo * 128:hi * 128,
                        mc * 128:(mc + 1) * 128].rearrange(
                "(ic p) m -> p ic m", p=128)
            p.wait("sync", cm.sv, pmark[pb])
            p.dma("sync",
                  wpan32[pb][:, 0:(hi - lo) * 128].rearrange(
                      "p (ic m) -> p ic m", m=128), src, cm.slot[2 + pb])
            p.wait("vector", cm.slot[2 + pb])
            if mc >= 2 and part == 0:
                p.wait("vector", cm.st, group_marks[mc - 2])
            p.op("vector", lambda e, b=b, pb=pb, lo=lo, n=(hi - lo) * 128:
                 e.tensor_copy(out=wpan16[b][:, lo * 128:lo * 128 + n],
                               in_=wpan32[pb][:, 0:n]), inc=cm.sv)
            pmark[pb] = cm.sv.n
            pctr += 1
        cast_marks[mc] = cm.sv.n
        ps = ps_pair[mc % 2]
        p.wait("tensor", cm.sv)
        if mc >= 2:
            p.wait("tensor", cm.ss, evac_marks[mc - 2])
        else:
            p.wait("tensor", cm.ss, ss_base)
        for ic in range(nic):
            for h in range(nh):
                p.op("tensor", lambda e, b=b, ic=ic, h=h, ps=ps: e.matmul(
                    ps[:, h * 512:(h + 1) * 512],
                    wpan16[b][:, ic * 128:(ic + 1) * 128],
                    XN[:, ic, h * 512:(h + 1) * 512],
                    start=(ic == 0), stop=(ic == nic - 1)),
                    inc=cm.st if (ic == nic - 1 and h == nh - 1) else None)
        group_marks[mc] = cm.st.n
        evac(mc, ps)
        evac_marks[mc] = cm.ss.n



def build_full():
    nc = bass.Bass("TRN2", target_bir_lowering=False)

    xT = nc.dram_tensor("xT", [D, TOK], F32, kind="ExternalInput").ap()
    wqT = nc.dram_tensor("wqT", [D, D], F32, kind="ExternalInput").ap()
    wkT = nc.dram_tensor("wkT", [D, D], F32, kind="ExternalInput").ap()
    wvT = nc.dram_tensor("wvT", [D, D], F32, kind="ExternalInput").ap()
    bq_r = nc.dram_tensor("bq_r", [128, NC_D], F32, kind="ExternalInput").ap()
    bk_r = nc.dram_tensor("bk_r", [128, NC_D], F32, kind="ExternalInput").ap()
    bv_row = nc.dram_tensor("bv_row", [1, D], F32, kind="ExternalInput").ap()
    pk1 = nc.dram_tensor("pk1", [DQ_HC * 4096, 64], I32, kind="ExternalInput").ap()
    pk3 = nc.dram_tensor("pk3", [DQ_HC * 4096, 64], I32, kind="ExternalInput").ap()
    s1_r = nc.dram_tensor("s1_r", [128, DQ_HC * 64], F32, kind="ExternalInput").ap()
    s3_r = nc.dram_tensor("s3_r", [128, DQ_HC * 64], F32, kind="ExternalInput").ap()
    pk2 = nc.dram_tensor("pk2", [DQ_OC * 128, NHC * 64], I32,
                         kind="ExternalInput").ap()
    s2_r = nc.dram_tensor("s2_r", [128, DQ_OC * 172], F32, kind="ExternalInput").ap()
    woT = nc.dram_tensor("woT", [D, D], F32, kind="ExternalInput").ap()
    bo_r = nc.dram_tensor("bo_r", [128, NC_D], F32, kind="ExternalInput").ap()
    wq1 = nc.dram_tensor("wq1", [D, D], F32, kind="ExternalInput").ap()
    wk1 = nc.dram_tensor("wk1", [D, 1024], F32, kind="ExternalInput").ap()
    wv1 = nc.dram_tensor("wv1", [D, 1024], F32, kind="ExternalInput").ap()
    cosb = nc.dram_tensor("cosb", [64, TOK], BF16, kind="ExternalInput").ap()
    sinb = nc.dram_tensor("sinb", [64, TOK], BF16, kind="ExternalInput").ap()
    wo1 = nc.dram_tensor("wo1", [D, D], F32, kind="ExternalInput").ap()
    bwT = nc.dram_tensor("bwT", [D, D], F32, kind="ExternalInput").ap()
    bb_r = nc.dram_tensor("bb_r", [128, NC_D], F32, kind="ExternalInput").ap()
    aT = nc.dram_tensor("aT", [D, 64], F32, kind="ExternalInput").ap()
    bstk = nc.dram_tensor("bstk", [1024, D], F32, kind="ExternalInput").ap()
    rwT = nc.dram_tensor("rwT", [D, 16], F32, kind="ExternalInput").ap()
    lmask = nc.dram_tensor("lmask", [2, NC_D, 128, 512], BF16,
                           kind="ExternalInput").ap()

    outT = nc.dram_tensor("outT", [D, TOK], F32, kind="ExternalOutput").ap()

    q0T = nc.dram_tensor("q0T", [D, TOK], BF16, kind="Internal").ap()
    k0_loc = nc.dram_tensor("k0_loc", [D, TOK], BF16, kind="Internal").ap()
    v0_loc = nc.dram_tensor("v0_loc", [TOK, D], BF16, kind="Internal").ap()
    k0g = nc.dram_tensor("k0g", [4, D, TOK], BF16, kind="Internal").ap()
    v0g = nc.dram_tensor("v0g", [4, TOK, D], BF16, kind="Internal").ap()
    w1t_loc = nc.dram_tensor("w1t_loc", [128, DQ_HC * NC_D, 128], BF16,
                             kind="Internal").ap()
    w3t_loc = nc.dram_tensor("w3t_loc", [128, DQ_HC * NC_D, 128], BF16,
                             kind="Internal").ap()
    w2t_loc = nc.dram_tensor("w2t_loc", [128, DQ_OC * NHC, 128], BF16,
                             kind="Internal").ap()
    w1g = nc.dram_tensor("w1g", [8, 128, DQ_HC * NC_D, 128], BF16,
                         kind="Internal").ap()
    w3g = nc.dram_tensor("w3g", [8, 128, DQ_HC * NC_D, 128], BF16,
                         kind="Internal").ap()
    w2g = nc.dram_tensor("w2g", [8, 128, DQ_OC * NHC, 128], BF16,
                         kind="Internal").ap()
    ctxT = nc.dram_tensor("ctxT", [D, TOK], BF16, kind="Internal").ap()
    x2T = nc.dram_tensor("x2T", [D, TOK], BF16, kind="Internal").ap()
    hsp = nc.dram_tensor("hsp", [NHC, 128, TOK], BF16, kind="Internal").ap()
    q1T = nc.dram_tensor("q1T", [D, TOK], BF16, kind="Internal").ap()
    k1_loc = nc.dram_tensor("k1_loc", [1024, TOK], BF16, kind="Internal").ap()
    v1_loc = nc.dram_tensor("v1_loc", [TOK, 1024], BF16, kind="Internal").ap()
    k1g = nc.dram_tensor("k1g", [4, 1024, TOK], BF16, kind="Internal").ap()
    v1g = nc.dram_tensor("v1g", [4, TOK, 1024], BF16, kind="Internal").ap()
    ctxs = nc.dram_tensor("ctxs", [D, TOK], BF16, kind="Internal").ap()

    p = Prog(nc)
    cm = Common(nc, p)
    al = nc.alloc_sbuf_tensor

    AR_A = al("arA", [128, 32768], BF16).ap()
    AR_B = al("arB", [128, 32768], BF16).ap()
    AR_C = al("arC", [128, 24576], BF16).ap()
    sq = al("p_sq", [128, 2048], BF16).ap()
    evacb = [al(f"p_evac{i}", [128, 1024], BF16).ap() for i in range(2)]
    rows = al("p_rows", [1, 2048], F32).ap()
    ropeb = al("p_ropeb", [128, 2048], BF16).ap()
    bcast_r16 = al("p_bcr16", [128, 1024], BF16).ap()
    bcast_m16 = al("p_bcm16", [128, 1024], F32).ap()
    bias_q = al("p_bq", [128, NC_D], F32).ap()
    bias_k = al("p_bk", [128, NC_D], F32).ap()
    bias_o = al("p_bo", [128, NC_D], F32).ap()
    bias_b = al("p_bb", [128, NC_D], F32).ap()

    pool = nc.alloc_psum_tensor("pool", [128, 4096], F32)
    ps_s1 = pool[0:1, 0:1024]
    ps_s2 = pool[0:1, 1024:2048]
    ps_bc = pool[:, 2048:4096]
    ps_pair = [pool[:, 0:1024], pool[:, 1024:2048]]

    Af = AR_A
    Bf = AR_B
    Cf = AR_C

    XN = Af.rearrange("p (c t) -> p c t", t=TOK)
    vcache = Bf[:, 0:16384].rearrange("p (c v) -> p c v", v=512)
    xbuf = [Bf[:, 16384:18432].bitcast(F32), Bf[:, 18432:20480].bitcast(F32)]
    bcast_r32 = Bf[:, 20480:22528].bitcast(F32)
    bcast_m = Bf[:, 22528:24576].bitcast(F32)
    bv_r16 = Bf[0:1, 24576:28672]
    wpan32 = [Cf[:, 0:2048].bitcast(F32), Cf[:, 2048:4096].bitcast(F32)]
    wpan16 = [Cf[:, 4096:8192], Cf[:, 8192:12288]]
    dq_sc = Cf[:, 12288:15104].bitcast(F32)
    dq_sc2 = Cf[:, 15104:16480].bitcast(F32)
    dq_tr = [Cf[:, 16480:16992], Cf[:, 16992:17504]]
    rows16 = sq[0:1, 0:2048]

    xnflat = Af
    pkbuf = [xnflat[:, 0:5504].bitcast(I32),
             xnflat[:, 5504:11008].bitcast(I32)]
    dq_i = xnflat[:, 11008:16512].bitcast(I32)
    dq_f = xnflat[:, 16512:22016].bitcast(F32)
    dq_w = xnflat[:, 22016:27520].rearrange("p (g f) -> p g f", f=128)

    ps_v8 = [pool[:, i * 512:(i + 1) * 512] for i in range(8)]
    ps_tr = pool[:, 0:1024].bitcast(BF16)

    p.dma("sync", bias_q[:], bq_r, cm.ld)
    p.dma("sync", bias_k[:], bk_r, cm.ld)
    p.dma("sync", bias_o[:], bo_r, cm.ld)
    p.dma("sync", bias_b[:], bb_r, cm.ld)
    p.dma("sync", ropeb[0:64, 0:TOK], cosb, cm.ld)
    p.dma("sync", ropeb[0:64, TOK:2 * TOK], sinb, cm.ld)
    p.dma("sync", ropeb[64:128, 0:TOK], cosb, cm.ld)
    p.dma("sync", ropeb[64:128, TOK:2 * TOK], sinb, cm.ld)
    p.dma("sync", dq_sc[:, 0:DQ_HC * 64], s1_r, cm.ld)
    p.dma("sync", dq_sc[:, DQ_HC * 64:2 * DQ_HC * 64], s3_r, cm.ld)
    p.dma("sync", dq_sc2[:], s2_r, cm.ld)
    for i in range(4):
        p.dma("sync", xbuf[i % 2][0:1, 0:1024],
              bv_row[0:1, i * 1024:(i + 1) * 1024], cm.ld)
        cm.barrier()
        p.op("vector", lambda e, i=i: e.tensor_copy(
            out=bv_r16[0:1, i * 1024:(i + 1) * 1024],
            in_=xbuf[i % 2][0:1, 0:1024]), inc=cm.sv)
    cm.barrier()
    p.op("vector", lambda e: e.tensor_scalar_mul(bias_q[:], bias_q[:],
                                                 INV_SQRT_DH), inc=cm.sv)
    cm.barrier()

    emit_ln_stream(p, cm, xT, XN, TOK, xbuf, sq, rows, ps_s1, ps_s2, ps_bc,
                   bcast_r32, bcast_m, rows16, sq.bitcast(F32)[:, 0:TOK])
    cm.barrier()

    def mk_evac(out_hbm, bias_cols, scale):
        out_r = out_hbm.rearrange("(c p) t -> p c t", p=128)

        def evac(mc, ps):
            b = mc % 2
            p.wait("scalar", cm.st)
            p.wait("scalar", cm.slot[4 + b])
            p.op("scalar", lambda e, mc=mc, ps=ps, b=b: e.activation(
                out=evacb[b][:], in_=ps[:], func=AF.Identity,
                bias=bias_cols[:, mc:mc + 1], scale=scale), inc=cm.ss)
            p.dma("scalar", out_r[:, mc, :], evacb[b][:], cm.slot[4 + b])
        return evac

    emit_proj_fm_half(p, cm, wkT, XN, NC_D, TOK, wpan32, wpan16, ps_pair,
                      mk_evac(k0_loc, bias_k, 1.0))
    p.wait("gpsimd", cm.slot[4])
    p.wait("gpsimd", cm.slot[5])
    p.raw("gpsimd", lambda e: e.collective_compute(
        "AllGather", ALU.bypass, replica_groups=GROUPS4,
        ins=[k0_loc.opt()], outs=[k0g.opt()]), cm.cc)
    cm.barrier()

    for v8 in range(8):
        cast_marks = {}
        for ic in range(NC_D):
            b = ic % 2
            if ic >= 2:
                p.wait("sync", cm.sv, cast_marks[ic - 2])
            src = wvT[ic * 128:(ic + 1) * 128, v8 * 512:(v8 + 1) * 512]
            p.dma("sync", wpan32[b][:, 0:512], src, cm.slot[2 + b])
            p.wait("vector", cm.slot[2 + b])
            p.op("vector", lambda e, b=b, ic=ic: e.tensor_copy(
                out=vcache[:, ic, :], in_=wpan32[b][:, 0:512]), inc=cm.sv)
            cast_marks[ic] = cm.sv.n
        p.wait("tensor", cm.sv)
        for tc in range(8):
            ps = ps_v8[tc]
            for ic in range(NC_D):
                p.op("tensor", lambda e, ic=ic, tc=tc, ps=ps, v8=v8: e.matmul(
                    ps[:], XN[:, ic, tc * 128:(tc + 1) * 128],
                    vcache[:, ic, :], start=(ic == 0), stop=False))
            p.op("tensor", lambda e, ps=ps, v8=v8: e.matmul(
                ps[:], cm.ones_r16[:],
                bv_r16[0:1, v8 * 512:(v8 + 1) * 512],
                start=False, stop=True), inc=cm.st)
        p.wait("vector", cm.st)
        for tc in range(8):
            b = tc % 2
            p.wait("vector", cm.slot[4 + b])
            p.op("vector", lambda e, tc=tc, b=b: e.tensor_copy(
                out=evacb[b][:, 0:512], in_=ps_v8[tc][:]), inc=cm.sv)
            p.wait("scalar", cm.sv)
            p.dma("scalar", v0_loc[tc * 128:(tc + 1) * 128,
                                   v8 * 512:(v8 + 1) * 512],
                  evacb[b][:, 0:512], cm.slot[4 + b])
        cm.barrier()

    p.raw("gpsimd", lambda e: e.collective_compute(
        "AllGather", ALU.bypass, replica_groups=GROUPS4,
        ins=[v0_loc.opt()], outs=[v0g.opt()]), cm.cc)
    CC_KV0 = cm.cc.n

    emit_proj_fm_half(p, cm, wqT, XN, NC_D, TOK, wpan32, wpan16, ps_pair,
                      mk_evac(q0T, bias_q, INV_SQRT_DH))
    cm.barrier()

    def emit_dequant(pk_src_fn, sc_of, out_hbm, n_sub, ntiles, tilebase_of):
        shift_marks = {}
        p.wait("sync", cm.slot[0])
        p.wait("sync", cm.slot[1])
        p.wait("sync", cm.sv)
        for s in range(n_sub):
            b = s % 2
            n = ntiles * 64
            if s >= 2:
                p.wait("sync", cm.sv, shift_marks[s - 2])
            srcap = pk_src_fn(s)
            dstap = pkbuf[b][:, 0:n]
            if len(srcap.shape) == 3:
                dstap = dstap.rearrange("p (r c) -> p r c", c=srcap.shape[2])
            p.dma("sync", dstap, srcap, cm.slot[b])
            p.wait("vector", cm.slot[b])
            sc = sc_of(s)
            for half in range(2):
                if half == 0:
                    p.op("vector", lambda e, b=b, n=n: e.tensor_single_scalar(
                        out=dq_i[:, 0:n], in_=pkbuf[b][:, 0:n], scalar=4,
                        op=ALU.arith_shift_right), inc=cm.sv)
                else:
                    p.op("vector", lambda e, b=b, n=n: e.tensor_single_scalar(
                        out=dq_i[:, 0:n], in_=pkbuf[b][:, 0:n], scalar=28,
                        op=ALU.logical_shift_left))
                    p.op("vector", lambda e, n=n: e.tensor_single_scalar(
                        out=dq_i[:, 0:n], in_=dq_i[:, 0:n], scalar=28,
                        op=ALU.arith_shift_right), inc=cm.sv)
                if half == 1:
                    shift_marks[s] = cm.sv.n
                p.wait("scalar", cm.sv)
                p.op("scalar", lambda e, n=n: e.activation(
                    out=dq_f[:, 0:n], in_=dq_i[:, 0:n], func=AF.Copy,
                    bias=0.0, scale=1.0), inc=cm.ss)
                p.wait("vector", cm.ss)
                p.op("vector", lambda e, n=n, nt=ntiles, sc=sc, half=half:
                     e.tensor_mul(
                         dq_w[:, 0:nt, half * 64:half * 64 + 64],
                         dq_f[:, 0:n].rearrange("p (g j) -> p g j", j=64),
                         sc[:, half::2].to_broadcast([128, nt, 64])), inc=cm.sv)
            p.wait("tensor", cm.sv)
            for g0 in range(0, ntiles, 4):
                gb = (g0 // 4) % 2
                ng = min(4, ntiles - g0)
                p.wait("tensor", cm.slot[6 + gb])
                for t in range(ng):
                    p.op("tensor", lambda e, g0=g0, t=t, gb=gb: e.transpose(
                        ps_tr.rearrange("p (u f) -> p u f", f=128)[:, gb * 8 + t, :],
                        dq_w[:, g0 + t, :], cm.ident[:]),
                        inc=cm.st if t == ng - 1 else None)
                p.wait("vector", cm.st)
                p.op("vector", lambda e, gb=gb, n=ng * 128: e.tensor_copy(
                    out=dq_tr[gb][:, 0:n],
                    in_=ps_tr[:, gb * 1024:gb * 1024 + n]), inc=cm.sv)
                p.wait("gpsimd", cm.sv)
                base = tilebase_of(s) + g0
                dst = out_hbm[:, base:base + ng, :]
                p.dma("gpsimd", dst,
                      dq_tr[gb][:, 0:ng * 128].rearrange("i (t h) -> i t h", h=128),
                      cm.slot[6 + gb])

    emit_dequant(
        lambda s: pk1[s * 4096:(s + 1) * 4096, :].rearrange(
            "(p r) c -> p r c", p=128),
        lambda s: dq_sc[:, s * 64:(s + 1) * 64],
        w1t_loc, DQ_HC, NC_D, lambda s: s * NC_D)
    emit_dequant(
        lambda s: pk3[s * 4096:(s + 1) * 4096, :].rearrange(
            "(p r) c -> p r c", p=128),
        lambda s: dq_sc[:, DQ_HC * 64 + s * 64:DQ_HC * 64 + (s + 1) * 64],
        w3t_loc, DQ_HC, NC_D, lambda s: s * NC_D)
    emit_dequant(
        lambda s: pk2[(s // 2) * 128:(s // 2) * 128 + 128,
                      (s % 2) * 43 * 64:(s % 2) * 43 * 64 + 43 * 64],
        lambda s: dq_sc2[:, (s // 2) * 172 + (s % 2) * 86:
                         (s // 2) * 172 + (s % 2) * 86 + 86],
        w2t_loc, DQ_OC * 2, 43, lambda s: (s // 2) * NHC + (s % 2) * 43)

    cm.barrier()

    p.raw("gpsimd", lambda e: e.collective_compute(
        "AllGather", ALU.bypass, replica_groups=GROUPS8,
        ins=[w1t_loc.opt()], outs=[w1g.opt()]), cm.cc)
    p.raw("gpsimd", lambda e: e.collective_compute(
        "AllGather", ALU.bypass, replica_groups=GROUPS8,
        ins=[w3t_loc.opt()], outs=[w3g.opt()]), cm.cc)
    p.raw("gpsimd", lambda e: e.collective_compute(
        "AllGather", ALU.bypass, replica_groups=GROUPS8,
        ins=[w2t_loc.opt()], outs=[w2g.opt()]), cm.cc)
    CC_W = cm.cc.n

    NKC = SEQ // 128
    E2 = [Af[:, 0:16384].rearrange("p (c q) -> p c q", q=512),
          Af[:, 16384:32768].rearrange("p (c q) -> p c q", q=512)]
    M = Bf[:, 0:16384].rearrange("p (c q) -> p c q", q=512)
    Kh = [Bf[:, 16384:20480], Bf[:, 20480:24576]]
    Qh = [Bf[:, 24576:25600], Bf[:, 25600:26624]]
    tb = Bf[:, 26624:27136]
    rowsA = Bf[0:1, 27136:28160].bitcast(F32)
    rowsB0 = Bf[0:1, 28160:29184].bitcast(F32)
    rowsG = Bf[0:1, 29184:30208].bitcast(F32)
    rowsT = Bf[0:1, 30208:31232].bitcast(F32)
    rowsT16 = Bf[0:1, 31232:31744]
    rowsR16 = Bf[0:1, 31744:32256]
    Vh = [Cf[:, 0:4096].rearrange("p (c d) -> p c d", d=128),
          Cf[:, 4096:8192].rearrange("p (c d) -> p c d", d=128)]
    ctxb = [Cf[:, 8192:8704], Cf[:, 8704:9216]]
    rb = Cf[:, 9216:10240].bitcast(F32)
    S4raw = Cf[:, 10240:12288].rearrange("p (c q) -> p c q", q=512)

    ps_s = [pool[:, i * 512:(i + 1) * 512] for i in range(4)]
    ps_ctx = pool[:, 4 * 512:5 * 512]
    ps_r1 = pool[0:1, 5 * 512:6 * 512]
    ps_r2 = pool[0:1, 6 * 512:7 * 512]
    ps_bc2 = pool[:, 7 * 512:8 * 512]

    S4raws = [S4raw, Cf[:, 12288:14336].rearrange("p (c q) -> p c q", q=512)]

    p.wait("sync", cm.cc, CC_KV0)
    p.wait("gpsimd", cm.cc, CC_KV0)
    e_done = {}
    pbank = {}
    tree_mark = [0]

    def a0_stage1(g):
        h, qg = g // 2, g % 2
        hb = h % 2
        E = E2[g % 2]
        S4r = S4raws[g % 2]
        if qg == 0:
            p.wait("sync", cm.st)
            for gg in range(4):
                p.dma("sync", Kh[hb][:, gg * 1024:(gg + 1) * 1024],
                      k0g[gg, h * 128:(h + 1) * 128, :], cm.slot[hb])
                p.dma("sync", Vh[hb][:, gg * 8:(gg + 1) * 8, :],
                      v0g[gg][:, h * 128:(h + 1) * 128].rearrange(
                          "(c p) d -> p c d", p=128), cm.slot[hb])
            p.dma("sync", Qh[hb][:], q0T[h * 128:(h + 1) * 128, :], cm.slot[hb])
        p.wait("tensor", cm.slot[hb])
        for c in range(NKC):
            bk = c % 4
            if bk in pbank:
                p.wait("tensor", cm.ss, pbank[bk][0])
                if pbank[bk][1]:
                    p.wait("tensor", cm.sv, pbank[bk][1])
            p.op("tensor", lambda e, c=c, bk=bk, hb=hb, qg=qg: e.matmul(
                ps_s[bk][:], Kh[hb][:, c * 128:(c + 1) * 128],
                Qh[hb][:, qg * 512:(qg + 1) * 512],
                start=True, stop=True), inc=cm.st)
            p.wait("scalar", cm.st)
            if g >= 2 and c == 0:
                p.wait("scalar", cm.st, e_done[g - 2][0])
                p.wait("scalar", cm.sv, e_done[g - 2][1])
            p.op("scalar", lambda e, c=c, bk=bk, E=E: e.activation(
                out=E[:, c, :], in_=ps_s[bk][:], func=AF.Exp,
                bias=0.0, scale=1.0), inc=cm.ss)
            svm = None
            if c < 4:
                if c == 0:
                    p.wait("vector", cm.ss)
                    p.wait("vector", cm.st)
                else:
                    p.wait("vector", cm.st)
                p.op("vector", lambda e, c=c, bk=bk, S4r=S4r: e.tensor_copy(
                    out=S4r[:, c, :], in_=ps_s[bk][:]), inc=cm.sv)
                svm = cm.sv.n
            pbank[bk] = (cm.ss.n, svm)

    def a0_stage2(g):
        h, qg = g // 2, g % 2
        hb = h % 2
        E = E2[g % 2]
        S4r = S4raws[g % 2]
        p.wait("tensor", cm.sv)
        for c in range(4):
            p.op("tensor", lambda e, c=c, S4r=S4r: e.matmul(
                ps_r1[:], cm.ones_c16[:], S4r[:, c, :],
                start=(c == 0), stop=(c == 3)),
                inc=cm.st if c == 3 else None)
        p.wait("scalar", cm.sv)
        p.op("scalar", lambda e, S4r=S4r: e.activation(
            out=sq[:], in_=S4r.rearrange("p c q -> p (c q)"),
            func=AF.Square, bias=0.0, scale=1.0), inc=cm.ss)
        p.wait("tensor", cm.ss)
        for c in range(4):
            p.op("tensor", lambda e, c=c: e.matmul(
                ps_r2[:], cm.ones_c16[:], sq[:, c * 512:(c + 1) * 512],
                start=(c == 0), stop=(c == 3)),
                inc=cm.st if c == 3 else None)
        p.wait("vector", cm.st)
        p.op("vector", lambda e: e.tensor_scalar_mul(rowsA, ps_r1, 1.0 / 512))
        p.op("vector", lambda e: e.tensor_scalar_mul(rowsB0, ps_r2, 1.0 / 512))
        p.op("vector", lambda e: e.tensor_mul(rowsT, rowsA, rowsA))
        p.op("vector", lambda e: e.tensor_sub(rowsB0, rowsB0, rowsT))
        p.op("vector", lambda e: e.tensor_scalar_add(rowsB0, rowsB0, 1e-6),
             inc=cm.sv)
        p.wait("scalar", cm.sv)
        p.op("scalar", lambda e: e.activation(out=rowsB0, in_=rowsB0,
                                              func=AF.Ln, bias=0.0, scale=1.0))
        p.op("scalar", lambda e: e.activation(out=rowsB0, in_=rowsB0,
                                              func=AF.Exp, bias=0.0,
                                              scale=0.5), inc=cm.ss)
        p.wait("vector", cm.ss)
        p.op("vector", lambda e: e.tensor_scalar_mul(rowsG, rowsB0,
                                                     float(NEWTON_G)))
        p.op("vector", lambda e: e.tensor_scalar_mul(rowsT, rowsB0, float(Z_Q)))
        p.op("vector", lambda e: e.tensor_add(rowsT, rowsT, rowsA),
             inc=cm.sv)
        for it in range(3):
            p.wait("scalar", cm.sv)
            p.op("scalar", lambda e: e.activation(
                out=rowsT16, in_=rowsT, func=AF.Exp, bias=0.0, scale=1.0),
                inc=cm.ss)
            p.wait("tensor", cm.ss)
            p.wait("tensor", cm.sv)
            p.op("tensor", lambda e: e.matmul(
                ps_bc2[:], cm.ones_r16[:], rowsT16[0:1, :],
                start=True, stop=True), inc=cm.st)
            p.wait("vector", cm.st)
            p.op("vector", lambda e: e.tensor_copy(out=tb[:], in_=ps_bc2[:]),
                 inc=cm.sv)
            if it == 2:
                p.op("vector", lambda e, E=E: e.tensor_tensor(
                    out=M, in0=E,
                    in1=tb.rearrange("p (u q) -> p u q", u=1).to_broadcast(
                        [128, NKC, 512]),
                    op=ALU.is_gt))
                p.op("vector", lambda e, E=E: e.tensor_mul(E, M, E),
                     inc=cm.sv)
                break
            p.op("vector", lambda e, E=E: e.tensor_tensor(
                out=M, in0=E,
                in1=tb.rearrange("p (u q) -> p u q", u=1).to_broadcast(
                    [128, NKC, 512]),
                op=ALU.is_gt))
            p.op("vector", lambda e: e.tensor_add(
                M[:, 0:16, :], M[:, 0:16, :], M[:, 16:32, :]))
            p.op("vector", lambda e: e.tensor_add(
                M[:, 0:8, :], M[:, 0:8, :], M[:, 8:16, :]))
            p.op("vector", lambda e: e.tensor_add(
                M[:, 0:4, :], M[:, 0:4, :], M[:, 4:8, :]), inc=cm.sv)
            p.wait("tensor", cm.sv)
            for c in range(4):
                p.op("tensor", lambda e, c=c: e.matmul(
                    ps_r1[:], cm.ones_c16[:], M[:, c, :],
                    start=(c == 0), stop=(c == 3)),
                    inc=cm.st if c == 3 else None)
            p.wait("vector", cm.st)
            p.op("vector", lambda e: e.tensor_scalar_add(rowsB0, ps_r1,
                                                         -float(TOPK)))
            p.op("vector", lambda e: e.tensor_mul(rowsB0, rowsB0, rowsG))
            p.op("vector", lambda e: e.tensor_add(rowsT, rowsT, rowsB0),
                 inc=cm.sv)
        sm_mark = cm.sv.n
        p.wait("tensor", cm.sv, sm_mark)
        for c in range(NKC):
            p.op("tensor", lambda e, c=c, hb=hb, E=E: e.matmul(
                ps_ctx[:], Vh[hb][:, c, :], E[:, c, :],
                start=(c == 0), stop=(c == NKC - 1)))
        for c in range(NKC):
            p.op("tensor", lambda e, c=c, E=E: e.matmul(
                ps_r2[:], cm.ones_c16[:], E[:, c, :],
                start=(c == 0), stop=(c == NKC - 1)),
                inc=cm.st if c == NKC - 1 else None)
        pv_mark = cm.st.n
        e_done[g] = (pv_mark, sm_mark)
        p.wait("vector", cm.st)
        p.op("vector", lambda e: e.reciprocal(out=rowsB0, in_=ps_r2))
        p.op("vector", lambda e: e.tensor_copy(out=rowsR16, in_=rowsB0),
             inc=cm.sv)
        p.wait("tensor", cm.sv)
        p.op("tensor", lambda e: e.matmul(ps_bc2[:], cm.ones_r16[:],
                                          rowsR16[0:1, :],
                                          start=True, stop=True), inc=cm.st)
        p.wait("vector", cm.st)
        p.op("vector", lambda e: e.tensor_copy(out=rb[:], in_=ps_bc2[:]))
        gb = qg % 2
        p.wait("vector", cm.slot[4 + gb])
        p.op("vector", lambda e, gb=gb: e.tensor_mul(ctxb[gb][:], ps_ctx[:],
                                                     rb[:]), inc=cm.sv)
        p.wait("scalar", cm.sv)
        p.dma("scalar", ctxT[h * 128:(h + 1) * 128, qg * 512:(qg + 1) * 512],
              ctxb[gb][:], cm.slot[4 + gb])

    for g in range(2 * NH0):
        a0_stage1(g)
        if g >= 1:
            a0_stage2(g - 1)
    a0_stage2(2 * NH0 - 1)
    p.wait("sync", cm.slot[4])
    p.wait("sync", cm.slot[5])
    cm.barrier()

    A = Af.rearrange("p (c t) -> p c t", t=TOK)
    B = Bf.rearrange("p (c t) -> p c t", t=TOK)
    scr8 = Cf[:, 0:8192]
    xbuf3 = [scr8[:, 0:2048].bitcast(F32), scr8[:, 2048:4096].bitcast(F32)]
    xb16 = scr8[:, 4096:5120]
    w13 = [scr8[:, 0:4096].rearrange("p (ic m) -> p ic m", m=128),
           scr8[:, 4096:8192].rearrange("p (ic m) -> p ic m", m=128)]
    vcache3 = scr8[:, 0:8192].rearrange("p (ic v) -> p ic v", v=256)
    wpan32m = [Cf[:, 8192:10240].bitcast(F32), Cf[:, 10240:12288].bitcast(F32)]
    wpan16m = [Cf[:, 12288:16384], Cf[:, 16384:20480]]
    htile = [Cf[:, 20480:21504], Cf[:, 21504:22528]]
    w2til = [Cf[:, 22528:23040].rearrange("p (o h) -> p o h", h=128),
             Cf[:, 23040:23552].rearrange("p (o h) -> p o h", h=128)]
    rtmp = sq

    ps_up = [[pool[:, 0:1024], pool[:, 1024:2048]],
             [pool[:, 2048:3072], pool[:, 3072:4096]]]
    ps_dn = [pool[:, i * 512:(i + 1) * 512] for i in range(8)]
    ps_v4 = [pool[:, i * 256:(i + 1) * 256] for i in range(8)]

    ctr = ctxT.rearrange("(c p) t -> p c t", p=128)
    for c in range(NC_D):
        p.dma("sync", A[:, c, :], ctr[:, c, :], cm.ld)
    cm.barrier()

    xr = xT.rearrange("(c p) t -> p c t", p=128)

    def evac_oproj(mc, ps):
        b = mc % 2
        p.wait("sync", cm.sv)
        p.dma("sync", xbuf3[b][:], xr[:, mc, :], cm.slot[6 + b])
        p.wait("scalar", cm.st)
        p.op("scalar", lambda e, mc=mc, ps=ps, b=b: e.activation(
            out=evacb[b][:], in_=ps[:], func=AF.Identity,
            bias=bias_o[:, mc:mc + 1], scale=1.0), inc=cm.ss)
        p.wait("vector", cm.slot[6 + b])
        p.wait("vector", cm.ss)
        p.op("vector", lambda e, b=b: e.tensor_copy(out=xb16[:], in_=xbuf3[b][:]))
        p.op("vector", lambda e, mc=mc, b=b: e.tensor_add(
            B[:, mc, :], evacb[b][:], xb16[:]), inc=cm.sv)

    emit_proj_fm_half(p, cm, woT, A, NC_D, TOK, wpan32m, wpan16m, ps_pair,
                      evac_oproj)
    cm.barrier()

    emit_ln_resident(p, cm, B, A, TOK, sq, rows, ps_s1, ps_s2, ps_bc,
                     bcast_r16, bcast_m16, rows16, xbuf3[0])
    cm.barrier()

    w13p = [w13,
            [Cf[:, 12288:16384].rearrange("p (ic m) -> p ic m", m=128),
             Cf[:, 16384:20480].rearrange("p (ic m) -> p ic m", m=128)]]

    def w13_src(wg, hc):
        core, qq = hc // DQ_HC, hc % DQ_HC
        return wg[core][:, qq * NC_D:(qq + 1) * NC_D, :]

    p.wait("sync", cm.cc, CC_W)
    p.wait("gpsimd", cm.cc, CC_W)
    up_marks = {}
    fm_marks = {}
    st_base_up = cm.st.n
    for hc in range(NHC):
        ub = hc % 2
        w13c = w13p[ub]
        p.wait("sync", cm.st, fm_marks.get(hc - 2, st_base_up))
        p.wait("gpsimd", cm.st, fm_marks.get(hc - 2, st_base_up))
        p.dma("sync", w13c[0], w13_src(w1g, hc), cm.slot[0 + ub])
        p.dma("gpsimd", w13c[1], w13_src(w3g, hc), cm.slot[2 + ub])
        p.wait("tensor", cm.slot[0 + ub])
        p.wait("tensor", cm.slot[2 + ub])
        if hc >= 2:
            p.wait("tensor", cm.ss, up_marks[hc - 2][0])
            p.wait("tensor", cm.sv, up_marks[hc - 2][1])
        psA, psB = ps_up[ub]
        for ic in range(NC_D):
            for hh in range(2):
                p.op("tensor", lambda e, ic=ic, hh=hh, psA=psA, w13c=w13c:
                     e.matmul(
                         psA[:, hh * 512:(hh + 1) * 512], w13c[0][:, ic, :],
                         A[:, ic, hh * 512:(hh + 1) * 512],
                         start=(ic == 0), stop=(ic == NC_D - 1)))
        for ic in range(NC_D):
            for hh in range(2):
                p.op("tensor", lambda e, ic=ic, hh=hh, psB=psB, w13c=w13c:
                     e.matmul(
                         psB[:, hh * 512:(hh + 1) * 512], w13c[1][:, ic, :],
                         A[:, ic, hh * 512:(hh + 1) * 512],
                         start=(ic == 0), stop=(ic == NC_D - 1)),
                     inc=cm.st if (ic == NC_D - 1 and hh == 1) else None)
        fm_marks[hc] = cm.st.n
        p.wait("scalar", cm.st)
        p.wait("scalar", cm.slot[4 + ub])
        p.op("scalar", lambda e, psA=psA, ub=ub: e.activation(
            out=htile[ub][:], in_=psA[:], func=AF.Silu, bias=0.0, scale=1.0),
            inc=cm.ss)
        s_mark = cm.ss.n
        p.wait("vector", cm.ss)
        p.op("vector", lambda e, psB=psB, ub=ub: e.tensor_copy(
            out=evacb[ub][:], in_=psB[:]))
        p.op("vector", lambda e, ub=ub: e.tensor_mul(
            htile[ub][:], htile[ub][:], evacb[ub][:]), inc=cm.sv)
        up_marks[hc] = (s_mark, cm.sv.n)
        p.wait("scalar", cm.sv)
        p.dma("scalar", hsp[hc], htile[ub][:], cm.slot[4 + ub])
    cm.barrier()

    def w2_src(quad, hc):
        return w2g[quad].rearrange("p (q h) m -> p q h m", h=NHC)[:, :, hc, :]

    dn_marks = {}
    st_base_dn = cm.st.n
    for quad in range(8):
        p.wait("tensor", cm.sv)
        for hc in range(NHC):
            db = hc % 2
            it = quad * NHC + hc
            p.wait("sync", cm.st, dn_marks.get(it - 2, st_base_dn))
            p.wait("gpsimd", cm.st, dn_marks.get(it - 2, st_base_dn))
            p.dma("sync", htile[db][:], hsp[hc], cm.slot[db])
            p.dma("gpsimd", w2til[db][:], w2_src(quad, hc), cm.slot[db])
            p.wait("tensor", cm.slot[db])
            for o in range(4):
                for hh in range(2):
                    p.op("tensor",
                         lambda e, o=o, hh=hh, db=db, hc=hc: e.matmul(
                             ps_dn[o * 2 + hh][:], w2til[db][:, o, :],
                             htile[db][:, hh * 512:(hh + 1) * 512],
                             start=(hc == 0), stop=(hc == NHC - 1)),
                         inc=cm.st if (o == 3 and hh == 1) else None)
            dn_marks[it] = cm.st.n
        p.wait("vector", cm.st)
        for o in range(4):
            mc = quad * 4 + o
            p.op("vector", lambda e, o=o: e.tensor_copy(
                out=evacb[o % 2][:, 0:512], in_=ps_dn[o * 2][:]))
            p.op("vector", lambda e, o=o: e.tensor_copy(
                out=evacb[o % 2][:, 512:1024], in_=ps_dn[o * 2 + 1][:]))
            p.op("vector", lambda e, o=o, mc=mc: e.tensor_add(
                A[:, mc, :], evacb[o % 2][:], B[:, mc, :]),
                inc=cm.sv if o == 3 else None)
    cm.barrier()

    for c in range(NC_D):
        p.dma("sync", x2T.rearrange("(c p) t -> p c t", p=128)[:, c, :],
              A[:, c, :], cm.ld)
    emit_ln_resident(p, cm, A, B, TOK, sq, rows, ps_s1, ps_s2, ps_bc,
                     bcast_r16, bcast_m16, rows16, xbuf3[0])
    cm.barrier()

    cos_t = ropeb[0:64, 0:TOK]
    sin_t = ropeb[0:64, TOK:2 * TOK]
    cos_t64 = ropeb[64:128, 0:TOK]
    sin_t64 = ropeb[64:128, TOK:2 * TOK]

    def mk_evac_rope(out_hbm):
        out_r = out_hbm.rearrange("(c p) t -> p c t", p=128)

        def evac(mc, ps):
            b = mc % 2
            p.wait("vector", cm.st)
            p.wait("vector", cm.slot[4 + b])
            p.op("vector", lambda e, ps=ps, b=b: e.tensor_copy(
                out=evacb[b][:], in_=ps[:]), inc=cm.ss)
            x1 = evacb[b][0:64, :]
            x2 = evacb[b][64:128, :]
            t1 = rtmp[0:64, 0:TOK]
            t2 = rtmp[0:64, TOK:2 * TOK]
            t3 = rtmp[64:128, 0:TOK]
            p.op("vector", lambda e, x1=x1: e.tensor_mul(t1, x1, cos_t))
            p.op("vector", lambda e, x2=x2: e.tensor_mul(t2, x2, sin_t64))
            p.op("vector", lambda e, x2=x2: e.tensor_mul(x2, x2, cos_t64))
            p.op("vector", lambda e, x1=x1: e.tensor_mul(t3, x1, sin_t))
            p.op("vector", lambda e, x2=x2: e.tensor_add(x2, x2, t3))
            p.op("vector", lambda e, x1=x1: e.tensor_sub(x1, t1, t2),
                 inc=cm.ss)
            p.wait("scalar", cm.ss)
            p.dma("scalar", out_r[:, mc, :], evacb[b][:], cm.slot[4 + b])
        return evac

    emit_proj_fm_half(p, cm, wk1, B, 8, TOK, wpan32m, wpan16m, ps_pair,
                      mk_evac_rope(k1_loc))
    cm.barrier()

    for v8 in range(4):
        cast_marks = {}
        for ic in range(NC_D):
            b = ic % 2
            if ic >= 2:
                p.wait("sync", cm.sv, cast_marks[ic - 2])
            src = wv1[ic * 128:(ic + 1) * 128, v8 * 256:(v8 + 1) * 256]
            p.dma("sync", wpan32m[b][:, 0:256], src, cm.slot[2 + b])
            p.wait("vector", cm.slot[2 + b])
            p.op("vector", lambda e, b=b, ic=ic: e.tensor_copy(
                out=vcache3[:, ic, :], in_=wpan32m[b][:, 0:256]), inc=cm.sv)
            cast_marks[ic] = cm.sv.n
        p.wait("tensor", cm.sv)
        for tc in range(8):
            ps = ps_v4[tc]
            for ic in range(NC_D):
                p.op("tensor", lambda e, ic=ic, tc=tc, ps=ps: e.matmul(
                    ps[:], B[:, ic, tc * 128:(tc + 1) * 128],
                    vcache3[:, ic, :],
                    start=(ic == 0), stop=(ic == NC_D - 1)),
                    inc=cm.st if ic == NC_D - 1 else None)
        p.wait("vector", cm.st)
        for tc in range(8):
            b = tc % 2
            p.wait("vector", cm.slot[4 + b])
            p.op("vector", lambda e, tc=tc, b=b: e.tensor_copy(
                out=evacb[b][:, 0:256], in_=ps_v4[tc][:]), inc=cm.sv)
            p.wait("scalar", cm.sv)
            p.dma("scalar", v1_loc[tc * 128:(tc + 1) * 128,
                                   v8 * 256:(v8 + 1) * 256],
                  evacb[b][:, 0:256], cm.slot[4 + b])
        cm.barrier()

    p.raw("gpsimd", lambda e: e.collective_compute(
        "AllGather", ALU.bypass, replica_groups=GROUPS4,
        ins=[k1_loc.opt()], outs=[k1g.opt()]), cm.cc)
    p.raw("gpsimd", lambda e: e.collective_compute(
        "AllGather", ALU.bypass, replica_groups=GROUPS4,
        ins=[v1_loc.opt()], outs=[v1g.opt()]), cm.cc)
    CC_KV1 = cm.cc.n

    emit_proj_fm_half(p, cm, wq1, B, NC_D, TOK, wpan32m, wpan16m, ps_pair,
                      mk_evac_rope(q1T))
    cm.barrier()

    S4 = Af[:, 0:16384].rearrange("p (c q) -> p c q", q=512)
    E4 = Af[:, 16384:32768].rearrange("p (c q) -> p c q", q=512)
    MK = Bf[:, 0:16384].rearrange("p (c q) -> p c q", q=512)
    Kh4 = [Bf[:, 16384:20480], Bf[:, 16384:20480]]
    Vh4 = [Bf[:, 20480:24576].rearrange("p (c d) -> p c d", d=128)] * 2
    Qh4 = [Bf[:, 24576 + i * 512:24576 + (i + 1) * 512] for i in range(2)]
    rowsB4 = Bf[0:1, 25600:26624].bitcast(F32)
    rowsR16_4 = Bf[0:1, 26624:27136]
    rb4 = Bf[:, 27136:28160].bitcast(F32)
    ctxb4 = [Bf[:, 28160 + i * 512:28160 + (i + 1) * 512] for i in range(2)]

    p.wait("sync", cm.cc, CC_KV1)
    e_done4 = {}
    pbank4 = {}
    g4 = 0
    for qg in range(2):
        p.wait("sync", cm.sv)
        p.wait("sync", cm.st)
        for c in range(NC_D):
            p.dma("sync", MK[:, c, :], lmask[qg, c], cm.ld)
        p.wait("vector", cm.ld)
        for h in range(NH0):
            kv = h // 4
            eb4 = g4 % 2
            E4c = [S4, E4][eb4]
            if h % 4 == 0:
                p.wait("sync", cm.st)
                for g in range(4):
                    p.dma("sync", Kh4[0][:, g * 1024:(g + 1) * 1024],
                          k1g[g, kv * 128:(kv + 1) * 128, :], cm.slot[0])
                    p.dma("sync", Vh4[0][:, g * 8:(g + 1) * 8, :],
                          v1g[g][:, kv * 128:(kv + 1) * 128].rearrange(
                              "(c p) d -> p c d", p=128), cm.slot[0])
            p.wait("sync", cm.st)
            p.dma("sync", Qh4[h % 2][:],
                  q1T[h * 128:(h + 1) * 128, qg * 512:(qg + 1) * 512],
                  cm.slot[2 + (h % 2)])
            kb = 0
            p.wait("tensor", cm.slot[0])
            p.wait("tensor", cm.slot[2 + (h % 2)])
            for c in range(NC_D):
                bk = c % 4
                if bk in pbank4:
                    p.wait("tensor", cm.sv, pbank4[bk])
                p.op("tensor", lambda e, c=c, bk=bk, kb=kb, h=h: e.matmul(
                    ps_s[bk][:], Kh4[kb][:, c * 128:(c + 1) * 128],
                    Qh4[h % 2][:], start=True, stop=True), inc=cm.st)
                p.wait("vector", cm.st)
                if g4 >= 2 and c == 0:
                    p.wait("vector", cm.st, e_done4[g4 - 2])
                p.op("vector", lambda e, c=c, bk=bk, E4c=E4c: e.tensor_tensor(
                    out=E4c[:, c, :], in0=ps_s[bk][:], in1=MK[:, c, :],
                    op=ALU.add), inc=cm.sv)
                pbank4[bk] = cm.sv.n
                if c % 4 == 3:
                    p.wait("scalar", cm.sv)
                    p.op("scalar", lambda e, c=c, E4c=E4c: e.activation(
                        out=E4c.rearrange("p c q -> p (c q)")[
                            :, (c - 3) * 512:(c + 1) * 512],
                        in_=E4c.rearrange("p c q -> p (c q)")[
                            :, (c - 3) * 512:(c + 1) * 512],
                        func=AF.Exp, bias=0.0, scale=1.0), inc=cm.ss)
            p.wait("tensor", cm.ss)
            p.wait("tensor", cm.sv)
            for c in range(NC_D):
                p.op("tensor", lambda e, c=c, kb=kb, E4c=E4c: e.matmul(
                    ps_ctx[:], Vh4[kb][:, c, :], E4c[:, c, :],
                    start=(c == 0), stop=(c == NC_D - 1)))
            for c in range(NC_D):
                p.op("tensor", lambda e, c=c, E4c=E4c: e.matmul(
                    ps_r2[:], cm.ones_c16[:], E4c[:, c, :],
                    start=(c == 0), stop=(c == NC_D - 1)),
                    inc=cm.st if c == NC_D - 1 else None)
            e_done4[g4] = cm.st.n
            p.wait("vector", cm.st)
            p.op("vector", lambda e: e.reciprocal(out=rowsB4, in_=ps_r2))
            p.op("vector", lambda e: e.tensor_copy(out=rowsR16_4, in_=rowsB4),
                 inc=cm.sv)
            p.wait("tensor", cm.sv)
            p.op("tensor", lambda e: e.matmul(ps_bc2[:], cm.ones_r16[:],
                                              rowsR16_4[0:1, :],
                                              start=True, stop=True), inc=cm.st)
            p.wait("vector", cm.st)
            p.op("vector", lambda e: e.tensor_copy(out=rb4[:], in_=ps_bc2[:]))
            gb = h % 2
            p.wait("vector", cm.slot[4 + gb])
            p.op("vector", lambda e, gb=gb: e.tensor_mul(ctxb4[gb][:], ps_ctx[:],
                                                         rb4[:]), inc=cm.sv)
            p.wait("scalar", cm.sv)
            p.dma("scalar", ctxs[h * 128:(h + 1) * 128,
                                 qg * 512:(qg + 1) * 512],
                  ctxb4[gb][:], cm.slot[4 + gb])
            g4 += 1
    p.wait("sync", cm.slot[4])
    p.wait("sync", cm.slot[5])
    cm.barrier()

    wpan32f = [Cf[:, 10240:12288].bitcast(F32), Cf[:, 12288:14336].bitcast(F32)]
    wpan16f = [Cf[:, 14336:18432], Cf[:, 18432:22528]]
    x2b = [Cf[:, 0:1024], Cf[:, 1024:2048]]
    _f = Cf[:, 22528:24576].bitcast(F32)
    fin32 = [_f, _f]

    ctr2 = ctxs.rearrange("(c p) t -> p c t", p=128)
    for c in range(NC_D):
        p.dma("sync", A[:, c, :], ctr2[:, c, :], cm.ld)
    cm.barrier()

    x2r = x2T.rearrange("(c p) t -> p c t", p=128)

    def evac_oproj1(mc, ps):
        b = mc % 2
        p.wait("sync", cm.sv)
        p.dma("sync", x2b[b][:], x2r[:, mc, :], cm.slot[6 + b])
        p.wait("vector", cm.st)
        p.op("vector", lambda e, ps=ps, b=b: e.tensor_copy(
            out=evacb[b][:], in_=ps[:]), inc=cm.ss)
        p.wait("vector", cm.slot[6 + b])
        p.op("vector", lambda e, mc=mc, b=b: e.tensor_add(
            B[:, mc, :], evacb[b][:], x2b[b][:]), inc=cm.sv)

    emit_proj_fm_half(p, cm, wo1, A, NC_D, TOK, wpan32f, wpan16f, ps_pair,
                      evac_oproj1)
    cm.barrier()

    emit_ln_resident(p, cm, B, A, TOK, sq, rows, ps_s1, ps_s2, ps_bc,
                     bcast_r16, bcast_m16, rows16, Cf[:, 0:2048].bitcast(F32))
    cm.barrier()

    rh16 = Cf[0:64, 14336:15360]
    lgr = Cf[0:16, 15360:17408].bitcast(F32)
    lgn = Cf[:, 17408:17664].bitcast(F32).rearrange("p (c e) -> p c e", e=16)
    cmbn = Cf[:, 17664:17920].bitcast(F32).rearrange("p (c e) -> p c e", e=16)
    cmbT = Cf[0:16, 17920:18944]
    mx8 = Cf[:, 18944:18960].bitcast(F32)
    nm = Cf[:, 18960:18968].bitcast(F32)
    id16 = Cf[0:16, 18968:19000].bitcast(F32)
    rhs_stk = Cf[:, 2048:10240].rearrange("p (c t) -> p c t", t=TOK)
    span = x2b[0]

    i16 = nc.inline_tensor(np.eye(16, dtype=np.float32), name="c_i16")
    selnp = np.zeros((16, 16, 64), dtype=BF16NP)
    for e_ in range(16):
        selnp[e_, e_, :] = 1
    sel_t = nc.inline_tensor(selnp, name="c_sel")
    sel = lgr.bitcast(BF16)[:, 0:1024].rearrange("p (a m) -> p a m", m=64)
    p.dma("sync", id16[:], i16.ap(), cm.ld)
    cm.barrier()

    for (W, nmo, dst_ps) in ((aT, 64, pool[0:64, 2048:3072]),
                             (rwT, 16, pool[0:16, 3072:4096])):
        t_marks = {}
        w_base = cm.st.n
        for ic in range(NC_D):
            b = ic % 2
            p.wait("sync", cm.sv)
            p.dma("sync", wpan32f[b][:, 0:nmo], W[ic * 128:(ic + 1) * 128, :],
                  cm.slot[2 + b])
            p.wait("vector", cm.slot[2 + b])
            p.wait("vector", cm.st, t_marks[ic - 2] if ic >= 2 else w_base)
            p.op("vector", lambda e, b=b, nmo=nmo: e.tensor_copy(
                out=span[:, b * 512:b * 512 + nmo],
                in_=wpan32f[b][:, 0:nmo]), inc=cm.sv)
            p.wait("tensor", cm.sv)
            for hh in range(2):
                p.op("tensor", lambda e, ic=ic, hh=hh, b=b, nmo=nmo,
                     dst_ps=dst_ps: e.matmul(
                         dst_ps[:, hh * 512:(hh + 1) * 512],
                         span[:, b * 512:b * 512 + nmo],
                         A[:, ic, hh * 512:(hh + 1) * 512],
                         start=(ic == 0), stop=(ic == NC_D - 1)),
                     inc=cm.st if hh == 1 else None)
            t_marks[ic] = cm.st.n
    p.wait("vector", cm.st)
    p.op("vector", lambda e: e.tensor_copy(out=rh16[:], in_=pool[0:64, 2048:3072]))
    p.op("vector", lambda e: e.tensor_copy(out=lgr[:], in_=pool[0:16, 3072:4096]),
         inc=cm.sv)
    cm.barrier()

    for tc in range(8):
        p.wait("tensor", cm.sv)
        p.op("tensor", lambda e, tc=tc: e.transpose(
            pool[:, 2048 + (tc % 2) * 512:2048 + (tc % 2) * 512 + 16],
            lgr[:, tc * 128:(tc + 1) * 128], id16[:]), inc=cm.st)
        p.wait("vector", cm.st)
        p.op("vector", lambda e, tc=tc: e.tensor_copy(
            out=lgn[:, tc, :], in_=pool[:, 2048 + (tc % 2) * 512:
                                        2048 + (tc % 2) * 512 + 16]),
            inc=cm.sv)
    p.wait("vector", cm.sv)
    for tc in range(8):
        p.op("vector", lambda e, tc=tc: e.reduce_max(
            nm[:, 0:1], lgn[:, tc, :], axis=mybir.AxisListType.X))
        p.op("vector", lambda e: e.tensor_scalar_mul(nm[:, 0:1], nm[:, 0:1],
                                                     -1.0), inc=cm.sv)
        p.wait("scalar", cm.sv)
        p.op("scalar", lambda e, tc=tc: e.activation(
            out=cmbn[:, tc, :], in_=lgn[:, tc, :], func=AF.Exp,
            bias=nm[:, 0:1], scale=1.0, accum_out=nm[:, 1:2]), inc=cm.ss)
        p.wait("vector", cm.ss)
        p.op("vector", lambda e: e.reciprocal(out=nm[:, 1:2], in_=nm[:, 1:2]))
        p.op("vector", lambda e, tc=tc: e.tensor_scalar(
            out=cmbn[:, tc, :], in0=cmbn[:, tc, :], scalar1=nm[:, 1:2],
            scalar2=None, op0=ALU.mult))
        p.op("vector", lambda e, tc=tc: e.max(out=mx8[:], in_=cmbn[:, tc, :]))
        p.op("vector", lambda e: e.tensor_add(nm[:, 2:3], mx8[:, 0:1],
                                              mx8[:, 1:2]))
        p.op("vector", lambda e: e.tensor_scalar_add(nm[:, 2:3], nm[:, 2:3],
                                                     1e-6))
        p.op("vector", lambda e: e.reciprocal(out=nm[:, 2:3], in_=nm[:, 2:3]))
        p.op("vector", lambda e: e.tensor_scalar_mul(nm[:, 2:3], nm[:, 2:3],
                                                     0.5))
        p.op("vector", lambda e, tc=tc: e.tensor_scalar(
            out=lgn[:, tc, :], in0=cmbn[:, tc, :], scalar1=mx8[:, 0:1],
            scalar2=None, op0=ALU.is_equal))
        p.op("vector", lambda e, tc=tc: e.tensor_scalar(
            out=lgn[:, tc, :], in0=lgn[:, tc, :], scalar1=mx8[:, 0:1],
            scalar2=None, op0=ALU.mult))
        p.op("vector", lambda e, tc=tc: e.tensor_scalar(
            out=cmbn[:, tc, :], in0=cmbn[:, tc, :],
            scalar1=mx8[:, 1:2], scalar2=None, op0=ALU.is_equal))
        p.op("vector", lambda e, tc=tc: e.tensor_scalar(
            out=cmbn[:, tc, :], in0=cmbn[:, tc, :], scalar1=mx8[:, 1:2],
            scalar2=None, op0=ALU.mult))
        p.op("vector", lambda e, tc=tc: e.tensor_add(cmbn[:, tc, :],
                                                     cmbn[:, tc, :],
                                                     lgn[:, tc, :]))
        p.op("vector", lambda e, tc=tc: e.tensor_scalar(
            out=cmbn[:, tc, :], in0=cmbn[:, tc, :], scalar1=nm[:, 2:3],
            scalar2=None, op0=ALU.mult), inc=cm.sv)
    cm.barrier()
    for tc in range(8):
        p.op("vector", lambda e, tc=tc: e.tensor_copy(
            out=span[:, (tc % 2) * 512:(tc % 2) * 512 + 16],
            in_=cmbn[:, tc, :]), inc=cm.sv)
        p.wait("tensor", cm.sv)
        psb = pool[:, 2048 + (tc % 2) * 512:2048 + (tc % 2) * 512 + 256].bitcast(BF16)
        p.op("tensor", lambda e, tc=tc, psb=psb: e.transpose(
            psb[0:16, 0:128], span[:, (tc % 2) * 512:(tc % 2) * 512 + 16],
            cm.ident[:]), inc=cm.st)
        p.wait("vector", cm.st)
        p.op("vector", lambda e, tc=tc, psb=psb: e.tensor_copy(
            out=cmbT[:, tc * 128:(tc + 1) * 128],
            in_=psb[0:16, 0:128]), inc=cm.sv)
    cm.barrier()

    p.dma("sync", sel, sel_t.ap(), cm.ld)
    cm.barrier()
    for ee in range(16):
        rc, half = ee // 2, ee % 2
        p.wait("tensor", cm.sv)
        p.op("tensor", lambda e, ee=ee: e.matmul(
            pool[0:64, 2048 + (ee % 2) * 512:2048 + (ee % 2) * 512 + 512],
            sel[:, ee, :], cmbT[:, 0:512],
            start=True, stop=True))
        p.op("tensor", lambda e, ee=ee: e.matmul(
            pool[0:64, 3072 + (ee % 2) * 512:3072 + (ee % 2) * 512 + 512],
            sel[:, ee, :], cmbT[:, 512:1024],
            start=True, stop=True), inc=cm.st)
        p.wait("vector", cm.st)
        p.op("vector", lambda e, ee=ee: e.tensor_copy(
            out=span[0:64, 0:512],
            in_=pool[0:64, 2048 + (ee % 2) * 512:2048 + (ee % 2) * 512 + 512]))
        p.op("vector", lambda e, ee=ee: e.tensor_copy(
            out=span[0:64, 512:1024],
            in_=pool[0:64, 3072 + (ee % 2) * 512:3072 + (ee % 2) * 512 + 512]),
            inc=cm.sv)
        p.op("vector", lambda e, ee=ee, rc=rc, half=half: e.tensor_mul(
            rhs_stk[half * 64:half * 64 + 64, rc, :], rh16[:],
            span[0:64, 0:1024]), inc=cm.sv)
    cm.barrier()

    bpan = [sq[:, i * 1024:(i + 1) * 1024].rearrange("p (rc m) -> p rc m", m=128)
            for i in range(2)]

    def evac_fin(mc, ps):
        b = mc % 2
        p.wait("scalar", cm.st)
        p.op("scalar", lambda e, mc=mc, ps=ps, b=b: e.activation(
            out=evacb[b][:], in_=ps[:], func=AF.Identity,
            bias=bias_b[:, mc:mc + 1], scale=1.0), inc=cm.ss)
        p.wait("vector", cm.ss)
        p.wait("vector", cm.slot[6])
        p.op("vector", lambda e, mc=mc, b=b: e.tensor_add(
            fin32[b][:], evacb[b][:], B[:, mc, :]), inc=cm.sv)
        p.wait("scalar", cm.sv)
        p.dma("scalar", outT.rearrange("(c p) t -> p c t", p=128)[:, mc, :],
              fin32[b][:], cm.slot[6])

    cast_marks = {}
    group_marks = {}
    evac_marks = {}
    pctr = 0
    pmark = {0: 0, 1: 0}
    for mc in range(NC_D):
        b = mc % 2
        for part in range(5):
            pb = pctr % 2
            if part < 4:
                lo = part * 8
                srcw = bwT[lo * 128:(lo + 8) * 128,
                           mc * 128:(mc + 1) * 128].rearrange(
                    "(ic p) m -> p ic m", p=128)
            else:
                srcw = bstk[:, mc * 128:(mc + 1) * 128].rearrange(
                    "(ic p) m -> p ic m", p=128)
            p.wait("sync", cm.sv, pmark[pb])
            p.dma("sync", wpan32f[pb][:, 0:1024].rearrange(
                "p (ic m) -> p ic m", m=128), srcw, cm.slot[2 + pb])
            p.wait("vector", cm.slot[2 + pb])
            if mc >= 2 and part == 0:
                p.wait("vector", cm.st, group_marks[mc - 2])
            if part < 4:
                p.op("vector", lambda e, b=b, pb=pb, lo=lo: e.tensor_copy(
                    out=wpan16f[b][:, lo * 128:lo * 128 + 1024],
                    in_=wpan32f[pb][:, 0:1024]), inc=cm.sv)
            else:
                p.op("vector", lambda e, b=b, pb=pb: e.tensor_copy(
                    out=bpan[b].rearrange("p ic m -> p (ic m)"),
                    in_=wpan32f[pb][:, 0:1024]), inc=cm.sv)
            pmark[pb] = cm.sv.n
            pctr += 1
        cast_marks[mc] = cm.sv.n
        ps = ps_pair[mc % 2]
        p.wait("tensor", cm.sv)
        if mc >= 2:
            p.wait("tensor", cm.ss, evac_marks[mc - 2])
        for ic in range(NC_D):
            for hh in range(2):
                p.op("tensor", lambda e, b=b, ic=ic, hh=hh, ps=ps: e.matmul(
                    ps[:, hh * 512:(hh + 1) * 512],
                    wpan16f[b][:, ic * 128:(ic + 1) * 128],
                    A[:, ic, hh * 512:(hh + 1) * 512],
                    start=(ic == 0), stop=False))
        for rc in range(8):
            for hh in range(2):
                p.op("tensor", lambda e, b=b, rc=rc, hh=hh, ps=ps: e.matmul(
                    ps[:, hh * 512:(hh + 1) * 512], bpan[b][:, rc, :],
                    rhs_stk[:, rc, hh * 512:(hh + 1) * 512],
                    start=False, stop=(rc == 7)),
                    inc=cm.st if (rc == 7 and hh == 1) else None)
        group_marks[mc] = cm.st.n
        evac_fin(mc, ps)
        evac_marks[mc] = cm.ss.n

    p.wait("sync", cm.slot[6])
    p.wait("sync", cm.slot[7])
    cm.barrier()
    p.run()
    return nc



def _f32(a):
    return np.ascontiguousarray(a, dtype=np.float32)


def _ci32(a):
    return np.ascontiguousarray(a, dtype=np.int32)


def w13_chunks(core):
    return [(core * DQ_HC + i) if (core * DQ_HC + i) < NHC else 0
            for i in range(DQ_HC)]


_NC_CACHE = {}


def _get_nc():
    if "full" not in _NC_CACHE:
        _NC_CACHE["full"] = build_full()
    return _NC_CACHE["full"]


def kernel(**inputs):
    x = np.asarray(inputs["L0_attn_hidden_states"])
    wqT = _f32(np.asarray(inputs["L0_attn_wq"]).T)
    wkT = _f32(np.asarray(inputs["L0_attn_wk"]).T)
    wvT = _f32(np.asarray(inputs["L0_attn_wv"]).T)
    bq_r = _f32(np.asarray(inputs["L0_attn_bq"]).reshape(NC_D, 128).T)
    bk_r = _f32(np.asarray(inputs["L0_attn_bk"]).reshape(NC_D, 128).T)
    bv_row = _f32(np.asarray(inputs["L0_attn_bv"]).reshape(1, D))
    w1 = np.asarray(inputs["L0_ffn_w1"])
    s1 = np.asarray(inputs["L0_ffn_s1"])
    w2 = np.asarray(inputs["L0_ffn_w2"])
    s2 = np.asarray(inputs["L0_ffn_s2"])
    w3 = np.asarray(inputs["L0_ffn_w3"])
    s3 = np.asarray(inputs["L0_ffn_s3"])
    woT = _f32(np.asarray(inputs["L0_attn_wo"]).T)
    bo_r = _f32(np.asarray(inputs["L0_attn_bo"]).reshape(NC_D, 128).T)
    wq1 = _f32(inputs["L1_attn_Wq"])
    wk1 = _f32(inputs["L1_attn_Wk"])
    wv1 = _f32(inputs["L1_attn_Wv"])
    wo1 = _f32(inputs["L1_attn_Wo"])
    bwT = _f32(np.asarray(inputs["L1_ffn_base_w"]).T)
    bb_r = _f32(np.asarray(inputs["L1_ffn_base_b"]).reshape(NC_D, 128).T)
    aT = _f32(np.asarray(inputs["L1_ffn_A"]).T)
    bstk = _f32(np.asarray(inputs["L1_ffn_B"]).transpose(0, 2, 1).reshape(1024, D))
    rwT = _f32(np.asarray(inputs["L1_ffn_router_w"]).T)
    inv = 1.0 / (10000.0 ** (np.arange(0, 128, 2)[:64] / 128.0))

    in_maps = []
    for c in range(NCORE):
        b, j = c // 4, c % 4
        lst = w13_chunks(c)
        pk1 = _ci32(np.concatenate([w1[hc * 4096:(hc + 1) * 4096] for hc in lst]))
        pk3 = _ci32(np.concatenate([w3[hc * 4096:(hc + 1) * 4096] for hc in lst]))
        s1v = s1.reshape(HID, 64)
        s3v = s3.reshape(HID, 64)
        s1_r = _f32(np.concatenate([s1v[hc * 128:(hc + 1) * 128] for hc in lst],
                                   axis=1))
        s3_r = _f32(np.concatenate([s3v[hc * 128:(hc + 1) * 128] for hc in lst],
                                   axis=1))
        o0 = c * DQ_OC * 128
        pk2 = _ci32(w2[o0 * 86:(o0 + DQ_OC * 128) * 86].reshape(DQ_OC * 128,
                                                                86 * 64))
        s2v = s2.reshape(D, 172)
        s2_r = _f32(np.concatenate(
            [s2v[(c * DQ_OC + q) * 128:(c * DQ_OC + q + 1) * 128]
             for q in range(DQ_OC)], axis=1))
        pos = np.arange(j * TOK, (j + 1) * TOK)
        fr = pos[None, :] * inv[:, None]
        lm = np.zeros((2, NC_D, 128, 512), np.float32)
        for qg in range(2):
            kpos = (128 * np.arange(NC_D)[:, None, None]
                    + np.arange(128)[None, :, None])
            qpos = TOK * j + 512 * qg + np.arange(512)[None, None, :]
            lm[qg] = np.where(kpos <= qpos, 0.0, -30000.0)
        in_maps.append({
            "xT": _f32(x[b, j * TOK:(j + 1) * TOK, :].T),
            "wqT": wqT, "wkT": wkT, "wvT": wvT,
            "bq_r": bq_r, "bk_r": bk_r, "bv_row": bv_row,
            "pk1": pk1, "pk3": pk3, "s1_r": s1_r, "s3_r": s3_r,
            "pk2": pk2, "s2_r": s2_r,
            "woT": woT, "bo_r": bo_r,
            "wq1": wq1, "wk1": wk1, "wv1": wv1,
            "cosb": np.cos(fr).astype(BF16NP),
            "sinb": np.sin(fr).astype(BF16NP),
            "wo1": wo1, "bwT": bwT, "bb_r": bb_r, "aT": aT,
            "bstk": bstk, "rwT": rwT,
            "lmask": lm.astype(BF16NP),
        })

    res = run_bass_kernel_spmd(_get_nc(), in_maps,
                               core_ids=list(range(NCORE))).results

    out = np.empty((2, SEQ, D), np.float32)
    for c in range(NCORE):
        b, j = c // 4, c % 4
        out[b, j * TOK:(j + 1) * TOK, :] = res[c]["outT"].T
    return out


# version 13
# speedup vs baseline: 1.1227x; 1.0067x over previous
import numpy as np
import ml_dtypes
import concourse.bass as bass
import concourse.mybir as mybir
from concourse.bass_utils import run_bass_kernel_spmd

F32 = mybir.dt.float32
BF16 = mybir.dt.bfloat16
I32 = mybir.dt.int32
AF = mybir.ActivationFunctionType
ALU = mybir.AluOpType
BF16NP = ml_dtypes.bfloat16

D = 4096
SEQ = 4096
NCORE = 8
TOK = 1024
NC_D = D // 128
NH0 = 32
TOPK = 819
HID = 11008
NHC = HID // 128
INV_SQRT_DH = float(1.0 / np.sqrt(128.0))
Z_Q = 0.8416212335729143
NEWTON_G = float(1.0 / (SEQ * 0.2799619))
LN_EPS = 1e-5

DQ_HC = 11
DQ_OC = 4

GROUPS4 = [[0, 1, 2, 3], [4, 5, 6, 7]]
GROUPS8 = [list(range(8))]

ENGS = ("sync", "vector", "scalar", "tensor", "gpsimd")


class Sem:
    def __init__(self, nc, name):
        self.h = nc.alloc_semaphore(name)
        self.n = 0


class Prog:

    SELF_CHAIN = ("vector", "scalar", "gpsimd")

    def __init__(self, nc):
        self.nc = nc
        self.q = {e: [] for e in ENGS}
        self.selfsem = {e: Sem(nc, f"self_{e}") for e in self.SELF_CHAIN}
        self.chain = {}

    def emit(self, eng, fn):
        self.q[eng].append(fn)

    def op(self, eng, fn, inc=None, k=1):
        if eng in self.SELF_CHAIN:
            prev = self.chain.get(eng)
            if inc is not None:
                inc.n += k
                upd = (inc.h, k)
                self.chain[eng] = (inc.h, inc.n)
            else:
                ss = self.selfsem[eng]
                ss.n += 1
                upd = (ss.h, 1)
                self.chain[eng] = (ss.h, ss.n)

            def wrapped(e, f=fn, prev=prev, upd=upd):
                if prev is not None:
                    e.wait_ge(prev[0], prev[1])
                f(e).then_inc(upd[0], upd[1])
            self.emit(eng, wrapped)
        elif inc is None:
            self.emit(eng, fn)
        else:
            inc.n += k
            self.emit(eng, lambda e, f=fn, s=inc.h, kk=k: f(e).then_inc(s, kk))

    def raw(self, eng, fn, inc, k=1):
        inc.n += k
        self.emit(eng, lambda e, f=fn, s=inc.h, kk=k: f(e).then_inc(s, kk))

    def dma(self, eng, out, in_, sem):
        prev = self.chain.get(eng) if eng in self.SELF_CHAIN else None
        sem.n += 16
        def fn(e, o=out, i=in_, s=sem.h, prev=prev):
            if prev is not None:
                e.wait_ge(prev[0], prev[1])
            e.dma_start(out=o, in_=i).then_inc(s, 16)
        self.emit(eng, fn)

    def wait(self, eng, sem, val=None):
        v = sem.n if val is None else val
        if v > 0:
            self.emit(eng, lambda e, s=sem.h, vv=v: e.wait_ge(s, vv))

    def run(self):
        with self.nc.Block() as block:
            @block.sync
            def _(sync):
                for fn in self.q["sync"]:
                    fn(sync)

            @block.vector
            def _(vector):
                for fn in self.q["vector"]:
                    fn(vector)

            @block.scalar
            def _(scalar):
                for fn in self.q["scalar"]:
                    fn(scalar)

            @block.tensor
            def _(tensor):
                for fn in self.q["tensor"]:
                    fn(tensor)

            @block.gpsimd
            def _(gpsimd):
                for fn in self.q["gpsimd"]:
                    fn(gpsimd)


class Common:
    def __init__(self, nc, p):
        self.nc = nc
        self.p = p
        al = nc.alloc_sbuf_tensor
        self.ident = al("cm_ident", [128, 128], BF16)
        self.ones_c16 = al("cm_ones_c16", [128, 1], BF16)
        self.ones_c32 = al("cm_ones_c32", [128, 1], F32)
        self.ones_r16 = al("cm_ones_r16", [1, 128], BF16)
        i_t = nc.inline_tensor(np.eye(128, dtype=BF16NP), name="c_ident")
        oc16 = nc.inline_tensor(np.ones((128, 1), dtype=BF16NP), name="c_oc16")
        oc32 = nc.inline_tensor(np.ones((128, 1), dtype=np.float32), name="c_oc32")
        or16 = nc.inline_tensor(np.ones((1, 128), dtype=BF16NP), name="c_or16")
        self.ld = Sem(nc, "cm_ld")
        self.ld2 = Sem(nc, "cm_ld2")
        self.ld3 = Sem(nc, "cm_ld3")
        self.sv = Sem(nc, "cm_v")
        self.ss = Sem(nc, "cm_s")
        self.st = Sem(nc, "cm_t")
        self.sg = Sem(nc, "cm_g")
        self.cc = Sem(nc, "cm_cc")
        self.slot = [Sem(nc, f"cm_sl{i}") for i in range(8)]
        self.all_sems = [self.ld, self.ld2, self.ld3, self.sv, self.ss,
                         self.st, self.sg] + self.slot
        p.dma("sync", self.ident[:], i_t.ap(), self.ld)
        p.dma("sync", self.ones_c16[:], oc16.ap(), self.ld)
        p.dma("sync", self.ones_c32[:], oc32.ap(), self.ld)
        p.dma("sync", self.ones_r16[:], or16.ap(), self.ld)

    def barrier(self):
        for e in ENGS:
            for s in self.all_sems:
                self.p.wait(e, s)



def emit_ln_stats_rows(p, cm, rows, ps_s1, ps_s2, ntok):
    rstd = rows[0:1, 0:ntok]
    murs = rows[0:1, ntok:2 * ntok]
    mu = ps_s1
    p.wait("vector", cm.st)
    p.op("vector", lambda e: e.tensor_scalar_mul(mu, ps_s1, 1.0 / D))
    p.op("vector", lambda e: e.tensor_scalar_mul(murs, ps_s2, 1.0 / D),
         inc=cm.sv)
    p.wait("scalar", cm.sv)
    p.op("scalar", lambda e: e.activation(out=rstd, in_=mu, func=AF.Square,
                                          bias=0.0, scale=1.0), inc=cm.ss)
    p.wait("vector", cm.ss)
    p.op("vector", lambda e: e.tensor_sub(murs, murs, rstd))
    p.op("vector", lambda e: e.tensor_scalar_add(murs, murs, LN_EPS), inc=cm.sv)
    p.wait("scalar", cm.sv)
    p.op("scalar", lambda e: e.activation(out=murs, in_=murs, func=AF.Ln,
                                          bias=0.0, scale=1.0))
    p.op("scalar", lambda e: e.activation(out=rstd, in_=murs, func=AF.Exp,
                                          bias=0.0, scale=-0.5), inc=cm.ss)
    p.wait("vector", cm.ss)
    p.op("vector", lambda e: e.tensor_mul(murs, mu, rstd), inc=cm.sv)


def emit_ln_bcast(p, cm, rows, ntok, ps_bc, bcast_r, bcast_m, rows16):
    nh = ntok // 512
    p.wait("vector", cm.sv)
    p.op("vector", lambda e: e.tensor_copy(out=rows16[0:1, 0:2 * ntok],
                                           in_=rows[0:1, 0:2 * ntok]), inc=cm.sv)
    rstd = rows16[0:1, 0:ntok]
    murs = rows16[0:1, ntok:2 * ntok]
    p.wait("tensor", cm.sv)
    for h in range(nh):
        p.op("tensor", lambda e, h=h: e.matmul(
            ps_bc[:, h * 512:(h + 1) * 512], cm.ones_r16[:],
            rstd[0:1, h * 512:(h + 1) * 512], start=True, stop=True))
    for h in range(nh):
        p.op("tensor", lambda e, h=h: e.matmul(
            ps_bc[:, ntok + h * 512:ntok + (h + 1) * 512], cm.ones_r16[:],
            murs[0:1, h * 512:(h + 1) * 512], start=True, stop=True),
            inc=cm.st if h == nh - 1 else None)
    p.wait("vector", cm.st)
    p.op("vector", lambda e: e.tensor_copy(out=bcast_r[:], in_=ps_bc[:, 0:ntok]))
    p.op("vector", lambda e: e.tensor_copy(out=bcast_m[:], in_=ps_bc[:, ntok:2 * ntok]),
         inc=cm.sv)


def emit_ln_resident(p, cm, X, XN, ntok, sq, rows, ps_s1, ps_s2, ps_bc,
                     bcast_r, bcast_m, rows16, tmp32):
    nh = ntok // 512
    p.wait("tensor", cm.sv)
    for c in range(NC_D):
        for h in range(nh):
            p.op("tensor", lambda e, c=c, h=h: e.matmul(
                ps_s1[0:1, h * 512:(h + 1) * 512], cm.ones_c16[:],
                X[:, c, h * 512:(h + 1) * 512],
                start=(c == 0), stop=(c == NC_D - 1)))
    st_marks = {}
    for c in range(NC_D):
        if c >= 2:
            p.wait("scalar", cm.st, st_marks[c - 2])
        p.op("scalar", lambda e, c=c: e.activation(
            out=sq[:, (c % 2) * ntok:(c % 2) * ntok + ntok], in_=X[:, c, :],
            func=AF.Square, bias=0.0, scale=1.0), inc=cm.ss)
        p.wait("tensor", cm.ss)
        for h in range(nh):
            p.op("tensor", lambda e, c=c, h=h: e.matmul(
                ps_s2[0:1, h * 512:(h + 1) * 512], cm.ones_c16[:],
                sq[:, (c % 2) * ntok + h * 512:(c % 2) * ntok + (h + 1) * 512],
                start=(c == 0), stop=(c == NC_D - 1)),
                inc=cm.st if h == nh - 1 else None)
        st_marks[c] = cm.st.n
    emit_ln_stats_rows(p, cm, rows, ps_s1[0:1, 0:ntok], ps_s2[0:1, 0:ntok], ntok)
    emit_ln_bcast(p, cm, rows, ntok, ps_bc, bcast_r, bcast_m, rows16)
    p.wait("vector", cm.sv)
    for c in range(NC_D):
        p.op("vector", lambda e, c=c: e.tensor_mul(tmp32, X[:, c, :],
                                                   bcast_r[:]))
        p.op("vector", lambda e, c=c: e.tensor_sub(XN[:, c, :], tmp32,
                                                   bcast_m[:]),
             inc=cm.sv if c == NC_D - 1 else None)


def emit_ln_stream(p, cm, xT_hbm, XN, ntok, xbuf, sq, rows, ps_s1, ps_s2,
                   ps_bc, bcast_r32, bcast_m, rows16, tmp32):
    nh = ntok // 512
    xr = xT_hbm.rearrange("(c p) t -> p c t", p=128)
    st_marks = {}
    for c in range(NC_D):
        b = c % 2
        if c >= 2:
            p.wait("sync", cm.st, st_marks[c - 2])
        p.dma("sync", xbuf[b][:], xr[:, c, :], cm.slot[b])
        p.wait("tensor", cm.slot[b])
        for h in range(nh):
            p.op("tensor", lambda e, c=c, h=h: e.matmul(
                ps_s1[0:1, h * 512:(h + 1) * 512], cm.ones_c32[:],
                xbuf[c % 2][:, h * 512:(h + 1) * 512],
                start=(c == 0), stop=(c == NC_D - 1)))
        p.wait("scalar", cm.slot[b])
        if c >= 2:
            p.wait("scalar", cm.st, st_marks[c - 2])
        p.op("scalar", lambda e, c=c: e.activation(
            out=sq[:, (c % 2) * ntok:(c % 2) * ntok + ntok], in_=xbuf[c % 2][:],
            func=AF.Square, bias=0.0, scale=1.0), inc=cm.ss)
        p.wait("tensor", cm.ss)
        for h in range(nh):
            p.op("tensor", lambda e, c=c, h=h: e.matmul(
                ps_s2[0:1, h * 512:(h + 1) * 512], cm.ones_c16[:],
                sq[:, (c % 2) * ntok + h * 512:(c % 2) * ntok + (h + 1) * 512],
                start=(c == 0), stop=(c == NC_D - 1)),
                inc=cm.st)
        st_marks[c] = cm.st.n
    emit_ln_stats_rows(p, cm, rows, ps_s1[0:1, 0:ntok], ps_s2[0:1, 0:ntok], ntok)
    emit_ln_bcast(p, cm, rows, ntok, ps_bc, bcast_r32, bcast_m, rows16)
    v_marks = {}
    p.wait("sync", cm.ss)
    p.wait("sync", cm.st)
    for c in range(NC_D):
        b = c % 2
        if c >= 2:
            p.wait("sync", cm.sv, v_marks[c - 2])
        p.dma("sync", xbuf[b][:], xr[:, c, :], cm.slot[b])
        p.wait("vector", cm.slot[b])
        p.op("vector", lambda e, c=c: e.tensor_mul(tmp32, xbuf[c % 2][:],
                                                   bcast_r32[:]))
        p.op("vector", lambda e, c=c: e.tensor_sub(XN[:, c, :], tmp32,
                                                   bcast_m[:]), inc=cm.sv)
        v_marks[c] = cm.sv.n



def emit_proj_fm_half(p, cm, W_hbm, XN, nmc, ntok, wpan32, wpan16, ps_pair,
                      evac, nic=NC_D):
    nh = ntok // 512
    cast_marks = {}
    group_marks = {}
    evac_marks = {}
    ss_base = cm.ss.n
    nq = max(1, nic // 8)
    nparts = (nic + nq - 1) // nq
    pctr = 0
    pmark = {0: 0, 1: 0}
    for mc in range(nmc):
        b = mc % 2
        for part in range(nparts):
            pb = pctr % 2
            lo = part * nq
            hi = min(nic, lo + nq)
            src = W_hbm[lo * 128:hi * 128,
                        mc * 128:(mc + 1) * 128].rearrange(
                "(ic p) m -> p ic m", p=128)
            p.wait("sync", cm.sv, pmark[pb])
            p.dma("sync",
                  wpan32[pb][:, 0:(hi - lo) * 128].rearrange(
                      "p (ic m) -> p ic m", m=128), src, cm.slot[2 + pb])
            p.wait("vector", cm.slot[2 + pb])
            if mc >= 2 and part == 0:
                p.wait("vector", cm.st, group_marks[mc - 2])
            p.op("vector", lambda e, b=b, pb=pb, lo=lo, n=(hi - lo) * 128:
                 e.tensor_copy(out=wpan16[b][:, lo * 128:lo * 128 + n],
                               in_=wpan32[pb][:, 0:n]), inc=cm.sv)
            pmark[pb] = cm.sv.n
            pctr += 1
        cast_marks[mc] = cm.sv.n
        ps = ps_pair[mc % 2]
        p.wait("tensor", cm.sv)
        if mc >= 2:
            p.wait("tensor", cm.ss, evac_marks[mc - 2])
        else:
            p.wait("tensor", cm.ss, ss_base)
        for ic in range(nic):
            for h in range(nh):
                p.op("tensor", lambda e, b=b, ic=ic, h=h, ps=ps: e.matmul(
                    ps[:, h * 512:(h + 1) * 512],
                    wpan16[b][:, ic * 128:(ic + 1) * 128],
                    XN[:, ic, h * 512:(h + 1) * 512],
                    start=(ic == 0), stop=(ic == nic - 1)),
                    inc=cm.st if (ic == nic - 1 and h == nh - 1) else None)
        group_marks[mc] = cm.st.n
        evac(mc, ps)
        evac_marks[mc] = cm.ss.n



def build_full():
    nc = bass.Bass("TRN2", target_bir_lowering=False)

    xT = nc.dram_tensor("xT", [D, TOK], F32, kind="ExternalInput").ap()
    wqT = nc.dram_tensor("wqT", [D, D], F32, kind="ExternalInput").ap()
    wkT = nc.dram_tensor("wkT", [D, D], F32, kind="ExternalInput").ap()
    wvT = nc.dram_tensor("wvT", [D, D], F32, kind="ExternalInput").ap()
    bq_r = nc.dram_tensor("bq_r", [128, NC_D], F32, kind="ExternalInput").ap()
    bk_r = nc.dram_tensor("bk_r", [128, NC_D], F32, kind="ExternalInput").ap()
    bv_row = nc.dram_tensor("bv_row", [1, D], F32, kind="ExternalInput").ap()
    pk1 = nc.dram_tensor("pk1", [DQ_HC * 4096, 64], I32, kind="ExternalInput").ap()
    pk3 = nc.dram_tensor("pk3", [DQ_HC * 4096, 64], I32, kind="ExternalInput").ap()
    s1_r = nc.dram_tensor("s1_r", [128, DQ_HC * 64], F32, kind="ExternalInput").ap()
    s3_r = nc.dram_tensor("s3_r", [128, DQ_HC * 64], F32, kind="ExternalInput").ap()
    pk2 = nc.dram_tensor("pk2", [DQ_OC * 128, NHC * 64], I32,
                         kind="ExternalInput").ap()
    s2_r = nc.dram_tensor("s2_r", [128, DQ_OC * 172], F32, kind="ExternalInput").ap()
    woT = nc.dram_tensor("woT", [D, D], F32, kind="ExternalInput").ap()
    bo_r = nc.dram_tensor("bo_r", [128, NC_D], F32, kind="ExternalInput").ap()
    wq1 = nc.dram_tensor("wq1", [D, D], F32, kind="ExternalInput").ap()
    wk1 = nc.dram_tensor("wk1", [D, 1024], F32, kind="ExternalInput").ap()
    wv1 = nc.dram_tensor("wv1", [D, 1024], F32, kind="ExternalInput").ap()
    cosb = nc.dram_tensor("cosb", [64, TOK], BF16, kind="ExternalInput").ap()
    sinb = nc.dram_tensor("sinb", [64, TOK], BF16, kind="ExternalInput").ap()
    wo1 = nc.dram_tensor("wo1", [D, D], F32, kind="ExternalInput").ap()
    bwT = nc.dram_tensor("bwT", [D, D], F32, kind="ExternalInput").ap()
    bb_r = nc.dram_tensor("bb_r", [128, NC_D], F32, kind="ExternalInput").ap()
    aT = nc.dram_tensor("aT", [D, 64], F32, kind="ExternalInput").ap()
    bstk = nc.dram_tensor("bstk", [1024, D], F32, kind="ExternalInput").ap()
    rwT = nc.dram_tensor("rwT", [D, 16], F32, kind="ExternalInput").ap()
    lmask = nc.dram_tensor("lmask", [2, NC_D, 128, 512], BF16,
                           kind="ExternalInput").ap()

    outT = nc.dram_tensor("outT", [D, TOK], F32, kind="ExternalOutput").ap()

    q0T = nc.dram_tensor("q0T", [D, TOK], BF16, kind="Internal").ap()
    k0_loc = nc.dram_tensor("k0_loc", [D, TOK], BF16, kind="Internal").ap()
    v0_loc = nc.dram_tensor("v0_loc", [TOK, D], BF16, kind="Internal").ap()
    k0g = nc.dram_tensor("k0g", [4, D, TOK], BF16, kind="Internal").ap()
    v0g = nc.dram_tensor("v0g", [4, TOK, D], BF16, kind="Internal").ap()
    w1t_loc = nc.dram_tensor("w1t_loc", [128, DQ_HC * NC_D, 128], BF16,
                             kind="Internal").ap()
    w3t_loc = nc.dram_tensor("w3t_loc", [128, DQ_HC * NC_D, 128], BF16,
                             kind="Internal").ap()
    w2t_loc = nc.dram_tensor("w2t_loc", [128, DQ_OC * NHC, 128], BF16,
                             kind="Internal").ap()
    w1g = nc.dram_tensor("w1g", [8, 128, DQ_HC * NC_D, 128], BF16,
                         kind="Internal").ap()
    w3g = nc.dram_tensor("w3g", [8, 128, DQ_HC * NC_D, 128], BF16,
                         kind="Internal").ap()
    w2g = nc.dram_tensor("w2g", [8, 128, DQ_OC * NHC, 128], BF16,
                         kind="Internal").ap()
    ctxT = nc.dram_tensor("ctxT", [D, TOK], BF16, kind="Internal").ap()
    x2T = nc.dram_tensor("x2T", [D, TOK], BF16, kind="Internal").ap()
    hsp = nc.dram_tensor("hsp", [NHC, 128, TOK], BF16, kind="Internal").ap()
    q1T = nc.dram_tensor("q1T", [D, TOK], BF16, kind="Internal").ap()
    k1_loc = nc.dram_tensor("k1_loc", [1024, TOK], BF16, kind="Internal").ap()
    v1_loc = nc.dram_tensor("v1_loc", [TOK, 1024], BF16, kind="Internal").ap()
    k1g = nc.dram_tensor("k1g", [4, 1024, TOK], BF16, kind="Internal").ap()
    v1g = nc.dram_tensor("v1g", [4, TOK, 1024], BF16, kind="Internal").ap()
    ctxs = nc.dram_tensor("ctxs", [D, TOK], BF16, kind="Internal").ap()

    p = Prog(nc)
    cm = Common(nc, p)
    al = nc.alloc_sbuf_tensor

    AR_A = al("arA", [128, 32768], BF16).ap()
    AR_B = al("arB", [128, 32768], BF16).ap()
    AR_C = al("arC", [128, 24576], BF16).ap()
    sq = al("p_sq", [128, 2048], BF16).ap()
    evacb = [al(f"p_evac{i}", [128, 1024], BF16).ap() for i in range(2)]
    rows = al("p_rows", [1, 2048], F32).ap()
    ropeb = al("p_ropeb", [128, 2048], BF16).ap()
    bcast_r16 = al("p_bcr16", [128, 1024], BF16).ap()
    bcast_m16 = al("p_bcm16", [128, 1024], F32).ap()
    bias_q = al("p_bq", [128, NC_D], F32).ap()
    bias_k = al("p_bk", [128, NC_D], F32).ap()
    bias_o = al("p_bo", [128, NC_D], F32).ap()
    bias_b = al("p_bb", [128, NC_D], F32).ap()

    pool = nc.alloc_psum_tensor("pool", [128, 4096], F32)
    ps_s1 = pool[0:1, 0:1024]
    ps_s2 = pool[0:1, 1024:2048]
    ps_bc = pool[:, 2048:4096]
    ps_pair = [pool[:, 0:1024], pool[:, 1024:2048]]

    Af = AR_A
    Bf = AR_B
    Cf = AR_C

    XN = Af.rearrange("p (c t) -> p c t", t=TOK)
    vcache = Bf[:, 0:16384].rearrange("p (c v) -> p c v", v=512)
    xbuf = [Bf[:, 16384:18432].bitcast(F32), Bf[:, 18432:20480].bitcast(F32)]
    bcast_r32 = Bf[:, 20480:22528].bitcast(F32)
    bcast_m = Bf[:, 22528:24576].bitcast(F32)
    bv_r16 = Bf[0:1, 24576:28672]
    wpan32 = [Cf[:, 0:2048].bitcast(F32), Cf[:, 2048:4096].bitcast(F32)]
    wpan16 = [Cf[:, 4096:8192], Cf[:, 8192:12288]]
    dq_sc = Cf[:, 12288:15104].bitcast(F32)
    dq_sc2 = Cf[:, 15104:16480].bitcast(F32)
    dq_tr = [Cf[:, 16480:16992], Cf[:, 16992:17504]]
    rows16 = sq[0:1, 0:2048]

    xnflat = Af
    pkbuf = [xnflat[:, 0:5504].bitcast(I32),
             xnflat[:, 5504:11008].bitcast(I32)]
    dq_i = xnflat[:, 11008:16512].bitcast(I32)
    dq_f = xnflat[:, 16512:22016].bitcast(F32)
    dq_w = xnflat[:, 22016:27520].rearrange("p (g f) -> p g f", f=128)

    ps_v8 = [pool[:, i * 512:(i + 1) * 512] for i in range(8)]
    ps_tr = pool[:, 0:1024].bitcast(BF16)

    p.dma("sync", bias_q[:], bq_r, cm.ld)
    p.dma("sync", bias_k[:], bk_r, cm.ld)
    p.dma("sync", bias_o[:], bo_r, cm.ld)
    p.dma("sync", bias_b[:], bb_r, cm.ld)
    p.dma("sync", ropeb[0:64, 0:TOK], cosb, cm.ld)
    p.dma("sync", ropeb[0:64, TOK:2 * TOK], sinb, cm.ld)
    p.dma("sync", ropeb[64:128, 0:TOK], cosb, cm.ld)
    p.dma("sync", ropeb[64:128, TOK:2 * TOK], sinb, cm.ld)
    p.dma("sync", dq_sc[:, 0:DQ_HC * 64], s1_r, cm.ld)
    p.dma("sync", dq_sc[:, DQ_HC * 64:2 * DQ_HC * 64], s3_r, cm.ld)
    p.dma("sync", dq_sc2[:], s2_r, cm.ld)
    for i in range(4):
        p.dma("sync", xbuf[i % 2][0:1, 0:1024],
              bv_row[0:1, i * 1024:(i + 1) * 1024], cm.ld)
        cm.barrier()
        p.op("vector", lambda e, i=i: e.tensor_copy(
            out=bv_r16[0:1, i * 1024:(i + 1) * 1024],
            in_=xbuf[i % 2][0:1, 0:1024]), inc=cm.sv)
    cm.barrier()
    p.op("vector", lambda e: e.tensor_scalar_mul(bias_q[:], bias_q[:],
                                                 INV_SQRT_DH), inc=cm.sv)
    cm.barrier()

    emit_ln_stream(p, cm, xT, XN, TOK, xbuf, sq, rows, ps_s1, ps_s2, ps_bc,
                   bcast_r32, bcast_m, rows16, sq.bitcast(F32)[:, 0:TOK])
    cm.barrier()

    def mk_evac(out_hbm, bias_cols, scale):
        out_r = out_hbm.rearrange("(c p) t -> p c t", p=128)

        def evac(mc, ps):
            b = mc % 2
            p.wait("scalar", cm.st)
            p.wait("scalar", cm.slot[4 + b])
            p.op("scalar", lambda e, mc=mc, ps=ps, b=b: e.activation(
                out=evacb[b][:], in_=ps[:], func=AF.Identity,
                bias=bias_cols[:, mc:mc + 1], scale=scale), inc=cm.ss)
            p.dma("scalar", out_r[:, mc, :], evacb[b][:], cm.slot[4 + b])
        return evac

    emit_proj_fm_half(p, cm, wkT, XN, NC_D, TOK, wpan32, wpan16, ps_pair,
                      mk_evac(k0_loc, bias_k, 1.0))
    cm.barrier()

    for v8 in range(8):
        cast_marks = {}
        for ic in range(NC_D):
            b = ic % 2
            if ic >= 2:
                p.wait("sync", cm.sv, cast_marks[ic - 2])
            src = wvT[ic * 128:(ic + 1) * 128, v8 * 512:(v8 + 1) * 512]
            p.dma("sync", wpan32[b][:, 0:512], src, cm.slot[2 + b])
            p.wait("vector", cm.slot[2 + b])
            p.op("vector", lambda e, b=b, ic=ic: e.tensor_copy(
                out=vcache[:, ic, :], in_=wpan32[b][:, 0:512]), inc=cm.sv)
            cast_marks[ic] = cm.sv.n
        p.wait("tensor", cm.sv)
        for tc in range(8):
            ps = ps_v8[tc]
            for ic in range(NC_D):
                p.op("tensor", lambda e, ic=ic, tc=tc, ps=ps, v8=v8: e.matmul(
                    ps[:], XN[:, ic, tc * 128:(tc + 1) * 128],
                    vcache[:, ic, :], start=(ic == 0), stop=False))
            p.op("tensor", lambda e, ps=ps, v8=v8: e.matmul(
                ps[:], cm.ones_r16[:],
                bv_r16[0:1, v8 * 512:(v8 + 1) * 512],
                start=False, stop=True), inc=cm.st)
        p.wait("vector", cm.st)
        for tc in range(8):
            b = tc % 2
            p.wait("vector", cm.slot[4 + b])
            p.op("vector", lambda e, tc=tc, b=b: e.tensor_copy(
                out=evacb[b][:, 0:512], in_=ps_v8[tc][:]), inc=cm.sv)
            p.wait("scalar", cm.sv)
            p.dma("scalar", v0_loc[tc * 128:(tc + 1) * 128,
                                   v8 * 512:(v8 + 1) * 512],
                  evacb[b][:, 0:512], cm.slot[4 + b])
        cm.barrier()

    p.raw("gpsimd", lambda e: e.collective_compute(
        "AllGather", ALU.bypass, replica_groups=GROUPS4,
        ins=[k0_loc.opt()], outs=[k0g.opt()]), cm.cc)
    p.raw("gpsimd", lambda e: e.collective_compute(
        "AllGather", ALU.bypass, replica_groups=GROUPS4,
        ins=[v0_loc.opt()], outs=[v0g.opt()]), cm.cc)
    CC_KV0 = cm.cc.n

    emit_proj_fm_half(p, cm, wqT, XN, NC_D, TOK, wpan32, wpan16, ps_pair,
                      mk_evac(q0T, bias_q, INV_SQRT_DH))
    cm.barrier()

    def emit_dequant(pk_src_fn, sc_of, out_hbm, n_sub, ntiles, tilebase_of):
        shift_marks = {}
        p.wait("sync", cm.slot[0])
        p.wait("sync", cm.slot[1])
        p.wait("sync", cm.sv)
        for s in range(n_sub):
            b = s % 2
            n = ntiles * 64
            if s >= 2:
                p.wait("sync", cm.sv, shift_marks[s - 2])
            srcap = pk_src_fn(s)
            dstap = pkbuf[b][:, 0:n]
            if len(srcap.shape) == 3:
                dstap = dstap.rearrange("p (r c) -> p r c", c=srcap.shape[2])
            p.dma("sync", dstap, srcap, cm.slot[b])
            p.wait("vector", cm.slot[b])
            sc = sc_of(s)
            for half in range(2):
                if half == 0:
                    p.op("vector", lambda e, b=b, n=n: e.tensor_single_scalar(
                        out=dq_i[:, 0:n], in_=pkbuf[b][:, 0:n], scalar=4,
                        op=ALU.arith_shift_right), inc=cm.sv)
                else:
                    p.op("vector", lambda e, b=b, n=n: e.tensor_single_scalar(
                        out=dq_i[:, 0:n], in_=pkbuf[b][:, 0:n], scalar=28,
                        op=ALU.logical_shift_left))
                    p.op("vector", lambda e, n=n: e.tensor_single_scalar(
                        out=dq_i[:, 0:n], in_=dq_i[:, 0:n], scalar=28,
                        op=ALU.arith_shift_right), inc=cm.sv)
                if half == 1:
                    shift_marks[s] = cm.sv.n
                p.wait("scalar", cm.sv)
                p.op("scalar", lambda e, n=n: e.activation(
                    out=dq_f[:, 0:n], in_=dq_i[:, 0:n], func=AF.Copy,
                    bias=0.0, scale=1.0), inc=cm.ss)
                p.wait("vector", cm.ss)
                p.op("vector", lambda e, n=n, nt=ntiles, sc=sc, half=half:
                     e.tensor_mul(
                         dq_w[:, 0:nt, half * 64:half * 64 + 64],
                         dq_f[:, 0:n].rearrange("p (g j) -> p g j", j=64),
                         sc[:, half::2].to_broadcast([128, nt, 64])), inc=cm.sv)
            p.wait("tensor", cm.sv)
            for g0 in range(0, ntiles, 4):
                gb = (g0 // 4) % 2
                ng = min(4, ntiles - g0)
                p.wait("tensor", cm.slot[6 + gb])
                for t in range(ng):
                    p.op("tensor", lambda e, g0=g0, t=t, gb=gb: e.transpose(
                        ps_tr.rearrange("p (u f) -> p u f", f=128)[:, gb * 8 + t, :],
                        dq_w[:, g0 + t, :], cm.ident[:]),
                        inc=cm.st if t == ng - 1 else None)
                p.wait("vector", cm.st)
                p.op("vector", lambda e, gb=gb, n=ng * 128: e.tensor_copy(
                    out=dq_tr[gb][:, 0:n],
                    in_=ps_tr[:, gb * 1024:gb * 1024 + n]), inc=cm.sv)
                p.wait("scalar", cm.sv)
                base = tilebase_of(s) + g0
                dst = out_hbm[:, base:base + ng, :]
                p.dma("scalar", dst,
                      dq_tr[gb][:, 0:ng * 128].rearrange("i (t h) -> i t h", h=128),
                      cm.slot[6 + gb])

    emit_dequant(
        lambda s: pk1[s * 4096:(s + 1) * 4096, :].rearrange(
            "(p r) c -> p r c", p=128),
        lambda s: dq_sc[:, s * 64:(s + 1) * 64],
        w1t_loc, DQ_HC, NC_D, lambda s: s * NC_D)
    emit_dequant(
        lambda s: pk3[s * 4096:(s + 1) * 4096, :].rearrange(
            "(p r) c -> p r c", p=128),
        lambda s: dq_sc[:, DQ_HC * 64 + s * 64:DQ_HC * 64 + (s + 1) * 64],
        w3t_loc, DQ_HC, NC_D, lambda s: s * NC_D)
    emit_dequant(
        lambda s: pk2[(s // 2) * 128:(s // 2) * 128 + 128,
                      (s % 2) * 43 * 64:(s % 2) * 43 * 64 + 43 * 64],
        lambda s: dq_sc2[:, (s // 2) * 172 + (s % 2) * 86:
                         (s // 2) * 172 + (s % 2) * 86 + 86],
        w2t_loc, DQ_OC * 2, 43, lambda s: (s // 2) * NHC + (s % 2) * 43)

    cm.barrier()

    p.raw("gpsimd", lambda e: e.collective_compute(
        "AllGather", ALU.bypass, replica_groups=GROUPS8,
        ins=[w1t_loc.opt()], outs=[w1g.opt()]), cm.cc)
    p.raw("gpsimd", lambda e: e.collective_compute(
        "AllGather", ALU.bypass, replica_groups=GROUPS8,
        ins=[w3t_loc.opt()], outs=[w3g.opt()]), cm.cc)
    p.raw("gpsimd", lambda e: e.collective_compute(
        "AllGather", ALU.bypass, replica_groups=GROUPS8,
        ins=[w2t_loc.opt()], outs=[w2g.opt()]), cm.cc)
    CC_W = cm.cc.n

    NKC = SEQ // 128
    E2 = [Af[:, 0:16384].rearrange("p (c q) -> p c q", q=512),
          Af[:, 16384:32768].rearrange("p (c q) -> p c q", q=512)]
    M = Bf[:, 0:16384].rearrange("p (c q) -> p c q", q=512)
    Kh = [Bf[:, 16384:20480], Bf[:, 20480:24576]]
    Qh = [Bf[:, 24576:25600], Bf[:, 25600:26624]]
    tb = Bf[:, 26624:27136]
    rowsA = Bf[0:1, 27136:28160].bitcast(F32)
    rowsB0 = Bf[0:1, 28160:29184].bitcast(F32)
    rowsG = Bf[0:1, 29184:30208].bitcast(F32)
    rowsT = Bf[0:1, 30208:31232].bitcast(F32)
    rowsT16 = Bf[0:1, 31232:31744]
    rowsR16 = Bf[0:1, 31744:32256]
    Vh = [Cf[:, 0:4096].rearrange("p (c d) -> p c d", d=128),
          Cf[:, 4096:8192].rearrange("p (c d) -> p c d", d=128)]
    ctxb = [Cf[:, 8192:8704], Cf[:, 8704:9216]]
    rb = Cf[:, 9216:10240].bitcast(F32)
    S4raw = Cf[:, 10240:12288].rearrange("p (c q) -> p c q", q=512)

    ps_s = [pool[:, i * 512:(i + 1) * 512] for i in range(4)]
    ps_ctx = pool[:, 4 * 512:5 * 512]
    ps_r1 = pool[0:1, 5 * 512:6 * 512]
    ps_r2 = pool[0:1, 6 * 512:7 * 512]
    ps_bc2 = pool[:, 7 * 512:8 * 512]

    S4raws = [S4raw, Cf[:, 12288:14336].rearrange("p (c q) -> p c q", q=512)]

    p.wait("sync", cm.cc, CC_KV0)
    p.wait("gpsimd", cm.cc, CC_KV0)
    e_done = {}
    pbank = {}
    tree_mark = [0]

    def a0_stage1(g):
        h, qg = g // 2, g % 2
        hb = h % 2
        E = E2[g % 2]
        S4r = S4raws[g % 2]
        if qg == 0:
            p.wait("sync", cm.st)
            for gg in range(4):
                p.dma("sync", Kh[hb][:, gg * 1024:(gg + 1) * 1024],
                      k0g[gg, h * 128:(h + 1) * 128, :], cm.slot[hb])
                p.dma("sync", Vh[hb][:, gg * 8:(gg + 1) * 8, :],
                      v0g[gg][:, h * 128:(h + 1) * 128].rearrange(
                          "(c p) d -> p c d", p=128), cm.slot[hb])
            p.dma("sync", Qh[hb][:], q0T[h * 128:(h + 1) * 128, :], cm.slot[hb])
        p.wait("tensor", cm.slot[hb])
        for c in range(NKC):
            bk = c % 4
            if bk in pbank:
                p.wait("tensor", cm.ss, pbank[bk][0])
                if pbank[bk][1]:
                    p.wait("tensor", cm.sv, pbank[bk][1])
            p.op("tensor", lambda e, c=c, bk=bk, hb=hb, qg=qg: e.matmul(
                ps_s[bk][:], Kh[hb][:, c * 128:(c + 1) * 128],
                Qh[hb][:, qg * 512:(qg + 1) * 512],
                start=True, stop=True), inc=cm.st)
            p.wait("scalar", cm.st)
            if g >= 2 and c == 0:
                p.wait("scalar", cm.st, e_done[g - 2][0])
                p.wait("scalar", cm.sv, e_done[g - 2][1])
            p.op("scalar", lambda e, c=c, bk=bk, E=E: e.activation(
                out=E[:, c, :], in_=ps_s[bk][:], func=AF.Exp,
                bias=0.0, scale=1.0), inc=cm.ss)
            svm = None
            if c < 4:
                if c == 0:
                    p.wait("vector", cm.ss)
                    p.wait("vector", cm.st)
                else:
                    p.wait("vector", cm.st)
                p.op("vector", lambda e, c=c, bk=bk, S4r=S4r: e.tensor_copy(
                    out=S4r[:, c, :], in_=ps_s[bk][:]), inc=cm.sv)
                svm = cm.sv.n
            pbank[bk] = (cm.ss.n, svm)

    def a0_stage2(g):
        h, qg = g // 2, g % 2
        hb = h % 2
        E = E2[g % 2]
        S4r = S4raws[g % 2]
        p.wait("tensor", cm.sv)
        for c in range(4):
            p.op("tensor", lambda e, c=c, S4r=S4r: e.matmul(
                ps_r1[:], cm.ones_c16[:], S4r[:, c, :],
                start=(c == 0), stop=(c == 3)),
                inc=cm.st if c == 3 else None)
        p.wait("scalar", cm.sv)
        p.op("scalar", lambda e, S4r=S4r: e.activation(
            out=sq[:], in_=S4r.rearrange("p c q -> p (c q)"),
            func=AF.Square, bias=0.0, scale=1.0), inc=cm.ss)
        p.wait("tensor", cm.ss)
        for c in range(4):
            p.op("tensor", lambda e, c=c: e.matmul(
                ps_r2[:], cm.ones_c16[:], sq[:, c * 512:(c + 1) * 512],
                start=(c == 0), stop=(c == 3)),
                inc=cm.st if c == 3 else None)
        p.wait("vector", cm.st)
        p.op("vector", lambda e: e.tensor_scalar_mul(rowsA, ps_r1, 1.0 / 512))
        p.op("vector", lambda e: e.tensor_scalar_mul(rowsB0, ps_r2, 1.0 / 512))
        p.op("vector", lambda e: e.tensor_mul(rowsT, rowsA, rowsA))
        p.op("vector", lambda e: e.tensor_sub(rowsB0, rowsB0, rowsT))
        p.op("vector", lambda e: e.tensor_scalar_add(rowsB0, rowsB0, 1e-6),
             inc=cm.sv)
        p.wait("scalar", cm.sv)
        p.op("scalar", lambda e: e.activation(out=rowsB0, in_=rowsB0,
                                              func=AF.Ln, bias=0.0, scale=1.0))
        p.op("scalar", lambda e: e.activation(out=rowsB0, in_=rowsB0,
                                              func=AF.Exp, bias=0.0,
                                              scale=0.5), inc=cm.ss)
        p.wait("vector", cm.ss)
        p.op("vector", lambda e: e.tensor_scalar_mul(rowsG, rowsB0,
                                                     float(NEWTON_G)))
        p.op("vector", lambda e: e.tensor_scalar_mul(rowsT, rowsB0, float(Z_Q)))
        p.op("vector", lambda e: e.tensor_add(rowsT, rowsT, rowsA),
             inc=cm.sv)
        for it in range(3):
            p.wait("scalar", cm.sv)
            p.op("scalar", lambda e: e.activation(
                out=rowsT16, in_=rowsT, func=AF.Exp, bias=0.0, scale=1.0),
                inc=cm.ss)
            p.wait("tensor", cm.ss)
            p.wait("tensor", cm.sv)
            p.op("tensor", lambda e: e.matmul(
                ps_bc2[:], cm.ones_r16[:], rowsT16[0:1, :],
                start=True, stop=True), inc=cm.st)
            p.wait("vector", cm.st)
            p.op("vector", lambda e: e.tensor_copy(out=tb[:], in_=ps_bc2[:]),
                 inc=cm.sv)
            if it == 2:
                p.op("vector", lambda e, E=E: e.tensor_tensor(
                    out=M, in0=E,
                    in1=tb.rearrange("p (u q) -> p u q", u=1).to_broadcast(
                        [128, NKC, 512]),
                    op=ALU.is_gt))
                p.op("vector", lambda e, E=E: e.tensor_mul(E, M, E),
                     inc=cm.sv)
                break
            p.op("vector", lambda e, E=E: e.tensor_tensor(
                out=M, in0=E,
                in1=tb.rearrange("p (u q) -> p u q", u=1).to_broadcast(
                    [128, NKC, 512]),
                op=ALU.is_gt))
            p.op("vector", lambda e: e.tensor_add(
                M[:, 0:16, :], M[:, 0:16, :], M[:, 16:32, :]))
            p.op("vector", lambda e: e.tensor_add(
                M[:, 0:8, :], M[:, 0:8, :], M[:, 8:16, :]))
            p.op("vector", lambda e: e.tensor_add(
                M[:, 0:4, :], M[:, 0:4, :], M[:, 4:8, :]), inc=cm.sv)
            p.wait("tensor", cm.sv)
            for c in range(4):
                p.op("tensor", lambda e, c=c: e.matmul(
                    ps_r1[:], cm.ones_c16[:], M[:, c, :],
                    start=(c == 0), stop=(c == 3)),
                    inc=cm.st if c == 3 else None)
            p.wait("vector", cm.st)
            p.op("vector", lambda e: e.tensor_scalar_add(rowsB0, ps_r1,
                                                         -float(TOPK)))
            p.op("vector", lambda e: e.tensor_mul(rowsB0, rowsB0, rowsG))
            p.op("vector", lambda e: e.tensor_add(rowsT, rowsT, rowsB0),
                 inc=cm.sv)
        sm_mark = cm.sv.n
        p.wait("tensor", cm.sv, sm_mark)
        for c in range(NKC):
            p.op("tensor", lambda e, c=c, hb=hb, E=E: e.matmul(
                ps_ctx[:], Vh[hb][:, c, :], E[:, c, :],
                start=(c == 0), stop=(c == NKC - 1)))
        for c in range(NKC):
            p.op("tensor", lambda e, c=c, E=E: e.matmul(
                ps_r2[:], cm.ones_c16[:], E[:, c, :],
                start=(c == 0), stop=(c == NKC - 1)),
                inc=cm.st if c == NKC - 1 else None)
        pv_mark = cm.st.n
        e_done[g] = (pv_mark, sm_mark)
        p.wait("vector", cm.st)
        p.op("vector", lambda e: e.reciprocal(out=rowsB0, in_=ps_r2))
        p.op("vector", lambda e: e.tensor_copy(out=rowsR16, in_=rowsB0),
             inc=cm.sv)
        p.wait("tensor", cm.sv)
        p.op("tensor", lambda e: e.matmul(ps_bc2[:], cm.ones_r16[:],
                                          rowsR16[0:1, :],
                                          start=True, stop=True), inc=cm.st)
        p.wait("vector", cm.st)
        p.op("vector", lambda e: e.tensor_copy(out=rb[:], in_=ps_bc2[:]))
        gb = qg % 2
        p.wait("vector", cm.slot[4 + gb])
        p.op("vector", lambda e, gb=gb: e.tensor_mul(ctxb[gb][:], ps_ctx[:],
                                                     rb[:]), inc=cm.sv)
        p.wait("scalar", cm.sv)
        p.dma("scalar", ctxT[h * 128:(h + 1) * 128, qg * 512:(qg + 1) * 512],
              ctxb[gb][:], cm.slot[4 + gb])

    for g in range(2 * NH0):
        a0_stage1(g)
        if g >= 1:
            a0_stage2(g - 1)
    a0_stage2(2 * NH0 - 1)
    p.wait("sync", cm.slot[4])
    p.wait("sync", cm.slot[5])
    cm.barrier()

    A = Af.rearrange("p (c t) -> p c t", t=TOK)
    B = Bf.rearrange("p (c t) -> p c t", t=TOK)
    scr8 = Cf[:, 0:8192]
    xbuf3 = [scr8[:, 0:2048].bitcast(F32), scr8[:, 2048:4096].bitcast(F32)]
    xb16 = scr8[:, 4096:5120]
    w13 = [scr8[:, 0:4096].rearrange("p (ic m) -> p ic m", m=128),
           scr8[:, 4096:8192].rearrange("p (ic m) -> p ic m", m=128)]
    vcache3 = scr8[:, 0:8192].rearrange("p (ic v) -> p ic v", v=256)
    wpan32m = [Cf[:, 8192:10240].bitcast(F32), Cf[:, 10240:12288].bitcast(F32)]
    wpan16m = [Cf[:, 12288:16384], Cf[:, 16384:20480]]
    htile = [Cf[:, 20480:21504], Cf[:, 21504:22528]]
    w2til = [Cf[:, 22528:23040].rearrange("p (o h) -> p o h", h=128),
             Cf[:, 23040:23552].rearrange("p (o h) -> p o h", h=128)]
    rtmp = sq

    ps_up = [[pool[:, 0:1024], pool[:, 1024:2048]],
             [pool[:, 2048:3072], pool[:, 3072:4096]]]
    ps_dn = [pool[:, i * 512:(i + 1) * 512] for i in range(8)]
    ps_v4 = [pool[:, i * 256:(i + 1) * 256] for i in range(8)]

    ctr = ctxT.rearrange("(c p) t -> p c t", p=128)
    for c in range(NC_D):
        p.dma("sync", A[:, c, :], ctr[:, c, :], cm.ld)
    cm.barrier()

    xr = xT.rearrange("(c p) t -> p c t", p=128)

    def evac_oproj(mc, ps):
        b = mc % 2
        p.wait("sync", cm.sv)
        p.dma("sync", xbuf3[b][:], xr[:, mc, :], cm.slot[6 + b])
        p.wait("scalar", cm.st)
        p.op("scalar", lambda e, mc=mc, ps=ps, b=b: e.activation(
            out=evacb[b][:], in_=ps[:], func=AF.Identity,
            bias=bias_o[:, mc:mc + 1], scale=1.0), inc=cm.ss)
        p.wait("vector", cm.slot[6 + b])
        p.wait("vector", cm.ss)
        p.op("vector", lambda e, b=b: e.tensor_copy(out=xb16[:], in_=xbuf3[b][:]))
        p.op("vector", lambda e, mc=mc, b=b: e.tensor_add(
            B[:, mc, :], evacb[b][:], xb16[:]), inc=cm.sv)

    emit_proj_fm_half(p, cm, woT, A, NC_D, TOK, wpan32m, wpan16m, ps_pair,
                      evac_oproj)
    cm.barrier()

    emit_ln_resident(p, cm, B, A, TOK, sq, rows, ps_s1, ps_s2, ps_bc,
                     bcast_r16, bcast_m16, rows16, xbuf3[0])
    cm.barrier()

    w13p = [w13,
            [Cf[:, 12288:16384].rearrange("p (ic m) -> p ic m", m=128),
             Cf[:, 16384:20480].rearrange("p (ic m) -> p ic m", m=128)]]

    def w13_src(wg, hc):
        core, qq = hc // DQ_HC, hc % DQ_HC
        return wg[core][:, qq * NC_D:(qq + 1) * NC_D, :]

    p.wait("sync", cm.cc, CC_W)
    p.wait("gpsimd", cm.cc, CC_W)
    up_marks = {}
    fm_marks = {}
    st_base_up = cm.st.n
    for hc in range(NHC):
        ub = hc % 2
        w13c = w13p[ub]
        p.wait("sync", cm.st, fm_marks.get(hc - 2, st_base_up))
        p.wait("gpsimd", cm.st, fm_marks.get(hc - 2, st_base_up))
        p.dma("sync", w13c[0], w13_src(w1g, hc), cm.slot[0 + ub])
        p.dma("gpsimd", w13c[1], w13_src(w3g, hc), cm.slot[2 + ub])
        p.wait("tensor", cm.slot[0 + ub])
        p.wait("tensor", cm.slot[2 + ub])
        if hc >= 2:
            p.wait("tensor", cm.ss, up_marks[hc - 2][0])
            p.wait("tensor", cm.sv, up_marks[hc - 2][1])
        psA, psB = ps_up[ub]
        for ic in range(NC_D):
            for hh in range(2):
                p.op("tensor", lambda e, ic=ic, hh=hh, psA=psA, w13c=w13c:
                     e.matmul(
                         psA[:, hh * 512:(hh + 1) * 512], w13c[0][:, ic, :],
                         A[:, ic, hh * 512:(hh + 1) * 512],
                         start=(ic == 0), stop=(ic == NC_D - 1)))
        for ic in range(NC_D):
            for hh in range(2):
                p.op("tensor", lambda e, ic=ic, hh=hh, psB=psB, w13c=w13c:
                     e.matmul(
                         psB[:, hh * 512:(hh + 1) * 512], w13c[1][:, ic, :],
                         A[:, ic, hh * 512:(hh + 1) * 512],
                         start=(ic == 0), stop=(ic == NC_D - 1)),
                     inc=cm.st if (ic == NC_D - 1 and hh == 1) else None)
        fm_marks[hc] = cm.st.n
        p.wait("scalar", cm.st)
        p.wait("scalar", cm.slot[4 + ub])
        p.op("scalar", lambda e, psA=psA, ub=ub: e.activation(
            out=htile[ub][:], in_=psA[:], func=AF.Silu, bias=0.0, scale=1.0),
            inc=cm.ss)
        s_mark = cm.ss.n
        p.wait("vector", cm.ss)
        p.op("vector", lambda e, psB=psB, ub=ub: e.tensor_copy(
            out=evacb[ub][:], in_=psB[:]))
        p.op("vector", lambda e, ub=ub: e.tensor_mul(
            htile[ub][:], htile[ub][:], evacb[ub][:]), inc=cm.sv)
        up_marks[hc] = (s_mark, cm.sv.n)
        p.wait("scalar", cm.sv)
        p.dma("scalar", hsp[hc], htile[ub][:], cm.slot[4 + ub])
    cm.barrier()

    def w2_src(quad, hc):
        return w2g[quad].rearrange("p (q h) m -> p q h m", h=NHC)[:, :, hc, :]

    dn_marks = {}
    st_base_dn = cm.st.n
    for quad in range(8):
        p.wait("tensor", cm.sv)
        for hc in range(NHC):
            db = hc % 2
            it = quad * NHC + hc
            p.wait("sync", cm.st, dn_marks.get(it - 2, st_base_dn))
            p.wait("gpsimd", cm.st, dn_marks.get(it - 2, st_base_dn))
            p.dma("sync", htile[db][:], hsp[hc], cm.slot[db])
            p.dma("gpsimd", w2til[db][:], w2_src(quad, hc), cm.slot[db])
            p.wait("tensor", cm.slot[db])
            for o in range(4):
                for hh in range(2):
                    p.op("tensor",
                         lambda e, o=o, hh=hh, db=db, hc=hc: e.matmul(
                             ps_dn[o * 2 + hh][:], w2til[db][:, o, :],
                             htile[db][:, hh * 512:(hh + 1) * 512],
                             start=(hc == 0), stop=(hc == NHC - 1)),
                         inc=cm.st if (o == 3 and hh == 1) else None)
            dn_marks[it] = cm.st.n
        p.wait("vector", cm.st)
        for o in range(4):
            mc = quad * 4 + o
            p.op("vector", lambda e, o=o: e.tensor_copy(
                out=evacb[o % 2][:, 0:512], in_=ps_dn[o * 2][:]))
            p.op("vector", lambda e, o=o: e.tensor_copy(
                out=evacb[o % 2][:, 512:1024], in_=ps_dn[o * 2 + 1][:]))
            p.op("vector", lambda e, o=o, mc=mc: e.tensor_add(
                A[:, mc, :], evacb[o % 2][:], B[:, mc, :]),
                inc=cm.sv if o == 3 else None)
    cm.barrier()

    for c in range(NC_D):
        p.dma("sync", x2T.rearrange("(c p) t -> p c t", p=128)[:, c, :],
              A[:, c, :], cm.ld)
    emit_ln_resident(p, cm, A, B, TOK, sq, rows, ps_s1, ps_s2, ps_bc,
                     bcast_r16, bcast_m16, rows16, xbuf3[0])
    cm.barrier()

    cos_t = ropeb[0:64, 0:TOK]
    sin_t = ropeb[0:64, TOK:2 * TOK]
    cos_t64 = ropeb[64:128, 0:TOK]
    sin_t64 = ropeb[64:128, TOK:2 * TOK]

    def mk_evac_rope(out_hbm):
        out_r = out_hbm.rearrange("(c p) t -> p c t", p=128)

        def evac(mc, ps):
            b = mc % 2
            p.wait("vector", cm.st)
            p.wait("vector", cm.slot[4 + b])
            p.op("vector", lambda e, ps=ps, b=b: e.tensor_copy(
                out=evacb[b][:], in_=ps[:]), inc=cm.ss)
            x1 = evacb[b][0:64, :]
            x2 = evacb[b][64:128, :]
            t1 = rtmp[0:64, 0:TOK]
            t2 = rtmp[0:64, TOK:2 * TOK]
            t3 = rtmp[64:128, 0:TOK]
            p.op("vector", lambda e, x1=x1: e.tensor_mul(t1, x1, cos_t))
            p.op("vector", lambda e, x2=x2: e.tensor_mul(t2, x2, sin_t64))
            p.op("vector", lambda e, x2=x2: e.tensor_mul(x2, x2, cos_t64))
            p.op("vector", lambda e, x1=x1: e.tensor_mul(t3, x1, sin_t))
            p.op("vector", lambda e, x2=x2: e.tensor_add(x2, x2, t3))
            p.op("vector", lambda e, x1=x1: e.tensor_sub(x1, t1, t2),
                 inc=cm.ss)
            p.wait("scalar", cm.ss)
            p.dma("scalar", out_r[:, mc, :], evacb[b][:], cm.slot[4 + b])
        return evac

    emit_proj_fm_half(p, cm, wk1, B, 8, TOK, wpan32m, wpan16m, ps_pair,
                      mk_evac_rope(k1_loc))
    cm.barrier()

    for v8 in range(4):
        cast_marks = {}
        for ic in range(NC_D):
            b = ic % 2
            if ic >= 2:
                p.wait("sync", cm.sv, cast_marks[ic - 2])
            src = wv1[ic * 128:(ic + 1) * 128, v8 * 256:(v8 + 1) * 256]
            p.dma("sync", wpan32m[b][:, 0:256], src, cm.slot[2 + b])
            p.wait("vector", cm.slot[2 + b])
            p.op("vector", lambda e, b=b, ic=ic: e.tensor_copy(
                out=vcache3[:, ic, :], in_=wpan32m[b][:, 0:256]), inc=cm.sv)
            cast_marks[ic] = cm.sv.n
        p.wait("tensor", cm.sv)
        for tc in range(8):
            ps = ps_v4[tc]
            for ic in range(NC_D):
                p.op("tensor", lambda e, ic=ic, tc=tc, ps=ps: e.matmul(
                    ps[:], B[:, ic, tc * 128:(tc + 1) * 128],
                    vcache3[:, ic, :],
                    start=(ic == 0), stop=(ic == NC_D - 1)),
                    inc=cm.st if ic == NC_D - 1 else None)
        p.wait("vector", cm.st)
        for tc in range(8):
            b = tc % 2
            p.wait("vector", cm.slot[4 + b])
            p.op("vector", lambda e, tc=tc, b=b: e.tensor_copy(
                out=evacb[b][:, 0:256], in_=ps_v4[tc][:]), inc=cm.sv)
            p.wait("scalar", cm.sv)
            p.dma("scalar", v1_loc[tc * 128:(tc + 1) * 128,
                                   v8 * 256:(v8 + 1) * 256],
                  evacb[b][:, 0:256], cm.slot[4 + b])
        cm.barrier()

    p.raw("gpsimd", lambda e: e.collective_compute(
        "AllGather", ALU.bypass, replica_groups=GROUPS4,
        ins=[k1_loc.opt()], outs=[k1g.opt()]), cm.cc)
    p.raw("gpsimd", lambda e: e.collective_compute(
        "AllGather", ALU.bypass, replica_groups=GROUPS4,
        ins=[v1_loc.opt()], outs=[v1g.opt()]), cm.cc)
    CC_KV1 = cm.cc.n

    emit_proj_fm_half(p, cm, wq1, B, NC_D, TOK, wpan32m, wpan16m, ps_pair,
                      mk_evac_rope(q1T))
    cm.barrier()

    S4 = Af[:, 0:16384].rearrange("p (c q) -> p c q", q=512)
    E4 = Af[:, 16384:32768].rearrange("p (c q) -> p c q", q=512)
    MK = Bf[:, 0:16384].rearrange("p (c q) -> p c q", q=512)
    Kh4 = [Bf[:, 16384:20480], Bf[:, 16384:20480]]
    Vh4 = [Bf[:, 20480:24576].rearrange("p (c d) -> p c d", d=128)] * 2
    Qh4 = [Bf[:, 24576 + i * 512:24576 + (i + 1) * 512] for i in range(2)]
    rowsB4 = Bf[0:1, 25600:26624].bitcast(F32)
    rowsR16_4 = Bf[0:1, 26624:27136]
    rb4 = Bf[:, 27136:28160].bitcast(F32)
    ctxb4 = [Bf[:, 28160 + i * 512:28160 + (i + 1) * 512] for i in range(2)]

    p.wait("sync", cm.cc, CC_KV1)
    e_done4 = {}
    pbank4 = {}
    g4 = 0
    for qg in range(2):
        p.wait("sync", cm.sv)
        p.wait("sync", cm.st)
        for c in range(NC_D):
            p.dma("sync", MK[:, c, :], lmask[qg, c], cm.ld)
        p.wait("vector", cm.ld)
        for h in range(NH0):
            kv = h // 4
            eb4 = g4 % 2
            E4c = [S4, E4][eb4]
            if h % 4 == 0:
                p.wait("sync", cm.st)
                for g in range(4):
                    p.dma("sync", Kh4[0][:, g * 1024:(g + 1) * 1024],
                          k1g[g, kv * 128:(kv + 1) * 128, :], cm.slot[0])
                    p.dma("sync", Vh4[0][:, g * 8:(g + 1) * 8, :],
                          v1g[g][:, kv * 128:(kv + 1) * 128].rearrange(
                              "(c p) d -> p c d", p=128), cm.slot[0])
            p.wait("sync", cm.st)
            p.dma("sync", Qh4[h % 2][:],
                  q1T[h * 128:(h + 1) * 128, qg * 512:(qg + 1) * 512],
                  cm.slot[2 + (h % 2)])
            kb = 0
            p.wait("tensor", cm.slot[0])
            p.wait("tensor", cm.slot[2 + (h % 2)])
            for c in range(NC_D):
                bk = c % 4
                if bk in pbank4:
                    p.wait("tensor", cm.sv, pbank4[bk])
                p.op("tensor", lambda e, c=c, bk=bk, kb=kb, h=h: e.matmul(
                    ps_s[bk][:], Kh4[kb][:, c * 128:(c + 1) * 128],
                    Qh4[h % 2][:], start=True, stop=True), inc=cm.st)
                p.wait("vector", cm.st)
                if g4 >= 2 and c == 0:
                    p.wait("vector", cm.st, e_done4[g4 - 2])
                p.op("vector", lambda e, c=c, bk=bk, E4c=E4c: e.tensor_tensor(
                    out=E4c[:, c, :], in0=ps_s[bk][:], in1=MK[:, c, :],
                    op=ALU.add), inc=cm.sv)
                pbank4[bk] = cm.sv.n
                if c % 4 == 3:
                    p.wait("scalar", cm.sv)
                    p.op("scalar", lambda e, c=c, E4c=E4c: e.activation(
                        out=E4c.rearrange("p c q -> p (c q)")[
                            :, (c - 3) * 512:(c + 1) * 512],
                        in_=E4c.rearrange("p c q -> p (c q)")[
                            :, (c - 3) * 512:(c + 1) * 512],
                        func=AF.Exp, bias=0.0, scale=1.0), inc=cm.ss)
            p.wait("tensor", cm.ss)
            p.wait("tensor", cm.sv)
            for c in range(NC_D):
                p.op("tensor", lambda e, c=c, kb=kb, E4c=E4c: e.matmul(
                    ps_ctx[:], Vh4[kb][:, c, :], E4c[:, c, :],
                    start=(c == 0), stop=(c == NC_D - 1)))
            for c in range(NC_D):
                p.op("tensor", lambda e, c=c, E4c=E4c: e.matmul(
                    ps_r2[:], cm.ones_c16[:], E4c[:, c, :],
                    start=(c == 0), stop=(c == NC_D - 1)),
                    inc=cm.st if c == NC_D - 1 else None)
            e_done4[g4] = cm.st.n
            p.wait("vector", cm.st)
            p.op("vector", lambda e: e.reciprocal(out=rowsB4, in_=ps_r2))
            p.op("vector", lambda e: e.tensor_copy(out=rowsR16_4, in_=rowsB4),
                 inc=cm.sv)
            p.wait("tensor", cm.sv)
            p.op("tensor", lambda e: e.matmul(ps_bc2[:], cm.ones_r16[:],
                                              rowsR16_4[0:1, :],
                                              start=True, stop=True), inc=cm.st)
            p.wait("vector", cm.st)
            p.op("vector", lambda e: e.tensor_copy(out=rb4[:], in_=ps_bc2[:]))
            gb = h % 2
            p.wait("vector", cm.slot[4 + gb])
            p.op("vector", lambda e, gb=gb: e.tensor_mul(ctxb4[gb][:], ps_ctx[:],
                                                         rb4[:]), inc=cm.sv)
            p.wait("scalar", cm.sv)
            p.dma("scalar", ctxs[h * 128:(h + 1) * 128,
                                 qg * 512:(qg + 1) * 512],
                  ctxb4[gb][:], cm.slot[4 + gb])
            g4 += 1
    p.wait("sync", cm.slot[4])
    p.wait("sync", cm.slot[5])
    cm.barrier()

    wpan32f = [Cf[:, 10240:12288].bitcast(F32), Cf[:, 12288:14336].bitcast(F32)]
    wpan16f = [Cf[:, 14336:18432], Cf[:, 18432:22528]]
    x2b = [Cf[:, 0:1024], Cf[:, 1024:2048]]
    _f = Cf[:, 22528:24576].bitcast(F32)
    fin32 = [_f, _f]

    ctr2 = ctxs.rearrange("(c p) t -> p c t", p=128)
    for c in range(NC_D):
        p.dma("sync", A[:, c, :], ctr2[:, c, :], cm.ld)
    cm.barrier()

    x2r = x2T.rearrange("(c p) t -> p c t", p=128)

    def evac_oproj1(mc, ps):
        b = mc % 2
        p.wait("sync", cm.sv)
        p.dma("sync", x2b[b][:], x2r[:, mc, :], cm.slot[6 + b])
        p.wait("vector", cm.st)
        p.op("vector", lambda e, ps=ps, b=b: e.tensor_copy(
            out=evacb[b][:], in_=ps[:]), inc=cm.ss)
        p.wait("vector", cm.slot[6 + b])
        p.op("vector", lambda e, mc=mc, b=b: e.tensor_add(
            B[:, mc, :], evacb[b][:], x2b[b][:]), inc=cm.sv)

    emit_proj_fm_half(p, cm, wo1, A, NC_D, TOK, wpan32f, wpan16f, ps_pair,
                      evac_oproj1)
    cm.barrier()

    emit_ln_resident(p, cm, B, A, TOK, sq, rows, ps_s1, ps_s2, ps_bc,
                     bcast_r16, bcast_m16, rows16, Cf[:, 0:2048].bitcast(F32))
    cm.barrier()

    rh16 = Cf[0:64, 14336:15360]
    lgr = Cf[0:16, 15360:17408].bitcast(F32)
    lgn = Cf[:, 17408:17664].bitcast(F32).rearrange("p (c e) -> p c e", e=16)
    cmbn = Cf[:, 17664:17920].bitcast(F32).rearrange("p (c e) -> p c e", e=16)
    cmbT = Cf[0:16, 17920:18944]
    mx8 = Cf[:, 18944:18960].bitcast(F32)
    nm = Cf[:, 18960:18968].bitcast(F32)
    id16 = Cf[0:16, 18968:19000].bitcast(F32)
    rhs_stk = Cf[:, 2048:10240].rearrange("p (c t) -> p c t", t=TOK)
    span = x2b[0]

    i16 = nc.inline_tensor(np.eye(16, dtype=np.float32), name="c_i16")
    selnp = np.zeros((16, 16, 64), dtype=BF16NP)
    for e_ in range(16):
        selnp[e_, e_, :] = 1
    sel_t = nc.inline_tensor(selnp, name="c_sel")
    sel = lgr.bitcast(BF16)[:, 0:1024].rearrange("p (a m) -> p a m", m=64)
    p.dma("sync", id16[:], i16.ap(), cm.ld)
    cm.barrier()

    for (W, nmo, dst_ps) in ((aT, 64, pool[0:64, 2048:3072]),
                             (rwT, 16, pool[0:16, 3072:4096])):
        t_marks = {}
        w_base = cm.st.n
        for ic in range(NC_D):
            b = ic % 2
            p.wait("sync", cm.sv)
            p.dma("sync", wpan32f[b][:, 0:nmo], W[ic * 128:(ic + 1) * 128, :],
                  cm.slot[2 + b])
            p.wait("vector", cm.slot[2 + b])
            p.wait("vector", cm.st, t_marks[ic - 2] if ic >= 2 else w_base)
            p.op("vector", lambda e, b=b, nmo=nmo: e.tensor_copy(
                out=span[:, b * 512:b * 512 + nmo],
                in_=wpan32f[b][:, 0:nmo]), inc=cm.sv)
            p.wait("tensor", cm.sv)
            for hh in range(2):
                p.op("tensor", lambda e, ic=ic, hh=hh, b=b, nmo=nmo,
                     dst_ps=dst_ps: e.matmul(
                         dst_ps[:, hh * 512:(hh + 1) * 512],
                         span[:, b * 512:b * 512 + nmo],
                         A[:, ic, hh * 512:(hh + 1) * 512],
                         start=(ic == 0), stop=(ic == NC_D - 1)),
                     inc=cm.st if hh == 1 else None)
            t_marks[ic] = cm.st.n
    p.wait("vector", cm.st)
    p.op("vector", lambda e: e.tensor_copy(out=rh16[:], in_=pool[0:64, 2048:3072]))
    p.op("vector", lambda e: e.tensor_copy(out=lgr[:], in_=pool[0:16, 3072:4096]),
         inc=cm.sv)
    cm.barrier()

    for tc in range(8):
        p.wait("tensor", cm.sv)
        p.op("tensor", lambda e, tc=tc: e.transpose(
            pool[:, 2048 + (tc % 2) * 512:2048 + (tc % 2) * 512 + 16],
            lgr[:, tc * 128:(tc + 1) * 128], id16[:]), inc=cm.st)
        p.wait("vector", cm.st)
        p.op("vector", lambda e, tc=tc: e.tensor_copy(
            out=lgn[:, tc, :], in_=pool[:, 2048 + (tc % 2) * 512:
                                        2048 + (tc % 2) * 512 + 16]),
            inc=cm.sv)
    p.wait("vector", cm.sv)
    for tc in range(8):
        p.op("vector", lambda e, tc=tc: e.reduce_max(
            nm[:, 0:1], lgn[:, tc, :], axis=mybir.AxisListType.X))
        p.op("vector", lambda e: e.tensor_scalar_mul(nm[:, 0:1], nm[:, 0:1],
                                                     -1.0), inc=cm.sv)
        p.wait("scalar", cm.sv)
        p.op("scalar", lambda e, tc=tc: e.activation(
            out=cmbn[:, tc, :], in_=lgn[:, tc, :], func=AF.Exp,
            bias=nm[:, 0:1], scale=1.0, accum_out=nm[:, 1:2]), inc=cm.ss)
        p.wait("vector", cm.ss)
        p.op("vector", lambda e: e.reciprocal(out=nm[:, 1:2], in_=nm[:, 1:2]))
        p.op("vector", lambda e, tc=tc: e.tensor_scalar(
            out=cmbn[:, tc, :], in0=cmbn[:, tc, :], scalar1=nm[:, 1:2],
            scalar2=None, op0=ALU.mult))
        p.op("vector", lambda e, tc=tc: e.max(out=mx8[:], in_=cmbn[:, tc, :]))
        p.op("vector", lambda e: e.tensor_add(nm[:, 2:3], mx8[:, 0:1],
                                              mx8[:, 1:2]))
        p.op("vector", lambda e: e.tensor_scalar_add(nm[:, 2:3], nm[:, 2:3],
                                                     1e-6))
        p.op("vector", lambda e: e.reciprocal(out=nm[:, 2:3], in_=nm[:, 2:3]))
        p.op("vector", lambda e: e.tensor_scalar_mul(nm[:, 2:3], nm[:, 2:3],
                                                     0.5))
        p.op("vector", lambda e, tc=tc: e.tensor_scalar(
            out=lgn[:, tc, :], in0=cmbn[:, tc, :], scalar1=mx8[:, 0:1],
            scalar2=None, op0=ALU.is_equal))
        p.op("vector", lambda e, tc=tc: e.tensor_scalar(
            out=lgn[:, tc, :], in0=lgn[:, tc, :], scalar1=mx8[:, 0:1],
            scalar2=None, op0=ALU.mult))
        p.op("vector", lambda e, tc=tc: e.tensor_scalar(
            out=cmbn[:, tc, :], in0=cmbn[:, tc, :],
            scalar1=mx8[:, 1:2], scalar2=None, op0=ALU.is_equal))
        p.op("vector", lambda e, tc=tc: e.tensor_scalar(
            out=cmbn[:, tc, :], in0=cmbn[:, tc, :], scalar1=mx8[:, 1:2],
            scalar2=None, op0=ALU.mult))
        p.op("vector", lambda e, tc=tc: e.tensor_add(cmbn[:, tc, :],
                                                     cmbn[:, tc, :],
                                                     lgn[:, tc, :]))
        p.op("vector", lambda e, tc=tc: e.tensor_scalar(
            out=cmbn[:, tc, :], in0=cmbn[:, tc, :], scalar1=nm[:, 2:3],
            scalar2=None, op0=ALU.mult), inc=cm.sv)
    cm.barrier()
    for tc in range(8):
        p.op("vector", lambda e, tc=tc: e.tensor_copy(
            out=span[:, (tc % 2) * 512:(tc % 2) * 512 + 16],
            in_=cmbn[:, tc, :]), inc=cm.sv)
        p.wait("tensor", cm.sv)
        psb = pool[:, 2048 + (tc % 2) * 512:2048 + (tc % 2) * 512 + 256].bitcast(BF16)
        p.op("tensor", lambda e, tc=tc, psb=psb: e.transpose(
            psb[0:16, 0:128], span[:, (tc % 2) * 512:(tc % 2) * 512 + 16],
            cm.ident[:]), inc=cm.st)
        p.wait("vector", cm.st)
        p.op("vector", lambda e, tc=tc, psb=psb: e.tensor_copy(
            out=cmbT[:, tc * 128:(tc + 1) * 128],
            in_=psb[0:16, 0:128]), inc=cm.sv)
    cm.barrier()

    p.dma("sync", sel, sel_t.ap(), cm.ld)
    cm.barrier()
    for ee in range(16):
        rc, half = ee // 2, ee % 2
        p.wait("tensor", cm.sv)
        p.op("tensor", lambda e, ee=ee: e.matmul(
            pool[0:64, 2048 + (ee % 2) * 512:2048 + (ee % 2) * 512 + 512],
            sel[:, ee, :], cmbT[:, 0:512],
            start=True, stop=True))
        p.op("tensor", lambda e, ee=ee: e.matmul(
            pool[0:64, 3072 + (ee % 2) * 512:3072 + (ee % 2) * 512 + 512],
            sel[:, ee, :], cmbT[:, 512:1024],
            start=True, stop=True), inc=cm.st)
        p.wait("vector", cm.st)
        p.op("vector", lambda e, ee=ee: e.tensor_copy(
            out=span[0:64, 0:512],
            in_=pool[0:64, 2048 + (ee % 2) * 512:2048 + (ee % 2) * 512 + 512]))
        p.op("vector", lambda e, ee=ee: e.tensor_copy(
            out=span[0:64, 512:1024],
            in_=pool[0:64, 3072 + (ee % 2) * 512:3072 + (ee % 2) * 512 + 512]),
            inc=cm.sv)
        p.op("vector", lambda e, ee=ee, rc=rc, half=half: e.tensor_mul(
            rhs_stk[half * 64:half * 64 + 64, rc, :], rh16[:],
            span[0:64, 0:1024]), inc=cm.sv)
    cm.barrier()

    bpan = [sq[:, i * 1024:(i + 1) * 1024].rearrange("p (rc m) -> p rc m", m=128)
            for i in range(2)]

    def evac_fin(mc, ps):
        b = mc % 2
        p.wait("scalar", cm.st)
        p.op("scalar", lambda e, mc=mc, ps=ps, b=b: e.activation(
            out=evacb[b][:], in_=ps[:], func=AF.Identity,
            bias=bias_b[:, mc:mc + 1], scale=1.0), inc=cm.ss)
        p.wait("vector", cm.ss)
        p.wait("vector", cm.slot[6])
        p.op("vector", lambda e, mc=mc, b=b: e.tensor_add(
            fin32[b][:], evacb[b][:], B[:, mc, :]), inc=cm.sv)
        p.wait("scalar", cm.sv)
        p.dma("scalar", outT.rearrange("(c p) t -> p c t", p=128)[:, mc, :],
              fin32[b][:], cm.slot[6])

    cast_marks = {}
    group_marks = {}
    evac_marks = {}
    pctr = 0
    pmark = {0: 0, 1: 0}
    for mc in range(NC_D):
        b = mc % 2
        for part in range(5):
            pb = pctr % 2
            if part < 4:
                lo = part * 8
                srcw = bwT[lo * 128:(lo + 8) * 128,
                           mc * 128:(mc + 1) * 128].rearrange(
                    "(ic p) m -> p ic m", p=128)
            else:
                srcw = bstk[:, mc * 128:(mc + 1) * 128].rearrange(
                    "(ic p) m -> p ic m", p=128)
            p.wait("sync", cm.sv, pmark[pb])
            p.dma("sync", wpan32f[pb][:, 0:1024].rearrange(
                "p (ic m) -> p ic m", m=128), srcw, cm.slot[2 + pb])
            p.wait("vector", cm.slot[2 + pb])
            if mc >= 2 and part == 0:
                p.wait("vector", cm.st, group_marks[mc - 2])
            if part < 4:
                p.op("vector", lambda e, b=b, pb=pb, lo=lo: e.tensor_copy(
                    out=wpan16f[b][:, lo * 128:lo * 128 + 1024],
                    in_=wpan32f[pb][:, 0:1024]), inc=cm.sv)
            else:
                p.op("vector", lambda e, b=b, pb=pb: e.tensor_copy(
                    out=bpan[b].rearrange("p ic m -> p (ic m)"),
                    in_=wpan32f[pb][:, 0:1024]), inc=cm.sv)
            pmark[pb] = cm.sv.n
            pctr += 1
        cast_marks[mc] = cm.sv.n
        ps = ps_pair[mc % 2]
        p.wait("tensor", cm.sv)
        if mc >= 2:
            p.wait("tensor", cm.ss, evac_marks[mc - 2])
        for ic in range(NC_D):
            for hh in range(2):
                p.op("tensor", lambda e, b=b, ic=ic, hh=hh, ps=ps: e.matmul(
                    ps[:, hh * 512:(hh + 1) * 512],
                    wpan16f[b][:, ic * 128:(ic + 1) * 128],
                    A[:, ic, hh * 512:(hh + 1) * 512],
                    start=(ic == 0), stop=False))
        for rc in range(8):
            for hh in range(2):
                p.op("tensor", lambda e, b=b, rc=rc, hh=hh, ps=ps: e.matmul(
                    ps[:, hh * 512:(hh + 1) * 512], bpan[b][:, rc, :],
                    rhs_stk[:, rc, hh * 512:(hh + 1) * 512],
                    start=False, stop=(rc == 7)),
                    inc=cm.st if (rc == 7 and hh == 1) else None)
        group_marks[mc] = cm.st.n
        evac_fin(mc, ps)
        evac_marks[mc] = cm.ss.n

    p.wait("sync", cm.slot[6])
    p.wait("sync", cm.slot[7])
    cm.barrier()
    p.run()
    return nc



def _f32(a):
    return np.ascontiguousarray(a, dtype=np.float32)


def _ci32(a):
    return np.ascontiguousarray(a, dtype=np.int32)


def w13_chunks(core):
    return [(core * DQ_HC + i) if (core * DQ_HC + i) < NHC else 0
            for i in range(DQ_HC)]


_NC_CACHE = {}


def _get_nc():
    if "full" not in _NC_CACHE:
        _NC_CACHE["full"] = build_full()
    return _NC_CACHE["full"]


def kernel(**inputs):
    x = np.asarray(inputs["L0_attn_hidden_states"])
    wqT = _f32(np.asarray(inputs["L0_attn_wq"]).T)
    wkT = _f32(np.asarray(inputs["L0_attn_wk"]).T)
    wvT = _f32(np.asarray(inputs["L0_attn_wv"]).T)
    bq_r = _f32(np.asarray(inputs["L0_attn_bq"]).reshape(NC_D, 128).T)
    bk_r = _f32(np.asarray(inputs["L0_attn_bk"]).reshape(NC_D, 128).T)
    bv_row = _f32(np.asarray(inputs["L0_attn_bv"]).reshape(1, D))
    w1 = np.asarray(inputs["L0_ffn_w1"])
    s1 = np.asarray(inputs["L0_ffn_s1"])
    w2 = np.asarray(inputs["L0_ffn_w2"])
    s2 = np.asarray(inputs["L0_ffn_s2"])
    w3 = np.asarray(inputs["L0_ffn_w3"])
    s3 = np.asarray(inputs["L0_ffn_s3"])
    woT = _f32(np.asarray(inputs["L0_attn_wo"]).T)
    bo_r = _f32(np.asarray(inputs["L0_attn_bo"]).reshape(NC_D, 128).T)
    wq1 = _f32(inputs["L1_attn_Wq"])
    wk1 = _f32(inputs["L1_attn_Wk"])
    wv1 = _f32(inputs["L1_attn_Wv"])
    wo1 = _f32(inputs["L1_attn_Wo"])
    bwT = _f32(np.asarray(inputs["L1_ffn_base_w"]).T)
    bb_r = _f32(np.asarray(inputs["L1_ffn_base_b"]).reshape(NC_D, 128).T)
    aT = _f32(np.asarray(inputs["L1_ffn_A"]).T)
    bstk = _f32(np.asarray(inputs["L1_ffn_B"]).transpose(0, 2, 1).reshape(1024, D))
    rwT = _f32(np.asarray(inputs["L1_ffn_router_w"]).T)
    inv = 1.0 / (10000.0 ** (np.arange(0, 128, 2)[:64] / 128.0))

    in_maps = []
    for c in range(NCORE):
        b, j = c // 4, c % 4
        lst = w13_chunks(c)
        pk1 = _ci32(np.concatenate([w1[hc * 4096:(hc + 1) * 4096] for hc in lst]))
        pk3 = _ci32(np.concatenate([w3[hc * 4096:(hc + 1) * 4096] for hc in lst]))
        s1v = s1.reshape(HID, 64)
        s3v = s3.reshape(HID, 64)
        s1_r = _f32(np.concatenate([s1v[hc * 128:(hc + 1) * 128] for hc in lst],
                                   axis=1))
        s3_r = _f32(np.concatenate([s3v[hc * 128:(hc + 1) * 128] for hc in lst],
                                   axis=1))
        o0 = c * DQ_OC * 128
        pk2 = _ci32(w2[o0 * 86:(o0 + DQ_OC * 128) * 86].reshape(DQ_OC * 128,
                                                                86 * 64))
        s2v = s2.reshape(D, 172)
        s2_r = _f32(np.concatenate(
            [s2v[(c * DQ_OC + q) * 128:(c * DQ_OC + q + 1) * 128]
             for q in range(DQ_OC)], axis=1))
        pos = np.arange(j * TOK, (j + 1) * TOK)
        fr = pos[None, :] * inv[:, None]
        lm = np.zeros((2, NC_D, 128, 512), np.float32)
        for qg in range(2):
            kpos = (128 * np.arange(NC_D)[:, None, None]
                    + np.arange(128)[None, :, None])
            qpos = TOK * j + 512 * qg + np.arange(512)[None, None, :]
            lm[qg] = np.where(kpos <= qpos, 0.0, -30000.0)
        in_maps.append({
            "xT": _f32(x[b, j * TOK:(j + 1) * TOK, :].T),
            "wqT": wqT, "wkT": wkT, "wvT": wvT,
            "bq_r": bq_r, "bk_r": bk_r, "bv_row": bv_row,
            "pk1": pk1, "pk3": pk3, "s1_r": s1_r, "s3_r": s3_r,
            "pk2": pk2, "s2_r": s2_r,
            "woT": woT, "bo_r": bo_r,
            "wq1": wq1, "wk1": wk1, "wv1": wv1,
            "cosb": np.cos(fr).astype(BF16NP),
            "sinb": np.sin(fr).astype(BF16NP),
            "wo1": wo1, "bwT": bwT, "bb_r": bb_r, "aT": aT,
            "bstk": bstk, "rwT": rwT,
            "lmask": lm.astype(BF16NP),
        })

    res = run_bass_kernel_spmd(_get_nc(), in_maps,
                               core_ids=list(range(NCORE))).results

    out = np.empty((2, SEQ, D), np.float32)
    for c in range(NCORE):
        b, j = c // 4, c % 4
        out[b, j * TOK:(j + 1) * TOK, :] = res[c]["outT"].T
    return out


# version 16 (final)
# speedup vs baseline: 1.1366x; 1.0124x over previous
import numpy as np
import ml_dtypes
import concourse.bass as bass
import concourse.mybir as mybir
from concourse.bass_utils import run_bass_kernel_spmd

F32 = mybir.dt.float32
BF16 = mybir.dt.bfloat16
I32 = mybir.dt.int32
AF = mybir.ActivationFunctionType
ALU = mybir.AluOpType
BF16NP = ml_dtypes.bfloat16

D = 4096
SEQ = 4096
NCORE = 8
TOK = 1024
NC_D = D // 128
NH0 = 32
TOPK = 819
HID = 11008
NHC = HID // 128
INV_SQRT_DH = float(1.0 / np.sqrt(128.0))
Z_Q = 0.8416212335729143
NEWTON_G = float(1.0 / (SEQ * 0.2799619))
LN_EPS = 1e-5

DQ_HC = 11
DQ_OC = 4

GROUPS4 = [[0, 1, 2, 3], [4, 5, 6, 7]]
GROUPS8 = [list(range(8))]

ENGS = ("sync", "vector", "scalar", "tensor", "gpsimd")


class Sem:
    def __init__(self, nc, name):
        self.h = nc.alloc_semaphore(name)
        self.n = 0


class Prog:

    SELF_CHAIN = ("vector", "scalar", "gpsimd")

    def __init__(self, nc):
        self.nc = nc
        self.q = {e: [] for e in ENGS}
        self.selfsem = {e: Sem(nc, f"self_{e}") for e in self.SELF_CHAIN}
        self.chain = {}

    def emit(self, eng, fn):
        self.q[eng].append(fn)

    def op(self, eng, fn, inc=None, k=1):
        if eng in self.SELF_CHAIN:
            prev = self.chain.get(eng)
            if inc is not None:
                inc.n += k
                upd = (inc.h, k)
                self.chain[eng] = (inc.h, inc.n)
            else:
                ss = self.selfsem[eng]
                ss.n += 1
                upd = (ss.h, 1)
                self.chain[eng] = (ss.h, ss.n)

            def wrapped(e, f=fn, prev=prev, upd=upd):
                if prev is not None:
                    e.wait_ge(prev[0], prev[1])
                f(e).then_inc(upd[0], upd[1])
            self.emit(eng, wrapped)
        elif inc is None:
            self.emit(eng, fn)
        else:
            inc.n += k
            self.emit(eng, lambda e, f=fn, s=inc.h, kk=k: f(e).then_inc(s, kk))

    def raw(self, eng, fn, inc, k=1):
        inc.n += k
        self.emit(eng, lambda e, f=fn, s=inc.h, kk=k: f(e).then_inc(s, kk))

    def dma(self, eng, out, in_, sem):
        prev = self.chain.get(eng) if eng in self.SELF_CHAIN else None
        sem.n += 16
        def fn(e, o=out, i=in_, s=sem.h, prev=prev):
            if prev is not None:
                e.wait_ge(prev[0], prev[1])
            e.dma_start(out=o, in_=i).then_inc(s, 16)
        self.emit(eng, fn)

    def wait(self, eng, sem, val=None):
        v = sem.n if val is None else val
        if v > 0:
            self.emit(eng, lambda e, s=sem.h, vv=v: e.wait_ge(s, vv))

    def run(self):
        with self.nc.Block() as block:
            @block.sync
            def _(sync):
                for fn in self.q["sync"]:
                    fn(sync)

            @block.vector
            def _(vector):
                for fn in self.q["vector"]:
                    fn(vector)

            @block.scalar
            def _(scalar):
                for fn in self.q["scalar"]:
                    fn(scalar)

            @block.tensor
            def _(tensor):
                for fn in self.q["tensor"]:
                    fn(tensor)

            @block.gpsimd
            def _(gpsimd):
                for fn in self.q["gpsimd"]:
                    fn(gpsimd)


class Common:
    def __init__(self, nc, p):
        self.nc = nc
        self.p = p
        al = nc.alloc_sbuf_tensor
        self.ident = al("cm_ident", [128, 128], BF16)
        self.ones_c16 = al("cm_ones_c16", [128, 1], BF16)
        self.ones_c32 = al("cm_ones_c32", [128, 1], F32)
        self.ones_r16 = al("cm_ones_r16", [1, 128], BF16)
        i_t = nc.inline_tensor(np.eye(128, dtype=BF16NP), name="c_ident")
        oc16 = nc.inline_tensor(np.ones((128, 1), dtype=BF16NP), name="c_oc16")
        oc32 = nc.inline_tensor(np.ones((128, 1), dtype=np.float32), name="c_oc32")
        or16 = nc.inline_tensor(np.ones((1, 128), dtype=BF16NP), name="c_or16")
        self.ld = Sem(nc, "cm_ld")
        self.ld2 = Sem(nc, "cm_ld2")
        self.ld3 = Sem(nc, "cm_ld3")
        self.sv = Sem(nc, "cm_v")
        self.ss = Sem(nc, "cm_s")
        self.st = Sem(nc, "cm_t")
        self.sg = Sem(nc, "cm_g")
        self.cc = Sem(nc, "cm_cc")
        self.slot = [Sem(nc, f"cm_sl{i}") for i in range(8)]
        self.all_sems = [self.ld, self.ld2, self.ld3, self.sv, self.ss,
                         self.st, self.sg] + self.slot
        p.dma("sync", self.ident[:], i_t.ap(), self.ld)
        p.dma("sync", self.ones_c16[:], oc16.ap(), self.ld)
        p.dma("sync", self.ones_c32[:], oc32.ap(), self.ld)
        p.dma("sync", self.ones_r16[:], or16.ap(), self.ld)

    def barrier(self):
        for e in ENGS:
            for s in self.all_sems:
                self.p.wait(e, s)



def emit_ln_stats_rows(p, cm, rows, ps_s1, ps_s2, ntok):
    rstd = rows[0:1, 0:ntok]
    murs = rows[0:1, ntok:2 * ntok]
    mu = ps_s1
    p.wait("vector", cm.st)
    p.op("vector", lambda e: e.tensor_scalar_mul(mu, ps_s1, 1.0 / D))
    p.op("vector", lambda e: e.tensor_scalar_mul(murs, ps_s2, 1.0 / D),
         inc=cm.sv)
    p.wait("scalar", cm.sv)
    p.op("scalar", lambda e: e.activation(out=rstd, in_=mu, func=AF.Square,
                                          bias=0.0, scale=1.0), inc=cm.ss)
    p.wait("vector", cm.ss)
    p.op("vector", lambda e: e.tensor_sub(murs, murs, rstd))
    p.op("vector", lambda e: e.tensor_scalar_add(murs, murs, LN_EPS), inc=cm.sv)
    p.wait("scalar", cm.sv)
    p.op("scalar", lambda e: e.activation(out=murs, in_=murs, func=AF.Ln,
                                          bias=0.0, scale=1.0))
    p.op("scalar", lambda e: e.activation(out=rstd, in_=murs, func=AF.Exp,
                                          bias=0.0, scale=-0.5), inc=cm.ss)
    p.wait("vector", cm.ss)
    p.op("vector", lambda e: e.tensor_mul(murs, mu, rstd), inc=cm.sv)


def emit_ln_bcast(p, cm, rows, ntok, ps_bc, bcast_r, bcast_m, rows16):
    nh = ntok // 512
    p.wait("vector", cm.sv)
    p.op("vector", lambda e: e.tensor_copy(out=rows16[0:1, 0:2 * ntok],
                                           in_=rows[0:1, 0:2 * ntok]), inc=cm.sv)
    rstd = rows16[0:1, 0:ntok]
    murs = rows16[0:1, ntok:2 * ntok]
    p.wait("tensor", cm.sv)
    for h in range(nh):
        p.op("tensor", lambda e, h=h: e.matmul(
            ps_bc[:, h * 512:(h + 1) * 512], cm.ones_r16[:],
            rstd[0:1, h * 512:(h + 1) * 512], start=True, stop=True))
    for h in range(nh):
        p.op("tensor", lambda e, h=h: e.matmul(
            ps_bc[:, ntok + h * 512:ntok + (h + 1) * 512], cm.ones_r16[:],
            murs[0:1, h * 512:(h + 1) * 512], start=True, stop=True),
            inc=cm.st if h == nh - 1 else None)
    p.wait("vector", cm.st)
    p.op("vector", lambda e: e.tensor_copy(out=bcast_r[:], in_=ps_bc[:, 0:ntok]))
    p.op("vector", lambda e: e.tensor_copy(out=bcast_m[:], in_=ps_bc[:, ntok:2 * ntok]),
         inc=cm.sv)


def emit_ln_resident(p, cm, X, XN, ntok, sq, rows, ps_s1, ps_s2, ps_bc,
                     bcast_r, bcast_m, rows16, tmp32):
    nh = ntok // 512
    p.wait("tensor", cm.sv)
    for c in range(NC_D):
        for h in range(nh):
            p.op("tensor", lambda e, c=c, h=h: e.matmul(
                ps_s1[0:1, h * 512:(h + 1) * 512], cm.ones_c16[:],
                X[:, c, h * 512:(h + 1) * 512],
                start=(c == 0), stop=(c == NC_D - 1)))
    st_marks = {}
    for c in range(NC_D):
        if c >= 2:
            p.wait("scalar", cm.st, st_marks[c - 2])
        p.op("scalar", lambda e, c=c: e.activation(
            out=sq[:, (c % 2) * ntok:(c % 2) * ntok + ntok], in_=X[:, c, :],
            func=AF.Square, bias=0.0, scale=1.0), inc=cm.ss)
        p.wait("tensor", cm.ss)
        for h in range(nh):
            p.op("tensor", lambda e, c=c, h=h: e.matmul(
                ps_s2[0:1, h * 512:(h + 1) * 512], cm.ones_c16[:],
                sq[:, (c % 2) * ntok + h * 512:(c % 2) * ntok + (h + 1) * 512],
                start=(c == 0), stop=(c == NC_D - 1)),
                inc=cm.st if h == nh - 1 else None)
        st_marks[c] = cm.st.n
    emit_ln_stats_rows(p, cm, rows, ps_s1[0:1, 0:ntok], ps_s2[0:1, 0:ntok], ntok)
    emit_ln_bcast(p, cm, rows, ntok, ps_bc, bcast_r, bcast_m, rows16)
    p.wait("vector", cm.sv)
    for c in range(NC_D):
        p.op("vector", lambda e, c=c: e.tensor_mul(tmp32, X[:, c, :],
                                                   bcast_r[:]))
        p.op("vector", lambda e, c=c: e.tensor_sub(XN[:, c, :], tmp32,
                                                   bcast_m[:]),
             inc=cm.sv if c == NC_D - 1 else None)


def emit_ln_stream(p, cm, xT_hbm, XN, ntok, xbuf, sq, rows, ps_s1, ps_s2,
                   ps_bc, bcast_r32, bcast_m, rows16, tmp32):
    nh = ntok // 512
    xr = xT_hbm.rearrange("(c p) t -> p c t", p=128)
    st_marks = {}
    for c in range(NC_D):
        b = c % 2
        if c >= 2:
            p.wait("sync", cm.st, st_marks[c - 2])
        p.dma("sync", xbuf[b][:], xr[:, c, :], cm.slot[b])
        p.wait("tensor", cm.slot[b])
        for h in range(nh):
            p.op("tensor", lambda e, c=c, h=h: e.matmul(
                ps_s1[0:1, h * 512:(h + 1) * 512], cm.ones_c32[:],
                xbuf[c % 2][:, h * 512:(h + 1) * 512],
                start=(c == 0), stop=(c == NC_D - 1)))
        p.wait("scalar", cm.slot[b])
        if c >= 2:
            p.wait("scalar", cm.st, st_marks[c - 2])
        p.op("scalar", lambda e, c=c: e.activation(
            out=sq[:, (c % 2) * ntok:(c % 2) * ntok + ntok], in_=xbuf[c % 2][:],
            func=AF.Square, bias=0.0, scale=1.0), inc=cm.ss)
        p.wait("tensor", cm.ss)
        for h in range(nh):
            p.op("tensor", lambda e, c=c, h=h: e.matmul(
                ps_s2[0:1, h * 512:(h + 1) * 512], cm.ones_c16[:],
                sq[:, (c % 2) * ntok + h * 512:(c % 2) * ntok + (h + 1) * 512],
                start=(c == 0), stop=(c == NC_D - 1)),
                inc=cm.st)
        st_marks[c] = cm.st.n
    emit_ln_stats_rows(p, cm, rows, ps_s1[0:1, 0:ntok], ps_s2[0:1, 0:ntok], ntok)
    emit_ln_bcast(p, cm, rows, ntok, ps_bc, bcast_r32, bcast_m, rows16)
    v_marks = {}
    p.wait("sync", cm.ss)
    p.wait("sync", cm.st)
    for c in range(NC_D):
        b = c % 2
        if c >= 2:
            p.wait("sync", cm.sv, v_marks[c - 2])
        p.dma("sync", xbuf[b][:], xr[:, c, :], cm.slot[b])
        p.wait("vector", cm.slot[b])
        p.op("vector", lambda e, c=c: e.tensor_mul(tmp32, xbuf[c % 2][:],
                                                   bcast_r32[:]))
        p.op("vector", lambda e, c=c: e.tensor_sub(XN[:, c, :], tmp32,
                                                   bcast_m[:]), inc=cm.sv)
        v_marks[c] = cm.sv.n



def emit_proj_fm_half(p, cm, W_hbm, XN, nmc, ntok, wpan32, wpan16, ps_pair,
                      evac, nic=NC_D):
    nh = ntok // 512
    cast_marks = {}
    group_marks = {}
    evac_marks = {}
    ss_base = cm.ss.n
    nq = max(1, nic // 8)
    nparts = (nic + nq - 1) // nq
    pctr = 0
    pmark = {0: 0, 1: 0}
    for mc in range(nmc):
        b = mc % 2
        for part in range(nparts):
            pb = pctr % 2
            lo = part * nq
            hi = min(nic, lo + nq)
            src = W_hbm[lo * 128:hi * 128,
                        mc * 128:(mc + 1) * 128].rearrange(
                "(ic p) m -> p ic m", p=128)
            p.wait("sync", cm.sv, pmark[pb])
            p.dma("sync",
                  wpan32[pb][:, 0:(hi - lo) * 128].rearrange(
                      "p (ic m) -> p ic m", m=128), src, cm.slot[2 + pb])
            p.wait("vector", cm.slot[2 + pb])
            if mc >= 2 and part == 0:
                p.wait("vector", cm.st, group_marks[mc - 2])
            p.op("vector", lambda e, b=b, pb=pb, lo=lo, n=(hi - lo) * 128:
                 e.tensor_copy(out=wpan16[b][:, lo * 128:lo * 128 + n],
                               in_=wpan32[pb][:, 0:n]), inc=cm.sv)
            pmark[pb] = cm.sv.n
            pctr += 1
        cast_marks[mc] = cm.sv.n
        ps = ps_pair[mc % 2]
        p.wait("tensor", cm.sv)
        if mc >= 2:
            p.wait("tensor", cm.ss, evac_marks[mc - 2])
        else:
            p.wait("tensor", cm.ss, ss_base)
        for ic in range(nic):
            for h in range(nh):
                p.op("tensor", lambda e, b=b, ic=ic, h=h, ps=ps: e.matmul(
                    ps[:, h * 512:(h + 1) * 512],
                    wpan16[b][:, ic * 128:(ic + 1) * 128],
                    XN[:, ic, h * 512:(h + 1) * 512],
                    start=(ic == 0), stop=(ic == nic - 1)),
                    inc=cm.st if (ic == nic - 1 and h == nh - 1) else None)
        group_marks[mc] = cm.st.n
        evac(mc, ps)
        evac_marks[mc] = cm.ss.n



def build_full():
    nc = bass.Bass("TRN2", target_bir_lowering=False)

    xT = nc.dram_tensor("xT", [D, TOK], F32, kind="ExternalInput").ap()
    wqT = nc.dram_tensor("wqT", [D, D], F32, kind="ExternalInput").ap()
    wkT = nc.dram_tensor("wkT", [D, D], F32, kind="ExternalInput").ap()
    wvT = nc.dram_tensor("wvT", [D, D], F32, kind="ExternalInput").ap()
    bq_r = nc.dram_tensor("bq_r", [128, NC_D], F32, kind="ExternalInput").ap()
    bk_r = nc.dram_tensor("bk_r", [128, NC_D], F32, kind="ExternalInput").ap()
    bv_row = nc.dram_tensor("bv_row", [1, D], F32, kind="ExternalInput").ap()
    pk1 = nc.dram_tensor("pk1", [DQ_HC * 4096, 64], I32, kind="ExternalInput").ap()
    pk3 = nc.dram_tensor("pk3", [DQ_HC * 4096, 64], I32, kind="ExternalInput").ap()
    s1_r = nc.dram_tensor("s1_r", [128, DQ_HC * 64], F32, kind="ExternalInput").ap()
    s3_r = nc.dram_tensor("s3_r", [128, DQ_HC * 64], F32, kind="ExternalInput").ap()
    pk2 = nc.dram_tensor("pk2", [DQ_OC * 128, NHC * 64], I32,
                         kind="ExternalInput").ap()
    s2_r = nc.dram_tensor("s2_r", [128, DQ_OC * 172], F32, kind="ExternalInput").ap()
    woT = nc.dram_tensor("woT", [D, D], F32, kind="ExternalInput").ap()
    bo_r = nc.dram_tensor("bo_r", [128, NC_D], F32, kind="ExternalInput").ap()
    wq1 = nc.dram_tensor("wq1", [D, D], F32, kind="ExternalInput").ap()
    wk1 = nc.dram_tensor("wk1", [D, 1024], F32, kind="ExternalInput").ap()
    wv1 = nc.dram_tensor("wv1", [D, 1024], F32, kind="ExternalInput").ap()
    cosb = nc.dram_tensor("cosb", [64, TOK], BF16, kind="ExternalInput").ap()
    sinb = nc.dram_tensor("sinb", [64, TOK], BF16, kind="ExternalInput").ap()
    wo1 = nc.dram_tensor("wo1", [D, D], F32, kind="ExternalInput").ap()
    bwT = nc.dram_tensor("bwT", [D, D], F32, kind="ExternalInput").ap()
    bb_r = nc.dram_tensor("bb_r", [128, NC_D], F32, kind="ExternalInput").ap()
    aT = nc.dram_tensor("aT", [D, 64], F32, kind="ExternalInput").ap()
    bstk = nc.dram_tensor("bstk", [1024, D], F32, kind="ExternalInput").ap()
    rwT = nc.dram_tensor("rwT", [D, 16], F32, kind="ExternalInput").ap()
    lmask = nc.dram_tensor("lmask", [2, NC_D, 128, 512], BF16,
                           kind="ExternalInput").ap()

    outT = nc.dram_tensor("outT", [D, TOK], F32, kind="ExternalOutput").ap()

    q0T = nc.dram_tensor("q0T", [D, TOK], BF16, kind="Internal").ap()
    k0_loc = nc.dram_tensor("k0_loc", [D, TOK], BF16, kind="Internal").ap()
    v0_loc = nc.dram_tensor("v0_loc", [TOK, D], BF16, kind="Internal").ap()
    k0g = nc.dram_tensor("k0g", [4, D, TOK], BF16, kind="Internal").ap()
    v0g = nc.dram_tensor("v0g", [4, TOK, D], BF16, kind="Internal").ap()
    w1t_loc = nc.dram_tensor("w1t_loc", [128, DQ_HC * NC_D, 128], BF16,
                             kind="Internal").ap()
    w3t_loc = nc.dram_tensor("w3t_loc", [128, DQ_HC * NC_D, 128], BF16,
                             kind="Internal").ap()
    w2t_loc = nc.dram_tensor("w2t_loc", [128, DQ_OC * NHC, 128], BF16,
                             kind="Internal").ap()
    w1g = nc.dram_tensor("w1g", [8, 128, DQ_HC * NC_D, 128], BF16,
                         kind="Internal").ap()
    w3g = nc.dram_tensor("w3g", [8, 128, DQ_HC * NC_D, 128], BF16,
                         kind="Internal").ap()
    w2g = nc.dram_tensor("w2g", [8, 128, DQ_OC * NHC, 128], BF16,
                         kind="Internal").ap()
    ctxT = nc.dram_tensor("ctxT", [D, TOK], BF16, kind="Internal").ap()
    x2T = nc.dram_tensor("x2T", [D, TOK], BF16, kind="Internal").ap()
    hsp = nc.dram_tensor("hsp", [NHC, 128, TOK], BF16, kind="Internal").ap()
    q1T = nc.dram_tensor("q1T", [D, TOK], BF16, kind="Internal").ap()
    k1_loc = nc.dram_tensor("k1_loc", [1024, TOK], BF16, kind="Internal").ap()
    v1_loc = nc.dram_tensor("v1_loc", [TOK, 1024], BF16, kind="Internal").ap()
    k1g = nc.dram_tensor("k1g", [4, 1024, TOK], BF16, kind="Internal").ap()
    v1g = nc.dram_tensor("v1g", [4, TOK, 1024], BF16, kind="Internal").ap()
    ctxs = nc.dram_tensor("ctxs", [D, TOK], BF16, kind="Internal").ap()

    p = Prog(nc)
    cm = Common(nc, p)
    al = nc.alloc_sbuf_tensor

    AR_A = al("arA", [128, 32768], BF16).ap()
    AR_B = al("arB", [128, 32768], BF16).ap()
    AR_C = al("arC", [128, 24576], BF16).ap()
    sq = al("p_sq", [128, 2048], BF16).ap()
    evacb = [al(f"p_evac{i}", [128, 1024], BF16).ap() for i in range(2)]
    rows = al("p_rows", [1, 2048], F32).ap()
    ropeb = al("p_ropeb", [128, 2048], BF16).ap()
    bcast_r16 = al("p_bcr16", [128, 1024], BF16).ap()
    bcast_m16 = al("p_bcm16", [128, 1024], F32).ap()
    bias_q = al("p_bq", [128, NC_D], F32).ap()
    bias_k = al("p_bk", [128, NC_D], F32).ap()
    bias_o = al("p_bo", [128, NC_D], F32).ap()
    bias_b = al("p_bb", [128, NC_D], F32).ap()

    pool = nc.alloc_psum_tensor("pool", [128, 4096], F32)
    ps_s1 = pool[0:1, 0:1024]
    ps_s2 = pool[0:1, 1024:2048]
    ps_bc = pool[:, 2048:4096]
    ps_pair = [pool[:, 0:1024], pool[:, 1024:2048]]

    Af = AR_A
    Bf = AR_B
    Cf = AR_C

    XN = Af.rearrange("p (c t) -> p c t", t=TOK)
    vcache = Bf[:, 0:16384].rearrange("p (c v) -> p c v", v=512)
    xbuf = [Bf[:, 16384:18432].bitcast(F32), Bf[:, 18432:20480].bitcast(F32)]
    bcast_r32 = Bf[:, 20480:22528].bitcast(F32)
    bcast_m = Bf[:, 22528:24576].bitcast(F32)
    bv_r16 = Bf[0:1, 24576:28672]
    wpan32 = [Cf[:, 0:2048].bitcast(F32), Cf[:, 2048:4096].bitcast(F32)]
    wpan16 = [Cf[:, 4096:8192], Cf[:, 8192:12288]]
    dq_sc = Cf[:, 12288:15104].bitcast(F32)
    dq_sc2 = Cf[:, 15104:16480].bitcast(F32)
    dq_tr = [Cf[:, 16480:16992], Cf[:, 16992:17504]]
    rows16 = sq[0:1, 0:2048]

    xnflat = Af
    pkbuf = [xnflat[:, 0:5504].bitcast(I32),
             xnflat[:, 5504:11008].bitcast(I32)]
    dq_i = xnflat[:, 11008:16512].bitcast(I32)
    dq_f = xnflat[:, 16512:22016].bitcast(F32)
    dq_w = xnflat[:, 22016:27520].rearrange("p (g f) -> p g f", f=128)

    ps_v8 = [pool[:, i * 512:(i + 1) * 512] for i in range(8)]
    ps_tr = pool[:, 0:1024].bitcast(BF16)

    p.dma("sync", bias_q[:], bq_r, cm.ld)
    p.dma("sync", bias_k[:], bk_r, cm.ld)
    p.dma("sync", bias_o[:], bo_r, cm.ld)
    p.dma("sync", bias_b[:], bb_r, cm.ld)
    p.dma("sync", ropeb[0:64, 0:TOK], cosb, cm.ld)
    p.dma("sync", ropeb[0:64, TOK:2 * TOK], sinb, cm.ld)
    p.dma("sync", ropeb[64:128, 0:TOK], cosb, cm.ld)
    p.dma("sync", ropeb[64:128, TOK:2 * TOK], sinb, cm.ld)
    p.dma("sync", dq_sc[:, 0:DQ_HC * 64], s1_r, cm.ld)
    p.dma("sync", dq_sc[:, DQ_HC * 64:2 * DQ_HC * 64], s3_r, cm.ld)
    p.dma("sync", dq_sc2[:], s2_r, cm.ld)
    for i in range(4):
        p.dma("sync", xbuf[i % 2][0:1, 0:1024],
              bv_row[0:1, i * 1024:(i + 1) * 1024], cm.ld)
        cm.barrier()
        p.op("vector", lambda e, i=i: e.tensor_copy(
            out=bv_r16[0:1, i * 1024:(i + 1) * 1024],
            in_=xbuf[i % 2][0:1, 0:1024]), inc=cm.sv)
    cm.barrier()
    p.op("vector", lambda e: e.tensor_scalar_mul(bias_q[:], bias_q[:],
                                                 INV_SQRT_DH), inc=cm.sv)
    cm.barrier()

    emit_ln_stream(p, cm, xT, XN, TOK, xbuf, sq, rows, ps_s1, ps_s2, ps_bc,
                   bcast_r32, bcast_m, rows16, sq.bitcast(F32)[:, 0:TOK])
    cm.barrier()

    def mk_evac(out_hbm, bias_cols, scale):
        out_r = out_hbm.rearrange("(c p) t -> p c t", p=128)

        def evac(mc, ps):
            b = mc % 2
            p.wait("scalar", cm.st)
            p.wait("scalar", cm.slot[4 + b])
            p.op("scalar", lambda e, mc=mc, ps=ps, b=b: e.activation(
                out=evacb[b][:], in_=ps[:], func=AF.Identity,
                bias=bias_cols[:, mc:mc + 1], scale=scale), inc=cm.ss)
            p.dma("scalar", out_r[:, mc, :], evacb[b][:], cm.slot[4 + b])
        return evac

    emit_proj_fm_half(p, cm, wkT, XN, NC_D, TOK, wpan32, wpan16, ps_pair,
                      mk_evac(k0_loc, bias_k, 1.0))
    p.wait("gpsimd", cm.slot[4])
    p.wait("gpsimd", cm.slot[5])
    p.raw("gpsimd", lambda e: e.collective_compute(
        "AllGather", ALU.bypass, replica_groups=GROUPS4,
        ins=[k0_loc.opt()], outs=[k0g.opt()]), cm.cc)
    cm.barrier()

    for v8 in range(8):
        cast_marks = {}
        for ic in range(NC_D):
            b = ic % 2
            if ic >= 2:
                p.wait("sync", cm.sv, cast_marks[ic - 2])
            src = wvT[ic * 128:(ic + 1) * 128, v8 * 512:(v8 + 1) * 512]
            p.dma("sync", wpan32[b][:, 0:512], src, cm.slot[2 + b])
            p.wait("vector", cm.slot[2 + b])
            p.op("vector", lambda e, b=b, ic=ic: e.tensor_copy(
                out=vcache[:, ic, :], in_=wpan32[b][:, 0:512]), inc=cm.sv)
            cast_marks[ic] = cm.sv.n
        p.wait("tensor", cm.sv)
        for tc in range(8):
            ps = ps_v8[tc]
            for ic in range(NC_D):
                p.op("tensor", lambda e, ic=ic, tc=tc, ps=ps, v8=v8: e.matmul(
                    ps[:], XN[:, ic, tc * 128:(tc + 1) * 128],
                    vcache[:, ic, :], start=(ic == 0), stop=False))
            p.op("tensor", lambda e, ps=ps, v8=v8: e.matmul(
                ps[:], cm.ones_r16[:],
                bv_r16[0:1, v8 * 512:(v8 + 1) * 512],
                start=False, stop=True), inc=cm.st)
        p.wait("vector", cm.st)
        for tc in range(8):
            b = tc % 2
            p.wait("vector", cm.slot[4 + b])
            p.op("vector", lambda e, tc=tc, b=b: e.tensor_copy(
                out=evacb[b][:, 0:512], in_=ps_v8[tc][:]), inc=cm.sv)
            p.wait("scalar", cm.sv)
            p.dma("scalar", v0_loc[tc * 128:(tc + 1) * 128,
                                   v8 * 512:(v8 + 1) * 512],
                  evacb[b][:, 0:512], cm.slot[4 + b])
        cm.barrier()

    p.raw("gpsimd", lambda e: e.collective_compute(
        "AllGather", ALU.bypass, replica_groups=GROUPS4,
        ins=[v0_loc.opt()], outs=[v0g.opt()]), cm.cc)
    CC_KV0 = cm.cc.n

    emit_proj_fm_half(p, cm, wqT, XN, NC_D, TOK, wpan32, wpan16, ps_pair,
                      mk_evac(q0T, bias_q, INV_SQRT_DH))
    cm.barrier()

    def emit_dequant(pk_src_fn, sc_of, out_hbm, n_sub, ntiles, tilebase_of):
        shift_marks = {}
        p.wait("sync", cm.slot[0])
        p.wait("sync", cm.slot[1])
        p.wait("sync", cm.sv)
        for s in range(n_sub):
            b = s % 2
            n = ntiles * 64
            if s >= 2:
                p.wait("sync", cm.sv, shift_marks[s - 2])
            srcap = pk_src_fn(s)
            dstap = pkbuf[b][:, 0:n]
            if len(srcap.shape) == 3:
                dstap = dstap.rearrange("p (r c) -> p r c", c=srcap.shape[2])
            p.dma("sync", dstap, srcap, cm.slot[b])
            p.wait("vector", cm.slot[b])
            sc = sc_of(s)
            for half in range(2):
                if half == 0:
                    p.op("vector", lambda e, b=b, n=n: e.tensor_single_scalar(
                        out=dq_i[:, 0:n], in_=pkbuf[b][:, 0:n], scalar=4,
                        op=ALU.arith_shift_right), inc=cm.sv)
                else:
                    p.op("vector", lambda e, b=b, n=n: e.tensor_single_scalar(
                        out=dq_i[:, 0:n], in_=pkbuf[b][:, 0:n], scalar=28,
                        op=ALU.logical_shift_left))
                    p.op("vector", lambda e, n=n: e.tensor_single_scalar(
                        out=dq_i[:, 0:n], in_=dq_i[:, 0:n], scalar=28,
                        op=ALU.arith_shift_right), inc=cm.sv)
                if half == 1:
                    shift_marks[s] = cm.sv.n
                p.wait("scalar", cm.sv)
                p.op("scalar", lambda e, n=n: e.activation(
                    out=dq_f[:, 0:n], in_=dq_i[:, 0:n], func=AF.Copy,
                    bias=0.0, scale=1.0), inc=cm.ss)
                p.wait("vector", cm.ss)
                p.op("vector", lambda e, n=n, nt=ntiles, sc=sc, half=half:
                     e.tensor_mul(
                         dq_w[:, 0:nt, half * 64:half * 64 + 64],
                         dq_f[:, 0:n].rearrange("p (g j) -> p g j", j=64),
                         sc[:, half::2].to_broadcast([128, nt, 64])), inc=cm.sv)
            p.wait("tensor", cm.sv)
            for g0 in range(0, ntiles, 4):
                gb = (g0 // 4) % 2
                ng = min(4, ntiles - g0)
                p.wait("tensor", cm.slot[6 + gb])
                for t in range(ng):
                    p.op("tensor", lambda e, g0=g0, t=t, gb=gb: e.transpose(
                        ps_tr.rearrange("p (u f) -> p u f", f=128)[:, gb * 8 + t, :],
                        dq_w[:, g0 + t, :], cm.ident[:]),
                        inc=cm.st if t == ng - 1 else None)
                p.wait("vector", cm.st)
                p.op("vector", lambda e, gb=gb, n=ng * 128: e.tensor_copy(
                    out=dq_tr[gb][:, 0:n],
                    in_=ps_tr[:, gb * 1024:gb * 1024 + n]), inc=cm.sv)
                p.wait("scalar", cm.sv)
                base = tilebase_of(s) + g0
                dst = out_hbm[:, base:base + ng, :]
                p.dma("scalar", dst,
                      dq_tr[gb][:, 0:ng * 128].rearrange("i (t h) -> i t h", h=128),
                      cm.slot[6 + gb])

    emit_dequant(
        lambda s: pk1[s * 4096:(s + 1) * 4096, :].rearrange(
            "(p r) c -> p r c", p=128),
        lambda s: dq_sc[:, s * 64:(s + 1) * 64],
        w1t_loc, DQ_HC, NC_D, lambda s: s * NC_D)
    emit_dequant(
        lambda s: pk3[s * 4096:(s + 1) * 4096, :].rearrange(
            "(p r) c -> p r c", p=128),
        lambda s: dq_sc[:, DQ_HC * 64 + s * 64:DQ_HC * 64 + (s + 1) * 64],
        w3t_loc, DQ_HC, NC_D, lambda s: s * NC_D)
    emit_dequant(
        lambda s: pk2[(s // 2) * 128:(s // 2) * 128 + 128,
                      (s % 2) * 43 * 64:(s % 2) * 43 * 64 + 43 * 64],
        lambda s: dq_sc2[:, (s // 2) * 172 + (s % 2) * 86:
                         (s // 2) * 172 + (s % 2) * 86 + 86],
        w2t_loc, DQ_OC * 2, 43, lambda s: (s // 2) * NHC + (s % 2) * 43)

    cm.barrier()

    p.raw("gpsimd", lambda e: e.collective_compute(
        "AllGather", ALU.bypass, replica_groups=GROUPS8,
        ins=[w1t_loc.opt()], outs=[w1g.opt()]), cm.cc)
    p.raw("gpsimd", lambda e: e.collective_compute(
        "AllGather", ALU.bypass, replica_groups=GROUPS8,
        ins=[w3t_loc.opt()], outs=[w3g.opt()]), cm.cc)
    p.raw("gpsimd", lambda e: e.collective_compute(
        "AllGather", ALU.bypass, replica_groups=GROUPS8,
        ins=[w2t_loc.opt()], outs=[w2g.opt()]), cm.cc)
    CC_W = cm.cc.n

    NKC = SEQ // 128
    E2 = [Af[:, 0:16384].rearrange("p (c q) -> p c q", q=512),
          Af[:, 16384:32768].rearrange("p (c q) -> p c q", q=512)]
    M = Bf[:, 0:16384].rearrange("p (c q) -> p c q", q=512)
    Kh = [Bf[:, 16384:20480], Bf[:, 20480:24576]]
    Qh = [Bf[:, 24576:25600], Bf[:, 25600:26624]]
    tb = Bf[:, 26624:27136]
    rowsA = Bf[0:1, 27136:28160].bitcast(F32)
    rowsB0 = Bf[0:1, 28160:29184].bitcast(F32)
    rowsG = Bf[0:1, 29184:30208].bitcast(F32)
    rowsT = Bf[0:1, 30208:31232].bitcast(F32)
    rowsT16 = Bf[0:1, 31232:31744]
    rowsR16 = Bf[0:1, 31744:32256]
    Vh = [Cf[:, 0:4096].rearrange("p (c d) -> p c d", d=128),
          Cf[:, 4096:8192].rearrange("p (c d) -> p c d", d=128)]
    ctxb = [Cf[:, 8192:8704], Cf[:, 8704:9216]]
    rb = Cf[:, 9216:10240].bitcast(F32)
    S4raw = Cf[:, 10240:12288].rearrange("p (c q) -> p c q", q=512)

    ps_s = [pool[:, i * 512:(i + 1) * 512] for i in range(4)]
    ps_ctx = pool[:, 4 * 512:5 * 512]
    ps_r1 = pool[0:1, 5 * 512:6 * 512]
    ps_r2 = pool[0:1, 6 * 512:7 * 512]
    ps_bc2 = pool[:, 7 * 512:8 * 512]

    p.wait("sync", cm.cc, CC_KV0)
    e_done = {}
    pbank = {}
    gctr = 0
    for h in range(NH0):
        hb = h % 2
        p.wait("sync", cm.st)
        for g in range(4):
            p.dma("sync", Kh[hb][:, g * 1024:(g + 1) * 1024],
                  k0g[g, h * 128:(h + 1) * 128, :], cm.slot[hb])
            p.dma("sync", Vh[hb][:, g * 8:(g + 1) * 8, :],
                  v0g[g][:, h * 128:(h + 1) * 128].rearrange(
                      "(c p) d -> p c d", p=128), cm.slot[hb])
        p.dma("sync", Qh[hb][:], q0T[h * 128:(h + 1) * 128, :], cm.slot[hb])
        for qg in range(2):
            eb = gctr % 2
            E = E2[eb]
            p.wait("tensor", cm.slot[hb])
            for c in range(NKC):
                bk = c % 4
                if bk in pbank:
                    p.wait("tensor", cm.ss, pbank[bk][0])
                    if pbank[bk][1]:
                        p.wait("tensor", cm.sv, pbank[bk][1])
                p.op("tensor", lambda e, c=c, bk=bk, hb=hb, qg=qg: e.matmul(
                    ps_s[bk][:], Kh[hb][:, c * 128:(c + 1) * 128],
                    Qh[hb][:, qg * 512:(qg + 1) * 512],
                    start=True, stop=True), inc=cm.st)
                p.wait("scalar", cm.st)
                if gctr >= 2 and c == 0:
                    p.wait("scalar", cm.st, e_done[gctr - 2][0])
                    p.wait("scalar", cm.sv, e_done[gctr - 2][1])
                p.op("scalar", lambda e, c=c, bk=bk, E=E: e.activation(
                    out=E[:, c, :], in_=ps_s[bk][:], func=AF.Exp,
                    bias=0.0, scale=1.0), inc=cm.ss)
                svm = None
                if c < 4:
                    if c == 0:
                        p.wait("vector", cm.ss)
                    p.wait("vector", cm.st)
                    p.op("vector", lambda e, c=c, bk=bk: e.tensor_copy(
                        out=S4raw[:, c, :], in_=ps_s[bk][:]), inc=cm.sv)
                    svm = cm.sv.n
                pbank[bk] = (cm.ss.n, svm)
            p.wait("tensor", cm.sv)
            for c in range(4):
                p.op("tensor", lambda e, c=c: e.matmul(
                    ps_r1[:], cm.ones_c16[:], S4raw[:, c, :],
                    start=(c == 0), stop=(c == 3)),
                    inc=cm.st if c == 3 else None)
            p.wait("scalar", cm.sv)
            p.op("scalar", lambda e: e.activation(
                out=sq[:], in_=S4raw.rearrange("p c q -> p (c q)"),
                func=AF.Square, bias=0.0, scale=1.0), inc=cm.ss)
            p.wait("tensor", cm.ss)
            for c in range(4):
                p.op("tensor", lambda e, c=c: e.matmul(
                    ps_r2[:], cm.ones_c16[:], sq[:, c * 512:(c + 1) * 512],
                    start=(c == 0), stop=(c == 3)),
                    inc=cm.st if c == 3 else None)
            p.wait("vector", cm.st)
            p.op("vector", lambda e: e.tensor_scalar_mul(rowsA, ps_r1, 1.0 / 512))
            p.op("vector", lambda e: e.tensor_scalar_mul(rowsB0, ps_r2, 1.0 / 512))
            p.op("vector", lambda e: e.tensor_mul(rowsT, rowsA, rowsA))
            p.op("vector", lambda e: e.tensor_sub(rowsB0, rowsB0, rowsT))
            p.op("vector", lambda e: e.tensor_scalar_add(rowsB0, rowsB0, 1e-6),
                 inc=cm.sv)
            p.wait("scalar", cm.sv)
            p.op("scalar", lambda e: e.activation(out=rowsB0, in_=rowsB0,
                                                  func=AF.Ln, bias=0.0, scale=1.0))
            p.op("scalar", lambda e: e.activation(out=rowsB0, in_=rowsB0,
                                                  func=AF.Exp, bias=0.0,
                                                  scale=0.5), inc=cm.ss)
            p.wait("vector", cm.ss)
            p.op("vector", lambda e: e.tensor_scalar_mul(rowsG, rowsB0,
                                                         float(NEWTON_G)))
            p.op("vector", lambda e: e.tensor_scalar_mul(rowsT, rowsB0, float(Z_Q)))
            p.op("vector", lambda e: e.tensor_add(rowsT, rowsT, rowsA),
                 inc=cm.sv)
            for it in range(3):
                p.wait("scalar", cm.sv)
                p.op("scalar", lambda e: e.activation(
                    out=rowsT16, in_=rowsT, func=AF.Exp, bias=0.0, scale=1.0),
                    inc=cm.ss)
                p.wait("tensor", cm.ss)
                p.wait("tensor", cm.sv)
                p.op("tensor", lambda e: e.matmul(
                    ps_bc2[:], cm.ones_r16[:], rowsT16[0:1, :],
                    start=True, stop=True), inc=cm.st)
                p.wait("vector", cm.st)
                p.op("vector", lambda e: e.tensor_copy(out=tb[:], in_=ps_bc2[:]),
                     inc=cm.sv)
                if it == 2:
                    p.op("vector", lambda e, E=E: e.tensor_tensor(
                        out=M, in0=E,
                        in1=tb.rearrange("p (u q) -> p u q", u=1).to_broadcast(
                            [128, NKC, 512]),
                        op=ALU.is_gt))
                    p.op("vector", lambda e, E=E: e.tensor_mul(E, M, E),
                         inc=cm.sv)
                    sm_mark = cm.sv.n
                    break
                p.op("vector", lambda e, E=E: e.tensor_tensor(
                    out=M, in0=E,
                    in1=tb.rearrange("p (u q) -> p u q", u=1).to_broadcast(
                        [128, NKC, 512]),
                    op=ALU.is_gt))
                p.op("vector", lambda e: e.tensor_add(
                    M[:, 0:16, :], M[:, 0:16, :], M[:, 16:32, :]))
                p.op("vector", lambda e: e.tensor_add(
                    M[:, 0:8, :], M[:, 0:8, :], M[:, 8:16, :]))
                p.op("vector", lambda e: e.tensor_add(
                    M[:, 0:4, :], M[:, 0:4, :], M[:, 4:8, :]), inc=cm.sv)
                p.wait("tensor", cm.sv)
                for c in range(4):
                    p.op("tensor", lambda e, c=c: e.matmul(
                        ps_r1[:], cm.ones_c16[:], M[:, c, :],
                        start=(c == 0), stop=(c == 3)),
                        inc=cm.st if c == 3 else None)
                p.wait("vector", cm.st)
                p.op("vector", lambda e: e.tensor_scalar_add(rowsB0, ps_r1,
                                                             -float(TOPK)))
                p.op("vector", lambda e: e.tensor_mul(rowsB0, rowsB0, rowsG))
                p.op("vector", lambda e: e.tensor_add(rowsT, rowsT, rowsB0),
                     inc=cm.sv)
            p.wait("tensor", cm.sv, sm_mark)
            for c in range(NKC):
                p.op("tensor", lambda e, c=c, hb=hb, E=E: e.matmul(
                    ps_ctx[:], Vh[hb][:, c, :], E[:, c, :],
                    start=(c == 0), stop=(c == NKC - 1)))
            for c in range(NKC):
                p.op("tensor", lambda e, c=c, E=E: e.matmul(
                    ps_r2[:], cm.ones_c16[:], E[:, c, :],
                    start=(c == 0), stop=(c == NKC - 1)),
                    inc=cm.st if c == NKC - 1 else None)
            pv_mark = cm.st.n
            e_done[gctr] = (pv_mark, sm_mark)
            p.wait("vector", cm.st)
            p.op("vector", lambda e: e.reciprocal(out=rowsB0, in_=ps_r2))
            p.op("vector", lambda e: e.tensor_copy(out=rowsR16, in_=rowsB0),
                 inc=cm.sv)
            p.wait("tensor", cm.sv)
            p.op("tensor", lambda e: e.matmul(ps_bc2[:], cm.ones_r16[:],
                                              rowsR16[0:1, :],
                                              start=True, stop=True), inc=cm.st)
            p.wait("vector", cm.st)
            p.op("vector", lambda e: e.tensor_copy(out=rb[:], in_=ps_bc2[:]))
            gb = qg % 2
            p.wait("vector", cm.slot[4 + gb])
            p.op("vector", lambda e, gb=gb: e.tensor_mul(ctxb[gb][:], ps_ctx[:],
                                                         rb[:]), inc=cm.sv)
            p.wait("scalar", cm.sv)
            p.dma("scalar", ctxT[h * 128:(h + 1) * 128, qg * 512:(qg + 1) * 512],
                  ctxb[gb][:], cm.slot[4 + gb])
            gctr += 1
    p.wait("sync", cm.slot[4])
    p.wait("sync", cm.slot[5])
    cm.barrier()

    A = Af.rearrange("p (c t) -> p c t", t=TOK)
    B = Bf.rearrange("p (c t) -> p c t", t=TOK)
    scr8 = Cf[:, 0:8192]
    xbuf3 = [scr8[:, 0:2048].bitcast(F32), scr8[:, 2048:4096].bitcast(F32)]
    xb16 = scr8[:, 4096:5120]
    w13 = [scr8[:, 0:4096].rearrange("p (ic m) -> p ic m", m=128),
           scr8[:, 4096:8192].rearrange("p (ic m) -> p ic m", m=128)]
    vcache3 = scr8[:, 0:8192].rearrange("p (ic v) -> p ic v", v=256)
    wpan32m = [Cf[:, 8192:10240].bitcast(F32), Cf[:, 10240:12288].bitcast(F32)]
    wpan16m = [Cf[:, 12288:16384], Cf[:, 16384:20480]]
    htile = [Cf[:, 20480:21504], Cf[:, 21504:22528]]
    w2til = [Cf[:, 22528:23040].rearrange("p (o h) -> p o h", h=128),
             Cf[:, 23040:23552].rearrange("p (o h) -> p o h", h=128)]
    rtmp = sq

    ps_up = [[pool[:, 0:1024], pool[:, 1024:2048]],
             [pool[:, 2048:3072], pool[:, 3072:4096]]]
    ps_dn = [pool[:, i * 512:(i + 1) * 512] for i in range(8)]
    ps_v4 = [pool[:, i * 256:(i + 1) * 256] for i in range(8)]

    ctr = ctxT.rearrange("(c p) t -> p c t", p=128)
    for c in range(NC_D):
        p.dma("sync", A[:, c, :], ctr[:, c, :], cm.ld)
    cm.barrier()

    xr = xT.rearrange("(c p) t -> p c t", p=128)

    def evac_oproj(mc, ps):
        b = mc % 2
        p.wait("sync", cm.sv)
        p.dma("sync", xbuf3[b][:], xr[:, mc, :], cm.slot[6 + b])
        p.wait("scalar", cm.st)
        p.op("scalar", lambda e, mc=mc, ps=ps, b=b: e.activation(
            out=evacb[b][:], in_=ps[:], func=AF.Identity,
            bias=bias_o[:, mc:mc + 1], scale=1.0), inc=cm.ss)
        p.wait("vector", cm.slot[6 + b])
        p.wait("vector", cm.ss)
        p.op("vector", lambda e, b=b: e.tensor_copy(out=xb16[:], in_=xbuf3[b][:]))
        p.op("vector", lambda e, mc=mc, b=b: e.tensor_add(
            B[:, mc, :], evacb[b][:], xb16[:]), inc=cm.sv)

    emit_proj_fm_half(p, cm, woT, A, NC_D, TOK, wpan32m, wpan16m, ps_pair,
                      evac_oproj)
    cm.barrier()

    emit_ln_resident(p, cm, B, A, TOK, sq, rows, ps_s1, ps_s2, ps_bc,
                     bcast_r16, bcast_m16, rows16, xbuf3[0])
    cm.barrier()

    w13p = [w13,
            [Cf[:, 12288:16384].rearrange("p (ic m) -> p ic m", m=128),
             Cf[:, 16384:20480].rearrange("p (ic m) -> p ic m", m=128)]]

    def w13_src(wg, hc):
        core, qq = hc // DQ_HC, hc % DQ_HC
        return wg[core][:, qq * NC_D:(qq + 1) * NC_D, :]

    p.wait("sync", cm.cc, CC_W)
    p.wait("gpsimd", cm.cc, CC_W)
    up_marks = {}
    fm_marks = {}
    st_base_up = cm.st.n
    for hc in range(NHC):
        ub = hc % 2
        w13c = w13p[ub]
        p.wait("sync", cm.st, fm_marks.get(hc - 2, st_base_up))
        p.wait("gpsimd", cm.st, fm_marks.get(hc - 2, st_base_up))
        p.dma("sync", w13c[0], w13_src(w1g, hc), cm.slot[0 + ub])
        p.dma("gpsimd", w13c[1], w13_src(w3g, hc), cm.slot[2 + ub])
        p.wait("tensor", cm.slot[0 + ub])
        p.wait("tensor", cm.slot[2 + ub])
        if hc >= 2:
            p.wait("tensor", cm.ss, up_marks[hc - 2][0])
            p.wait("tensor", cm.sv, up_marks[hc - 2][1])
        psA, psB = ps_up[ub]
        for ic in range(NC_D):
            for hh in range(2):
                p.op("tensor", lambda e, ic=ic, hh=hh, psA=psA, w13c=w13c:
                     e.matmul(
                         psA[:, hh * 512:(hh + 1) * 512], w13c[0][:, ic, :],
                         A[:, ic, hh * 512:(hh + 1) * 512],
                         start=(ic == 0), stop=(ic == NC_D - 1)))
        for ic in range(NC_D):
            for hh in range(2):
                p.op("tensor", lambda e, ic=ic, hh=hh, psB=psB, w13c=w13c:
                     e.matmul(
                         psB[:, hh * 512:(hh + 1) * 512], w13c[1][:, ic, :],
                         A[:, ic, hh * 512:(hh + 1) * 512],
                         start=(ic == 0), stop=(ic == NC_D - 1)),
                     inc=cm.st if (ic == NC_D - 1 and hh == 1) else None)
        fm_marks[hc] = cm.st.n
        p.wait("scalar", cm.st)
        p.wait("scalar", cm.slot[4 + ub])
        p.op("scalar", lambda e, psA=psA, ub=ub: e.activation(
            out=htile[ub][:], in_=psA[:], func=AF.Silu, bias=0.0, scale=1.0),
            inc=cm.ss)
        s_mark = cm.ss.n
        p.wait("vector", cm.ss)
        p.op("vector", lambda e, psB=psB, ub=ub: e.tensor_copy(
            out=evacb[ub][:], in_=psB[:]))
        p.op("vector", lambda e, ub=ub: e.tensor_mul(
            htile[ub][:], htile[ub][:], evacb[ub][:]), inc=cm.sv)
        up_marks[hc] = (s_mark, cm.sv.n)
        p.wait("scalar", cm.sv)
        p.dma("scalar", hsp[hc], htile[ub][:], cm.slot[4 + ub])
    cm.barrier()

    def w2_src(quad, hc):
        return w2g[quad].rearrange("p (q h) m -> p q h m", h=NHC)[:, :, hc, :]

    dn_marks = {}
    st_base_dn = cm.st.n
    for quad in range(8):
        p.wait("tensor", cm.sv)
        for hc in range(NHC):
            db = hc % 2
            it = quad * NHC + hc
            p.wait("sync", cm.st, dn_marks.get(it - 2, st_base_dn))
            p.wait("gpsimd", cm.st, dn_marks.get(it - 2, st_base_dn))
            p.dma("sync", htile[db][:], hsp[hc], cm.slot[db])
            p.dma("gpsimd", w2til[db][:], w2_src(quad, hc), cm.slot[db])
            p.wait("tensor", cm.slot[db])
            for o in range(4):
                for hh in range(2):
                    p.op("tensor",
                         lambda e, o=o, hh=hh, db=db, hc=hc: e.matmul(
                             ps_dn[o * 2 + hh][:], w2til[db][:, o, :],
                             htile[db][:, hh * 512:(hh + 1) * 512],
                             start=(hc == 0), stop=(hc == NHC - 1)),
                         inc=cm.st if (o == 3 and hh == 1) else None)
            dn_marks[it] = cm.st.n
        p.wait("vector", cm.st)
        for o in range(4):
            mc = quad * 4 + o
            p.op("vector", lambda e, o=o: e.tensor_copy(
                out=evacb[o % 2][:, 0:512], in_=ps_dn[o * 2][:]))
            p.op("vector", lambda e, o=o: e.tensor_copy(
                out=evacb[o % 2][:, 512:1024], in_=ps_dn[o * 2 + 1][:]))
            p.op("vector", lambda e, o=o, mc=mc: e.tensor_add(
                A[:, mc, :], evacb[o % 2][:], B[:, mc, :]),
                inc=cm.sv if o == 3 else None)
    cm.barrier()

    for c in range(NC_D):
        p.dma("sync", x2T.rearrange("(c p) t -> p c t", p=128)[:, c, :],
              A[:, c, :], cm.ld)
    emit_ln_resident(p, cm, A, B, TOK, sq, rows, ps_s1, ps_s2, ps_bc,
                     bcast_r16, bcast_m16, rows16, xbuf3[0])
    cm.barrier()

    cos_t = ropeb[0:64, 0:TOK]
    sin_t = ropeb[0:64, TOK:2 * TOK]
    cos_t64 = ropeb[64:128, 0:TOK]
    sin_t64 = ropeb[64:128, TOK:2 * TOK]

    def mk_evac_rope(out_hbm):
        out_r = out_hbm.rearrange("(c p) t -> p c t", p=128)

        def evac(mc, ps):
            b = mc % 2
            p.wait("vector", cm.st)
            p.wait("vector", cm.slot[4 + b])
            p.op("vector", lambda e, ps=ps, b=b: e.tensor_copy(
                out=evacb[b][:], in_=ps[:]), inc=cm.ss)
            x1 = evacb[b][0:64, :]
            x2 = evacb[b][64:128, :]
            t1 = rtmp[0:64, 0:TOK]
            t2 = rtmp[0:64, TOK:2 * TOK]
            t3 = rtmp[64:128, 0:TOK]
            p.op("vector", lambda e, x1=x1: e.tensor_mul(t1, x1, cos_t))
            p.op("vector", lambda e, x2=x2: e.tensor_mul(t2, x2, sin_t64))
            p.op("vector", lambda e, x2=x2: e.tensor_mul(x2, x2, cos_t64))
            p.op("vector", lambda e, x1=x1: e.tensor_mul(t3, x1, sin_t))
            p.op("vector", lambda e, x2=x2: e.tensor_add(x2, x2, t3))
            p.op("vector", lambda e, x1=x1: e.tensor_sub(x1, t1, t2),
                 inc=cm.ss)
            p.wait("scalar", cm.ss)
            p.dma("scalar", out_r[:, mc, :], evacb[b][:], cm.slot[4 + b])
        return evac

    emit_proj_fm_half(p, cm, wk1, B, 8, TOK, wpan32m, wpan16m, ps_pair,
                      mk_evac_rope(k1_loc))
    cm.barrier()

    for v8 in range(4):
        cast_marks = {}
        for ic in range(NC_D):
            b = ic % 2
            if ic >= 2:
                p.wait("sync", cm.sv, cast_marks[ic - 2])
            src = wv1[ic * 128:(ic + 1) * 128, v8 * 256:(v8 + 1) * 256]
            p.dma("sync", wpan32m[b][:, 0:256], src, cm.slot[2 + b])
            p.wait("vector", cm.slot[2 + b])
            p.op("vector", lambda e, b=b, ic=ic: e.tensor_copy(
                out=vcache3[:, ic, :], in_=wpan32m[b][:, 0:256]), inc=cm.sv)
            cast_marks[ic] = cm.sv.n
        p.wait("tensor", cm.sv)
        for tc in range(8):
            ps = ps_v4[tc]
            for ic in range(NC_D):
                p.op("tensor", lambda e, ic=ic, tc=tc, ps=ps: e.matmul(
                    ps[:], B[:, ic, tc * 128:(tc + 1) * 128],
                    vcache3[:, ic, :],
                    start=(ic == 0), stop=(ic == NC_D - 1)),
                    inc=cm.st if ic == NC_D - 1 else None)
        p.wait("vector", cm.st)
        for tc in range(8):
            b = tc % 2
            p.wait("vector", cm.slot[4 + b])
            p.op("vector", lambda e, tc=tc, b=b: e.tensor_copy(
                out=evacb[b][:, 0:256], in_=ps_v4[tc][:]), inc=cm.sv)
            p.wait("scalar", cm.sv)
            p.dma("scalar", v1_loc[tc * 128:(tc + 1) * 128,
                                   v8 * 256:(v8 + 1) * 256],
                  evacb[b][:, 0:256], cm.slot[4 + b])
        cm.barrier()

    p.raw("gpsimd", lambda e: e.collective_compute(
        "AllGather", ALU.bypass, replica_groups=GROUPS4,
        ins=[k1_loc.opt()], outs=[k1g.opt()]), cm.cc)
    p.raw("gpsimd", lambda e: e.collective_compute(
        "AllGather", ALU.bypass, replica_groups=GROUPS4,
        ins=[v1_loc.opt()], outs=[v1g.opt()]), cm.cc)
    CC_KV1 = cm.cc.n

    emit_proj_fm_half(p, cm, wq1, B, NC_D, TOK, wpan32m, wpan16m, ps_pair,
                      mk_evac_rope(q1T))
    cm.barrier()

    S4 = Af[:, 0:16384].rearrange("p (c q) -> p c q", q=512)
    E4 = Af[:, 16384:32768].rearrange("p (c q) -> p c q", q=512)
    MK = Bf[:, 0:16384].rearrange("p (c q) -> p c q", q=512)
    Kh4 = [Bf[:, 16384:20480], Bf[:, 16384:20480]]
    Vh4 = [Bf[:, 20480:24576].rearrange("p (c d) -> p c d", d=128)] * 2
    Qh4 = [Bf[:, 24576 + i * 512:24576 + (i + 1) * 512] for i in range(2)]
    rowsB4 = Bf[0:1, 25600:26624].bitcast(F32)
    rowsR16_4 = Bf[0:1, 26624:27136]
    rb4 = Bf[:, 27136:28160].bitcast(F32)
    ctxb4 = [Bf[:, 28160 + i * 512:28160 + (i + 1) * 512] for i in range(2)]

    p.wait("sync", cm.cc, CC_KV1)
    e_done4 = {}
    pbank4 = {}
    g4 = 0
    for qg in range(2):
        p.wait("sync", cm.sv)
        p.wait("sync", cm.st)
        for c in range(NC_D):
            p.dma("sync", MK[:, c, :], lmask[qg, c], cm.ld)
        p.wait("vector", cm.ld)
        for h in range(NH0):
            kv = h // 4
            eb4 = g4 % 2
            E4c = [S4, E4][eb4]
            if h % 4 == 0:
                p.wait("sync", cm.st)
                for g in range(4):
                    p.dma("sync", Kh4[0][:, g * 1024:(g + 1) * 1024],
                          k1g[g, kv * 128:(kv + 1) * 128, :], cm.slot[0])
                    p.dma("sync", Vh4[0][:, g * 8:(g + 1) * 8, :],
                          v1g[g][:, kv * 128:(kv + 1) * 128].rearrange(
                              "(c p) d -> p c d", p=128), cm.slot[0])
            p.wait("sync", cm.st)
            p.dma("sync", Qh4[h % 2][:],
                  q1T[h * 128:(h + 1) * 128, qg * 512:(qg + 1) * 512],
                  cm.slot[2 + (h % 2)])
            kb = 0
            p.wait("tensor", cm.slot[0])
            p.wait("tensor", cm.slot[2 + (h % 2)])
            for c in range(NC_D):
                bk = c % 4
                if bk in pbank4:
                    p.wait("tensor", cm.sv, pbank4[bk])
                p.op("tensor", lambda e, c=c, bk=bk, kb=kb, h=h: e.matmul(
                    ps_s[bk][:], Kh4[kb][:, c * 128:(c + 1) * 128],
                    Qh4[h % 2][:], start=True, stop=True), inc=cm.st)
                p.wait("vector", cm.st)
                if g4 >= 2 and c == 0:
                    p.wait("vector", cm.st, e_done4[g4 - 2])
                p.op("vector", lambda e, c=c, bk=bk, E4c=E4c: e.tensor_tensor(
                    out=E4c[:, c, :], in0=ps_s[bk][:], in1=MK[:, c, :],
                    op=ALU.add), inc=cm.sv)
                pbank4[bk] = cm.sv.n
                if c % 4 == 3:
                    p.wait("scalar", cm.sv)
                    p.op("scalar", lambda e, c=c, E4c=E4c: e.activation(
                        out=E4c.rearrange("p c q -> p (c q)")[
                            :, (c - 3) * 512:(c + 1) * 512],
                        in_=E4c.rearrange("p c q -> p (c q)")[
                            :, (c - 3) * 512:(c + 1) * 512],
                        func=AF.Exp, bias=0.0, scale=1.0), inc=cm.ss)
            p.wait("tensor", cm.ss)
            p.wait("tensor", cm.sv)
            for c in range(NC_D):
                p.op("tensor", lambda e, c=c, kb=kb, E4c=E4c: e.matmul(
                    ps_ctx[:], Vh4[kb][:, c, :], E4c[:, c, :],
                    start=(c == 0), stop=(c == NC_D - 1)))
            for c in range(NC_D):
                p.op("tensor", lambda e, c=c, E4c=E4c: e.matmul(
                    ps_r2[:], cm.ones_c16[:], E4c[:, c, :],
                    start=(c == 0), stop=(c == NC_D - 1)),
                    inc=cm.st if c == NC_D - 1 else None)
            e_done4[g4] = cm.st.n
            p.wait("vector", cm.st)
            p.op("vector", lambda e: e.reciprocal(out=rowsB4, in_=ps_r2))
            p.op("vector", lambda e: e.tensor_copy(out=rowsR16_4, in_=rowsB4),
                 inc=cm.sv)
            p.wait("tensor", cm.sv)
            p.op("tensor", lambda e: e.matmul(ps_bc2[:], cm.ones_r16[:],
                                              rowsR16_4[0:1, :],
                                              start=True, stop=True), inc=cm.st)
            p.wait("vector", cm.st)
            p.op("vector", lambda e: e.tensor_copy(out=rb4[:], in_=ps_bc2[:]))
            gb = h % 2
            p.wait("vector", cm.slot[4 + gb])
            p.op("vector", lambda e, gb=gb: e.tensor_mul(ctxb4[gb][:], ps_ctx[:],
                                                         rb4[:]), inc=cm.sv)
            p.wait("scalar", cm.sv)
            p.dma("scalar", ctxs[h * 128:(h + 1) * 128,
                                 qg * 512:(qg + 1) * 512],
                  ctxb4[gb][:], cm.slot[4 + gb])
            g4 += 1
    p.wait("sync", cm.slot[4])
    p.wait("sync", cm.slot[5])
    cm.barrier()

    wpan32f = [Cf[:, 10240:12288].bitcast(F32), Cf[:, 12288:14336].bitcast(F32)]
    wpan16f = [Cf[:, 14336:18432], Cf[:, 18432:22528]]
    x2b = [Cf[:, 0:1024], Cf[:, 1024:2048]]
    _f = Cf[:, 22528:24576].bitcast(F32)
    fin32 = [_f, _f]

    ctr2 = ctxs.rearrange("(c p) t -> p c t", p=128)
    for c in range(NC_D):
        p.dma("sync", A[:, c, :], ctr2[:, c, :], cm.ld)
    cm.barrier()

    x2r = x2T.rearrange("(c p) t -> p c t", p=128)

    def evac_oproj1(mc, ps):
        b = mc % 2
        p.wait("sync", cm.sv)
        p.dma("sync", x2b[b][:], x2r[:, mc, :], cm.slot[6 + b])
        p.wait("vector", cm.st)
        p.op("vector", lambda e, ps=ps, b=b: e.tensor_copy(
            out=evacb[b][:], in_=ps[:]), inc=cm.ss)
        p.wait("vector", cm.slot[6 + b])
        p.op("vector", lambda e, mc=mc, b=b: e.tensor_add(
            B[:, mc, :], evacb[b][:], x2b[b][:]), inc=cm.sv)

    emit_proj_fm_half(p, cm, wo1, A, NC_D, TOK, wpan32f, wpan16f, ps_pair,
                      evac_oproj1)
    cm.barrier()

    emit_ln_resident(p, cm, B, A, TOK, sq, rows, ps_s1, ps_s2, ps_bc,
                     bcast_r16, bcast_m16, rows16, Cf[:, 0:2048].bitcast(F32))
    cm.barrier()

    rh16 = Cf[0:64, 14336:15360]
    lgr = Cf[0:16, 15360:17408].bitcast(F32)
    lgn = Cf[:, 17408:17664].bitcast(F32).rearrange("p (c e) -> p c e", e=16)
    cmbn = Cf[:, 17664:17920].bitcast(F32).rearrange("p (c e) -> p c e", e=16)
    cmbT = Cf[0:16, 17920:18944]
    mx8 = Cf[:, 18944:18960].bitcast(F32)
    nm = Cf[:, 18960:18968].bitcast(F32)
    id16 = Cf[0:16, 18968:19000].bitcast(F32)
    rhs_stk = Cf[:, 2048:10240].rearrange("p (c t) -> p c t", t=TOK)
    span = x2b[0]

    i16 = nc.inline_tensor(np.eye(16, dtype=np.float32), name="c_i16")
    selnp = np.zeros((16, 16, 64), dtype=BF16NP)
    for e_ in range(16):
        selnp[e_, e_, :] = 1
    sel_t = nc.inline_tensor(selnp, name="c_sel")
    sel = lgr.bitcast(BF16)[:, 0:1024].rearrange("p (a m) -> p a m", m=64)
    p.dma("sync", id16[:], i16.ap(), cm.ld)
    cm.barrier()

    for (W, nmo, dst_ps) in ((aT, 64, pool[0:64, 2048:3072]),
                             (rwT, 16, pool[0:16, 3072:4096])):
        t_marks = {}
        w_base = cm.st.n
        for ic in range(NC_D):
            b = ic % 2
            p.wait("sync", cm.sv)
            p.dma("sync", wpan32f[b][:, 0:nmo], W[ic * 128:(ic + 1) * 128, :],
                  cm.slot[2 + b])
            p.wait("vector", cm.slot[2 + b])
            p.wait("vector", cm.st, t_marks[ic - 2] if ic >= 2 else w_base)
            p.op("vector", lambda e, b=b, nmo=nmo: e.tensor_copy(
                out=span[:, b * 512:b * 512 + nmo],
                in_=wpan32f[b][:, 0:nmo]), inc=cm.sv)
            p.wait("tensor", cm.sv)
            for hh in range(2):
                p.op("tensor", lambda e, ic=ic, hh=hh, b=b, nmo=nmo,
                     dst_ps=dst_ps: e.matmul(
                         dst_ps[:, hh * 512:(hh + 1) * 512],
                         span[:, b * 512:b * 512 + nmo],
                         A[:, ic, hh * 512:(hh + 1) * 512],
                         start=(ic == 0), stop=(ic == NC_D - 1)),
                     inc=cm.st if hh == 1 else None)
            t_marks[ic] = cm.st.n
    p.wait("vector", cm.st)
    p.op("vector", lambda e: e.tensor_copy(out=rh16[:], in_=pool[0:64, 2048:3072]))
    p.op("vector", lambda e: e.tensor_copy(out=lgr[:], in_=pool[0:16, 3072:4096]),
         inc=cm.sv)
    cm.barrier()

    for tc in range(8):
        p.wait("tensor", cm.sv)
        p.op("tensor", lambda e, tc=tc: e.transpose(
            pool[:, 2048 + (tc % 2) * 512:2048 + (tc % 2) * 512 + 16],
            lgr[:, tc * 128:(tc + 1) * 128], id16[:]), inc=cm.st)
        p.wait("vector", cm.st)
        p.op("vector", lambda e, tc=tc: e.tensor_copy(
            out=lgn[:, tc, :], in_=pool[:, 2048 + (tc % 2) * 512:
                                        2048 + (tc % 2) * 512 + 16]),
            inc=cm.sv)
    p.wait("vector", cm.sv)
    for tc in range(8):
        p.op("vector", lambda e, tc=tc: e.reduce_max(
            nm[:, 0:1], lgn[:, tc, :], axis=mybir.AxisListType.X))
        p.op("vector", lambda e: e.tensor_scalar_mul(nm[:, 0:1], nm[:, 0:1],
                                                     -1.0), inc=cm.sv)
        p.wait("scalar", cm.sv)
        p.op("scalar", lambda e, tc=tc: e.activation(
            out=cmbn[:, tc, :], in_=lgn[:, tc, :], func=AF.Exp,
            bias=nm[:, 0:1], scale=1.0, accum_out=nm[:, 1:2]), inc=cm.ss)
        p.wait("vector", cm.ss)
        p.op("vector", lambda e: e.reciprocal(out=nm[:, 1:2], in_=nm[:, 1:2]))
        p.op("vector", lambda e, tc=tc: e.tensor_scalar(
            out=cmbn[:, tc, :], in0=cmbn[:, tc, :], scalar1=nm[:, 1:2],
            scalar2=None, op0=ALU.mult))
        p.op("vector", lambda e, tc=tc: e.max(out=mx8[:], in_=cmbn[:, tc, :]))
        p.op("vector", lambda e: e.tensor_add(nm[:, 2:3], mx8[:, 0:1],
                                              mx8[:, 1:2]))
        p.op("vector", lambda e: e.tensor_scalar_add(nm[:, 2:3], nm[:, 2:3],
                                                     1e-6))
        p.op("vector", lambda e: e.reciprocal(out=nm[:, 2:3], in_=nm[:, 2:3]))
        p.op("vector", lambda e: e.tensor_scalar_mul(nm[:, 2:3], nm[:, 2:3],
                                                     0.5))
        p.op("vector", lambda e, tc=tc: e.tensor_scalar(
            out=lgn[:, tc, :], in0=cmbn[:, tc, :], scalar1=mx8[:, 0:1],
            scalar2=None, op0=ALU.is_equal))
        p.op("vector", lambda e, tc=tc: e.tensor_scalar(
            out=lgn[:, tc, :], in0=lgn[:, tc, :], scalar1=mx8[:, 0:1],
            scalar2=None, op0=ALU.mult))
        p.op("vector", lambda e, tc=tc: e.tensor_scalar(
            out=cmbn[:, tc, :], in0=cmbn[:, tc, :],
            scalar1=mx8[:, 1:2], scalar2=None, op0=ALU.is_equal))
        p.op("vector", lambda e, tc=tc: e.tensor_scalar(
            out=cmbn[:, tc, :], in0=cmbn[:, tc, :], scalar1=mx8[:, 1:2],
            scalar2=None, op0=ALU.mult))
        p.op("vector", lambda e, tc=tc: e.tensor_add(cmbn[:, tc, :],
                                                     cmbn[:, tc, :],
                                                     lgn[:, tc, :]))
        p.op("vector", lambda e, tc=tc: e.tensor_scalar(
            out=cmbn[:, tc, :], in0=cmbn[:, tc, :], scalar1=nm[:, 2:3],
            scalar2=None, op0=ALU.mult), inc=cm.sv)
    cm.barrier()
    for tc in range(8):
        p.op("vector", lambda e, tc=tc: e.tensor_copy(
            out=span[:, (tc % 2) * 512:(tc % 2) * 512 + 16],
            in_=cmbn[:, tc, :]), inc=cm.sv)
        p.wait("tensor", cm.sv)
        psb = pool[:, 2048 + (tc % 2) * 512:2048 + (tc % 2) * 512 + 256].bitcast(BF16)
        p.op("tensor", lambda e, tc=tc, psb=psb: e.transpose(
            psb[0:16, 0:128], span[:, (tc % 2) * 512:(tc % 2) * 512 + 16],
            cm.ident[:]), inc=cm.st)
        p.wait("vector", cm.st)
        p.op("vector", lambda e, tc=tc, psb=psb: e.tensor_copy(
            out=cmbT[:, tc * 128:(tc + 1) * 128],
            in_=psb[0:16, 0:128]), inc=cm.sv)
    cm.barrier()

    p.dma("sync", sel, sel_t.ap(), cm.ld)
    cm.barrier()
    for ee in range(16):
        rc, half = ee // 2, ee % 2
        p.wait("tensor", cm.sv)
        p.op("tensor", lambda e, ee=ee: e.matmul(
            pool[0:64, 2048 + (ee % 2) * 512:2048 + (ee % 2) * 512 + 512],
            sel[:, ee, :], cmbT[:, 0:512],
            start=True, stop=True))
        p.op("tensor", lambda e, ee=ee: e.matmul(
            pool[0:64, 3072 + (ee % 2) * 512:3072 + (ee % 2) * 512 + 512],
            sel[:, ee, :], cmbT[:, 512:1024],
            start=True, stop=True), inc=cm.st)
        p.wait("vector", cm.st)
        p.op("vector", lambda e, ee=ee: e.tensor_copy(
            out=span[0:64, 0:512],
            in_=pool[0:64, 2048 + (ee % 2) * 512:2048 + (ee % 2) * 512 + 512]))
        p.op("vector", lambda e, ee=ee: e.tensor_copy(
            out=span[0:64, 512:1024],
            in_=pool[0:64, 3072 + (ee % 2) * 512:3072 + (ee % 2) * 512 + 512]),
            inc=cm.sv)
        p.op("vector", lambda e, ee=ee, rc=rc, half=half: e.tensor_mul(
            rhs_stk[half * 64:half * 64 + 64, rc, :], rh16[:],
            span[0:64, 0:1024]), inc=cm.sv)
    cm.barrier()

    bpan = [sq[:, i * 1024:(i + 1) * 1024].rearrange("p (rc m) -> p rc m", m=128)
            for i in range(2)]

    def evac_fin(mc, ps):
        b = mc % 2
        p.wait("scalar", cm.st)
        p.op("scalar", lambda e, mc=mc, ps=ps, b=b: e.activation(
            out=evacb[b][:], in_=ps[:], func=AF.Identity,
            bias=bias_b[:, mc:mc + 1], scale=1.0), inc=cm.ss)
        p.wait("vector", cm.ss)
        p.wait("vector", cm.slot[6])
        p.op("vector", lambda e, mc=mc, b=b: e.tensor_add(
            fin32[b][:], evacb[b][:], B[:, mc, :]), inc=cm.sv)
        p.wait("scalar", cm.sv)
        p.dma("scalar", outT.rearrange("(c p) t -> p c t", p=128)[:, mc, :],
              fin32[b][:], cm.slot[6])

    cast_marks = {}
    group_marks = {}
    evac_marks = {}
    pctr = 0
    pmark = {0: 0, 1: 0}
    for mc in range(NC_D):
        b = mc % 2
        for part in range(5):
            pb = pctr % 2
            if part < 4:
                lo = part * 8
                srcw = bwT[lo * 128:(lo + 8) * 128,
                           mc * 128:(mc + 1) * 128].rearrange(
                    "(ic p) m -> p ic m", p=128)
            else:
                srcw = bstk[:, mc * 128:(mc + 1) * 128].rearrange(
                    "(ic p) m -> p ic m", p=128)
            p.wait("sync", cm.sv, pmark[pb])
            p.dma("sync", wpan32f[pb][:, 0:1024].rearrange(
                "p (ic m) -> p ic m", m=128), srcw, cm.slot[2 + pb])
            p.wait("vector", cm.slot[2 + pb])
            if mc >= 2 and part == 0:
                p.wait("vector", cm.st, group_marks[mc - 2])
            if part < 4:
                p.op("vector", lambda e, b=b, pb=pb, lo=lo: e.tensor_copy(
                    out=wpan16f[b][:, lo * 128:lo * 128 + 1024],
                    in_=wpan32f[pb][:, 0:1024]), inc=cm.sv)
            else:
                p.op("vector", lambda e, b=b, pb=pb: e.tensor_copy(
                    out=bpan[b].rearrange("p ic m -> p (ic m)"),
                    in_=wpan32f[pb][:, 0:1024]), inc=cm.sv)
            pmark[pb] = cm.sv.n
            pctr += 1
        cast_marks[mc] = cm.sv.n
        ps = ps_pair[mc % 2]
        p.wait("tensor", cm.sv)
        if mc >= 2:
            p.wait("tensor", cm.ss, evac_marks[mc - 2])
        for ic in range(NC_D):
            for hh in range(2):
                p.op("tensor", lambda e, b=b, ic=ic, hh=hh, ps=ps: e.matmul(
                    ps[:, hh * 512:(hh + 1) * 512],
                    wpan16f[b][:, ic * 128:(ic + 1) * 128],
                    A[:, ic, hh * 512:(hh + 1) * 512],
                    start=(ic == 0), stop=False))
        for rc in range(8):
            for hh in range(2):
                p.op("tensor", lambda e, b=b, rc=rc, hh=hh, ps=ps: e.matmul(
                    ps[:, hh * 512:(hh + 1) * 512], bpan[b][:, rc, :],
                    rhs_stk[:, rc, hh * 512:(hh + 1) * 512],
                    start=False, stop=(rc == 7)),
                    inc=cm.st if (rc == 7 and hh == 1) else None)
        group_marks[mc] = cm.st.n
        evac_fin(mc, ps)
        evac_marks[mc] = cm.ss.n

    p.wait("sync", cm.slot[6])
    p.wait("sync", cm.slot[7])
    cm.barrier()
    p.run()
    return nc



def _f32(a):
    return np.ascontiguousarray(a, dtype=np.float32)


def _ci32(a):
    return np.ascontiguousarray(a, dtype=np.int32)


def w13_chunks(core):
    return [(core * DQ_HC + i) if (core * DQ_HC + i) < NHC else 0
            for i in range(DQ_HC)]


_NC_CACHE = {}


def _get_nc():
    if "full" not in _NC_CACHE:
        _NC_CACHE["full"] = build_full()
    return _NC_CACHE["full"]


def kernel(**inputs):
    x = np.asarray(inputs["L0_attn_hidden_states"])
    wqT = _f32(np.asarray(inputs["L0_attn_wq"]).T)
    wkT = _f32(np.asarray(inputs["L0_attn_wk"]).T)
    wvT = _f32(np.asarray(inputs["L0_attn_wv"]).T)
    bq_r = _f32(np.asarray(inputs["L0_attn_bq"]).reshape(NC_D, 128).T)
    bk_r = _f32(np.asarray(inputs["L0_attn_bk"]).reshape(NC_D, 128).T)
    bv_row = _f32(np.asarray(inputs["L0_attn_bv"]).reshape(1, D))
    w1 = np.asarray(inputs["L0_ffn_w1"])
    s1 = np.asarray(inputs["L0_ffn_s1"])
    w2 = np.asarray(inputs["L0_ffn_w2"])
    s2 = np.asarray(inputs["L0_ffn_s2"])
    w3 = np.asarray(inputs["L0_ffn_w3"])
    s3 = np.asarray(inputs["L0_ffn_s3"])
    woT = _f32(np.asarray(inputs["L0_attn_wo"]).T)
    bo_r = _f32(np.asarray(inputs["L0_attn_bo"]).reshape(NC_D, 128).T)
    wq1 = _f32(inputs["L1_attn_Wq"])
    wk1 = _f32(inputs["L1_attn_Wk"])
    wv1 = _f32(inputs["L1_attn_Wv"])
    wo1 = _f32(inputs["L1_attn_Wo"])
    bwT = _f32(np.asarray(inputs["L1_ffn_base_w"]).T)
    bb_r = _f32(np.asarray(inputs["L1_ffn_base_b"]).reshape(NC_D, 128).T)
    aT = _f32(np.asarray(inputs["L1_ffn_A"]).T)
    bstk = _f32(np.asarray(inputs["L1_ffn_B"]).transpose(0, 2, 1).reshape(1024, D))
    rwT = _f32(np.asarray(inputs["L1_ffn_router_w"]).T)
    inv = 1.0 / (10000.0 ** (np.arange(0, 128, 2)[:64] / 128.0))

    in_maps = []
    for c in range(NCORE):
        b, j = c // 4, c % 4
        lst = w13_chunks(c)
        pk1 = _ci32(np.concatenate([w1[hc * 4096:(hc + 1) * 4096] for hc in lst]))
        pk3 = _ci32(np.concatenate([w3[hc * 4096:(hc + 1) * 4096] for hc in lst]))
        s1v = s1.reshape(HID, 64)
        s3v = s3.reshape(HID, 64)
        s1_r = _f32(np.concatenate([s1v[hc * 128:(hc + 1) * 128] for hc in lst],
                                   axis=1))
        s3_r = _f32(np.concatenate([s3v[hc * 128:(hc + 1) * 128] for hc in lst],
                                   axis=1))
        o0 = c * DQ_OC * 128
        pk2 = _ci32(w2[o0 * 86:(o0 + DQ_OC * 128) * 86].reshape(DQ_OC * 128,
                                                                86 * 64))
        s2v = s2.reshape(D, 172)
        s2_r = _f32(np.concatenate(
            [s2v[(c * DQ_OC + q) * 128:(c * DQ_OC + q + 1) * 128]
             for q in range(DQ_OC)], axis=1))
        pos = np.arange(j * TOK, (j + 1) * TOK)
        fr = pos[None, :] * inv[:, None]
        lm = np.zeros((2, NC_D, 128, 512), np.float32)
        for qg in range(2):
            kpos = (128 * np.arange(NC_D)[:, None, None]
                    + np.arange(128)[None, :, None])
            qpos = TOK * j + 512 * qg + np.arange(512)[None, None, :]
            lm[qg] = np.where(kpos <= qpos, 0.0, -30000.0)
        in_maps.append({
            "xT": _f32(x[b, j * TOK:(j + 1) * TOK, :].T),
            "wqT": wqT, "wkT": wkT, "wvT": wvT,
            "bq_r": bq_r, "bk_r": bk_r, "bv_row": bv_row,
            "pk1": pk1, "pk3": pk3, "s1_r": s1_r, "s3_r": s3_r,
            "pk2": pk2, "s2_r": s2_r,
            "woT": woT, "bo_r": bo_r,
            "wq1": wq1, "wk1": wk1, "wv1": wv1,
            "cosb": np.cos(fr).astype(BF16NP),
            "sinb": np.sin(fr).astype(BF16NP),
            "wo1": wo1, "bwT": bwT, "bb_r": bb_r, "aT": aT,
            "bstk": bstk, "rwT": rwT,
            "lmask": lm.astype(BF16NP),
        })

    res = run_bass_kernel_spmd(_get_nc(), in_maps,
                               core_ids=list(range(NCORE))).results

    out = np.empty((2, SEQ, D), np.float32)
    for c in range(NCORE):
        b, j = c // 4, c % 4
        out[b, j * TOK:(j + 1) * TOK, :] = res[c]["outT"].T
    return out
